# version 04
# baseline (speedup 1.0000x reference)
import contextlib
import math
import numpy as np
import ml_dtypes
import concourse.bass as bass
import concourse.mybir as mybir
from concourse.bass_utils import run_bass_kernel_spmd

F32 = mybir.dt.float32
BF16 = mybir.dt.bfloat16
AF = mybir.ActivationFunctionType
ALU = mybir.AluOpType
AX = mybir.AxisListType
BFNP = ml_dtypes.bfloat16

DEBUG = False
DEBUG2 = False
S, D = 4096, 1024
NT = S // 128
NQB = S // 512
NDC = D // 128
LAM = 0.01
N_CORES = 8


def host_consts():
    c = {}
    c["c_ident"] = np.eye(128, dtype=np.float32)
    ql = np.arange(512, dtype=np.float64)[None, :]
    kl = np.arange(128, dtype=np.float64)[:, None]
    c["c_cmask"] = np.concatenate(
        [np.where(kl + 128 * j > ql, np.float64(-1e30), 0.0) for j in range(4)],
        axis=1).astype(np.float32)
    slopes = 2.0 ** (-np.linspace(1.0, 8.0, 16))
    p = np.arange(128, dtype=np.float64)
    bias_k = np.zeros((128, 16 * 32))
    for h in range(16):
        for kt in range(32):
            bias_k[:, h * 32 + kt] = slopes[h] * (kt * 128 + p)
    c["c_alibi_k"] = bias_k.astype(np.float32)
    q = np.arange(4096, dtype=np.float64)
    c["c_slq8"] = np.concatenate(
        [8.0 * slopes[h] * q for h in range(16)])[None, :].astype(np.float32)
    k = np.arange(4096, dtype=np.float64)
    c["c_dcol"] = np.exp(LAM * k).reshape(32, 128).T.astype(np.float32)
    c["c_dcolinv"] = np.exp(-LAM * k).reshape(32, 128).T.astype(np.float32)
    c["c_dscale"] = np.exp(-2 * LAM * k).reshape(32, 128).T.astype(np.float32)
    c["c_dqlo"] = (np.exp(-LAM * k) / 32.0).reshape(32, 128).T.astype(np.float32)
    c["c_dqhi"] = (np.exp(LAM * k) / 32.0).reshape(32, 128).T.astype(np.float32)
    c["c_dpatch"] = np.concatenate(
        [np.exp(-2 * LAM * np.maximum(kl + 128 * j - ql, 0.0)) for j in range(4)],
        axis=1).astype(np.float32)
    c["c_ones512"] = np.ones((1, 512), np.float32)
    c["c_negones"] = np.full((1, 128), -1.0, np.float32)
    kc = np.zeros((128, 8), np.float32)
    kc[:, 0] = 1e-5
    kc[:, 1] = -25.0
    kc[:, 2] = -15.0
    kc[:, 3] = -np.log(32.0)
    kc[:, 4] = 1e-20
    kc[:, 5] = np.log(2.5)
    c["c_kconst"] = kc
    c["c_onesp"] = np.ones((128, 1024), np.float32)
    c["c_onescol"] = np.ones((128, 1), np.float32)
    c["c_zeroblk"] = np.zeros((128, 128), np.float32)
    c["c_zpad"] = np.ones((128, 512), np.float32)
    return c


class NcProxy:
    def __init__(self, nc):
        self._nc = nc
        self._uid = 0

    def __getattr__(self, k):
        return getattr(self._nc, k)

    def sbuf_tensor(self, name, shape, dt):
        self._uid += 1
        return self._nc.sbuf_tensor(f"{name}_u{self._uid}", shape, dt)

    def psum_tensor(self, name, shape, dt):
        self._uid += 1
        return self._nc.psum_tensor(f"{name}_u{self._uid}", shape, dt)


class Ctx:
    def __init__(self, nc, stack):
        self.nc = NcProxy(nc)
        self.stack = stack
        self.block = stack.enter_context(nc.Block())
        self.sems = {}
        self.cnt = {}
        self.sb = {}
        self.dram = {}
        self._bar_n = 0

    def sem(self, name):
        if name not in self.sems:
            self.sems[name] = self.stack.enter_context(
                self.nc._nc.semaphore(name))
            self.cnt[name] = 0
        return self.sems[name]

    def base(self, name):
        self.sem(name)
        return self.cnt[name]

    def commit(self, name, total):
        self.cnt[name] += total

    def barrier(self):
        bar = self.sem("bar")
        self._bar_n += 1
        tgt = 5 * self._bar_n
        self.cnt["bar"] = tgt
        for eng in ("tensor", "scalar", "vector", "sync", "gpsimd"):
            @getattr(self.block, eng)
            def _(e, tgt=tgt, bar=bar):
                e.drain().then_inc(bar, 1)
                e.wait_ge(bar, tgt)


def emit_consts(c):
    nc, st = c.nc, c.stack
    def sbt(name, shape, dt):
        h = st.enter_context(nc.sbuf_tensor(name, shape, dt))
        c.sb[name] = h
        return h

    names = [("ident_bf", "c_ident", [128, 128], BF16),
             ("ident_f", "c_ident", [128, 128], F32),
             ("ones1", "c_ones512", [1, 512], BF16),
             ("negones_f", "c_negones", [1, 128], F32),
             ("zeroblk", "c_zeroblk", [128, 128], BF16),
             ("zpad", "c_zpad", [128, 512], BF16),
             ("kconst", "c_kconst", [128, 8], F32)]
    for n_, _, sh, dt in names:
        sbt(n_, sh, dt)

    ld = c.sem("g_ld")
    ld0 = c.base("g_ld")

    @c.block.gpsimd
    def _(g):
        for n_, dn, sh, dt in names:
            g.dma_start(out=c.sb[n_][:], in_=c.dram[dn][:, :]).then_inc(ld, 16)
        g.wait_ge(ld, ld0 + 16 * len(names))

    c.commit("g_ld", 16 * len(names))
    c.barrier()


def emit_sub_consts(c, st, names):
    nc = c.nc
    ld = c.sem("g_ld")
    ld0 = c.base("g_ld")
    for n_, dn, sh, dt in names:
        c.sb[n_] = st.enter_context(nc.sbuf_tensor(n_, sh, dt))

    @c.block.gpsimd
    def _(g):
        for n_, dn, sh, dt in names:
            g.dma_start(out=c.sb[n_][:], in_=c.dram[dn][:, :]).then_inc(ld, 16)
        g.wait_ge(ld, ld0 + 16 * len(names))
    c.commit("g_ld", 16 * len(names))
    c.barrier()


CAUSAL_CONSTS = [("cmask", "c_cmask", [128, 4 * 512], BF16),
                 ("alibi_k", "c_alibi_k", [128, 16 * 32], F32)]
DECAY_CONSTS = [("dcol", "c_dcol", [128, 32], F32),
                ("dcolinv", "c_dcolinv", [128, 32], F32),
                ("dqlo", "c_dqlo", [128, 32], F32),
                ("dqhi", "c_dqhi", [128, 32], F32),
                ("dpatch", "c_dpatch", [128, 4 * 512], F32),
                ("onescol", "c_onescol", [128, 1], BF16)]


def emit_bias_prep(c, fm_specs, row_specs, st=None):
    nc = c.nc
    st = st if st is not None else c.stack
    for name, dn in fm_specs:
        F = c.dram[dn].shape[-1]
        c.sb[name] = st.enter_context(nc.sbuf_tensor(name, [128, F // 128], F32))
    for name, dn in row_specs:
        F = c.dram[dn].shape[-1]
        c.sb[name] = st.enter_context(nc.sbuf_tensor(name, [1, F], BF16))
    ld = c.sem("g_ld"); pe = c.sem("g_pe"); ve = c.sem("g_ve")
    ld0 = c.base("g_ld"); pe0 = c.base("g_pe"); ve0 = c.base("g_ve")
    n = len(fm_specs)
    with contextlib.ExitStack() as loc:
        tmp = loc.enter_context(nc.sbuf_tensor("bp_t", [16, 128], F32))
        ps = loc.enter_context(nc.psum_tensor("bp_p", [128, 16], F32))

        @c.block.gpsimd
        def _(g):
            for i, (name, dn) in enumerate(fm_specs):
                if i >= 1:
                    g.wait_ge(pe, pe0 + i)
                g.dma_start(out=tmp[0:c.dram[dn].shape[-1] // 128, :],
                            in_=c.dram[dn].rearrange("o (cc p) -> (o cc) p",
                                                     p=128)).then_inc(ld, 16)
            for name, dn in row_specs:
                g.dma_start(out=c.sb[name][:],
                            in_=c.dram[dn][:, :]).then_inc(ld, 16)
            g.wait_ge(ld, ld0 + 16 * (len(fm_specs) + len(row_specs)))

        @c.block.tensor
        def _(t):
            for i, (name, dn) in enumerate(fm_specs):
                nc_ = c.dram[dn].shape[-1] // 128
                t.wait_ge(ld, ld0 + 16 * (i + 1))
                if i >= 1:
                    t.wait_ge(ve, ve0 + i)
                t.transpose(ps[:, 0:nc_], tmp[0:nc_, :],
                            c.sb["ident_f"][0:nc_, 0:nc_]).then_inc(pe, 1)

        @c.block.vector
        def _(v):
            for i, (name, dn) in enumerate(fm_specs):
                nc_ = c.dram[dn].shape[-1] // 128
                v.wait_ge(pe, pe0 + i + 1)
                v.tensor_copy(c.sb[name][:, 0:nc_], ps[:, 0:nc_])
                v.drain().then_inc(ve, 1)

        c.commit("g_ld", 16 * (n + len(row_specs)))
        c.commit("g_pe", n)
        c.commit("g_ve", n)
    c.barrier()


def emit_ln_transpose(c, x_src, xnt_dst, tiles=None, xn_cb=None, n_cb_inc=0,
                      act_cb=None, n_acb_inc=0, pre_xn=None,
                      arena_to_hbm=None):
    nc = c.nc
    tiles = list(range(NT)) if tiles is None else list(tiles)
    n = len(tiles)
    ld = c.sem("ln_ld"); va = c.sem("ln_va"); sa = c.sem("ln_sa")
    vb = c.sem("ln_vb"); sb = c.sem("ln_sb"); ve = c.sem("ln_ve")
    pe = c.sem("ln_pe"); ev = c.sem("ln_ev"); hb = c.sem("ln_hb")
    ld0 = c.base("ln_ld"); va0 = c.base("ln_va"); sa0 = c.base("ln_sa")
    vb0 = c.base("ln_vb"); sb0 = c.base("ln_sb"); ve0 = c.base("ln_ve")
    pe0 = c.base("ln_pe"); ev0 = c.base("ln_ev"); hb0 = c.base("ln_hb")
    vpt = 1 + n_cb_inc

    with contextlib.ExitStack() as ph:
        xtile = [ph.enter_context(nc.sbuf_tensor(f"lnx{i}", [128, 1024], F32))
                 for i in range(2)]
        xn_bf = [ph.enter_context(nc.sbuf_tensor(f"lnn{i}", [128, 1024], BF16))
                 for i in range(2)]
        stat = ph.enter_context(nc.sbuf_tensor("lnst", [128, 12], F32))
        xsq = ph.enter_context(nc.sbuf_tensor("lnsq", [128, 1024], F32))
        ps_tr = [ph.enter_context(nc.psum_tensor(f"lnp{i}", [128, 1024], BF16))
                 for i in range(2)]

        @c.block.sync
        def _(sy):
            for i, t in enumerate(tiles):
                b = i % 2
                if i >= 2:
                    sy.wait_ge(ve, ve0 + vpt * (i - 1))
                    sy.wait_ge(sa, sa0 + (i - 2) + 1)
                sy.dma_start(out=xtile[b][:],
                             in_=x_src[t * 128:(t + 1) * 128, :]).then_inc(ld, 16)

        def evac(v, i):
            b = i % 2
            v.wait_ge(pe, pe0 + i + 1)
            src = ps_tr[b][:].rearrange("p (cc q) -> p cc q", q=128)
            v.tensor_copy(xnt_dst(i, tiles[i]), src)
            v.drain().then_inc(ev, 1)

        @c.block.vector
        def _(v):
            for i, t in enumerate(tiles):
                b = i % 2
                v.wait_ge(ld, ld0 + 16 * (i + 1))
                if i >= 1:
                    v.wait_ge(sa, sa0 + i)
                v.reduce_sum(stat[:, 0:1], xtile[b][:], axis=AX.X)
                v.drain().then_inc(va, 1)
                v.wait_ge(sa, sa0 + i + 1)
                v.tensor_sub(stat[:, 5:6], stat[:, 3:4], stat[:, 4:5])
                v.drain().then_inc(vb, 1)
                v.wait_ge(sb, sb0 + i + 1)
                if i >= 2:
                    v.wait_ge(pe, pe0 + i - 1)
                if pre_xn is not None:
                    pre_xn(v, i)
                v.tensor_scalar(xn_bf[b][:], xtile[b][:], stat[:, 2:3],
                                stat[:, 7:8], ALU.subtract,
                                ALU.mult).then_inc(ve, 1)
                if xn_cb is not None:
                    xn_cb(v, i, t, xn_bf[b], stat)
                if i >= 1:
                    evac(v, i - 1)
            evac(v, n - 1)

        @c.block.scalar
        def _(s):
            for i, t in enumerate(tiles):
                b = i % 2
                s.wait_ge(ld, ld0 + 16 * (i + 1))
                s.activation(xsq[:], xtile[b][:], AF.Square,
                             accum_out=stat[:, 1:2])
                s.wait_ge(va, va0 + i + 1)
                s.activation(stat[:, 2:3], stat[:, 0:1], AF.Copy,
                             scale=1.0 / 1024)
                s.activation(stat[:, 4:5], stat[:, 2:3], AF.Square)
                s.activation(stat[:, 3:4], stat[:, 1:2], AF.Copy,
                             scale=1.0 / 1024)
                s.drain().then_inc(sa, 1)
                s.wait_ge(vb, vb0 + i + 1)
                s.activation(stat[:, 6:7], stat[:, 5:6], AF.Ln,
                             bias=c.sb["kconst"][:, 0:1])
                s.activation(stat[:, 7:8], stat[:, 6:7], AF.Exp, scale=-0.5)
                s.drain().then_inc(sb, 1)
                if act_cb is not None:
                    s.wait_ge(ve, ve0 + vpt * i + vpt)
                    act_cb(s, i, t, stat, xn_bf[b], xsq)

        @c.block.tensor
        def _(t_):
            for i, t in enumerate(tiles):
                b = i % 2
                t_.wait_ge(ve, ve0 + vpt * i + 1)
                if i >= 2:
                    t_.wait_ge(ev, ev0 + (i - 1))
                for cc in range(8):
                    ins = t_.transpose(ps_tr[b][:, cc * 128:(cc + 1) * 128],
                                       xn_bf[b][:, cc * 128:(cc + 1) * 128],
                                       c.sb["ident_bf"][:])
                ins.then_inc(pe, 1)

        if arena_to_hbm is not None:
            arena, hbm_ap = arena_to_hbm
            @c.block.gpsimd
            def _(g):
                g.wait_ge(ev, ev0 + n)
                for cc in range(8):
                    g.dma_start(out=hbm_ap[cc, :, :],
                                in_=arena[:].rearrange(
                                    "p (cc q) -> p cc q", q=4096)[:, cc, :]
                                ).then_inc(hb, 16)
                g.wait_ge(hb, hb0 + 16 * 8)

        c.commit("ln_ld", 16 * n)
        c.commit("ln_va", n)
        c.commit("ln_sa", n)
        c.commit("ln_vb", n)
        c.commit("ln_sb", n)
        c.commit("ln_ve", vpt * n)
        c.commit("ln_pe", n)
        c.commit("ln_ev", n)
        if arena_to_hbm is not None:
            c.commit("ln_hb", 16 * 8)
    c.barrier()


def emit_proj_fm(c, w_sb, xnt_arena, out_cb, oc_list, bias_fm=None,
                 scale=None, n_dc=8, tok=4096):
    nc = c.nc
    pe = c.sem("pj_pe"); ev = c.sem("pj_ev")
    pe0 = c.base("pj_pe"); ev0 = c.base("pj_ev")
    ntb = tok // 512
    xv = xnt_arena[:].rearrange("p (dc q) -> p dc q", q=tok)

    with contextlib.ExitStack() as ph:
        ps = [ph.enter_context(nc.psum_tensor(f"pjp{i}", [128, 512], F32))
              for i in range(8)]

        @c.block.tensor
        def _(t):
            for j, oc in enumerate(oc_list):
                for dc in range(n_dc):
                    for tb in range(ntb):
                        if j >= 1 and dc == 0:
                            t.wait_ge(ev, ev0 + (j - 1) * ntb + tb + 1)
                        ins = t.matmul(ps[tb][:],
                                       w_sb[:, dc, oc * 128:(oc + 1) * 128],
                                       xv[:, dc, tb * 512:(tb + 1) * 512],
                                       start=(dc == 0), stop=(dc == n_dc - 1))
                        if dc == n_dc - 1:
                            ins.then_inc(pe, 1)

        @c.block.scalar
        def _(s):
            for j, oc in enumerate(oc_list):
                for tb in range(ntb):
                    s.wait_ge(pe, pe0 + j * ntb + tb + 1)
                    out_cb(s, oc, tb, ps[tb][:])

        c.commit("pj_pe", len(oc_list) * ntb)
        c.commit("pj_ev", len(oc_list) * ntb)


def attn_core(c, cfg):
    raise NotImplementedError


def emit_attention(c, p):
    nc, st = c.nc, c.stack
    H = p["H"]
    sep_z = p.get("sep_z", False)
    oz_rows = p["oz_rows"]
    o_rows = p["o_rows"]
    hinc = p.get("hprep_incs", {})
    qinc = p.get("qprep_incs", {})

    semnames = ["a_pe", "a_se", "a_ve", "a_gld", "a_xld", "a_st"]
    CNT = {sn: [c.base(sn)] for sn in semnames}

    def bump(sn, n=1):
        CNT[sn][0] += n
        return CNT[sn][0]

    sched = []
    items = []
    hqs = []
    ygroups = []
    meta = {"hq": {}, "qb": {}}
    for qb in range(NQB):
        kts = p["kts"](qb)
        e = {"k": "qprep", "qb": qb,
             "marks": {sn: bump(sn, qinc.get(sn, 0)) for sn in semnames}}
        sched.append(e)
        meta["qb"][qb] = e["marks"]
        for h in range(H):
            hq = len(hqs)
            e = {"k": "hprep", "qb": qb, "h": h, "hq": hq,
                 "marks": {sn: bump(sn, hinc.get(sn, 0)) for sn in semnames}}
            sched.append(e)
            mk = {"prep": e["marks"]}
            for i, kt in enumerate(kts):
                j = len(items)
                dj = p["diag"](qb, kt)
                e = {"k": "item", "qb": qb, "h": h, "kt": kt, "j": j,
                     "hq": hq, "dj": dj,
                     "pe_s": bump("a_pe"),
                     "ve_d": bump("a_ve") if dj is not None else None,
                     "se_e": bump("a_se"), "pe_pv": bump("a_pe"),
                     "first": i == 0, "last": i == len(kts) - 1}
                sched.append(e)
                items.append(e)
            mk["pe_s_last"] = items[-1]["pe_s"]
            mk["pe_pv_last"] = items[-1]["pe_pv"]
            e = {"k": "norm", "qb": qb, "h": h, "hq": hq,
                 "se_ln": bump("a_se"), "pe_b": bump("a_pe"),
                 "se_rz": bump("a_se"),
                 "ve_o": bump("a_ve"), "last_j": len(items) - 1}
            sched.append(e)
            mk["norm"] = e
            hqs.append(mk)
            meta["hq"][(qb, h)] = mk
        for t4 in range(4):
            e = {"k": "oproj", "qb": qb, "t4": t4, "hq_end": len(hqs) - 1,
                 "xld": bump("a_xld", 16), "g": [None, None],
                 "st": bump("a_st", 16)}
            if p["oproj_mode"] == "mm":
                for ob in range(2):
                    g = {"pe": bump("a_pe"), "ve": bump("a_ve"),
                         "idx": len(ygroups)}
                    e["g"][ob] = g
                    ygroups.append(g)
            else:
                g = {"pe": bump("a_pe"), "ve": bump("a_ve"),
                     "idx": len(ygroups)}
                e["g"][0] = e["g"][1] = g
                ygroups.append(g)
            sched.append(e)
    oprojs = [e for e in sched if e["k"] == "oproj"]
    meta["items"] = items
    meta["hqs"] = hqs

    with contextlib.ExitStack() as ph:
        ps_s = [ph.enter_context(nc.psum_tensor(f"aps{i}", [128, 512], F32))
                for i in range(2)]
        ps_oz = [ph.enter_context(nc.psum_tensor(f"aoz{i}", [oz_rows, 512], F32))
                 for i in range(2)]
        ps_b = ph.enter_context(nc.psum_tensor("abc", [o_rows, 512], F32))
        hd = {}
        hd["ps_b"] = ps_b
        n_py = p.get("ps_y_n", 2)
        if p["oproj_mode"] == "mm":
            ps_y = [ph.enter_context(nc.psum_tensor(f"apy{i}", [128, 512], F32))
                    for i in range(n_py)]
        else:
            ps_yt = ph.enter_context(nc.psum_tensor("apyt", [128, 1024], BF16))
        if sep_z:
            n_pz = p.get("ps_z_n", 2)
            ps_z = [ph.enter_context(nc.psum_tensor(f"apz{i}", [1, 512], F32))
                    for i in range(n_pz)]
            hd["ps_z"] = ps_z
            hd["n_pz"] = n_pz
        pt = [ph.enter_context(nc.sbuf_tensor(f"apt{i}", [128, 512], BF16))
              for i in range(2)]
        lnz = [ph.enter_context(nc.sbuf_tensor("alnz", [1, 512], F32))]
        rz = [ph.enter_context(nc.sbuf_tensor(f"arz{i}", [o_rows, 512], BF16))
              for i in range(2)]
        obuf = ph.enter_context(nc.sbuf_tensor("aobuf", [128, 8 * 512], BF16))
        xres = [ph.enter_context(nc.sbuf_tensor(f"axr{i}", [128, 1024], F32))
                for i in range(2)]
        hd["ps_oz"] = ps_oz
        hd["ps_s"] = ps_s
        obv = obuf[:].rearrange("p (cc q) -> p cc q", q=512)
        hd["obv"] = obv

        sem = {sn: c.sem(sn) for sn in semnames}
        sem_pe = sem["a_pe"]; sem_se = sem["a_se"]; sem_ve = sem["a_ve"]
        sem_xl = sem["a_xld"]; sem_st = sem["a_st"]

        def run(engname, eng):
            for e in sched:
                k = e["k"]
                if k == "qprep":
                    if p["qprep"] is not None:
                        p["qprep"](engname, eng, e["qb"], meta, hd)
                elif k == "hprep":
                    if p["hprep"] is not None:
                        p["hprep"](engname, eng, e["h"], e["qb"], e["hq"],
                                   meta, hd)
                elif k == "item":
                    j, h, qb, kt = e["j"], e["h"], e["qb"], e["kt"]
                    par = j % 2
                    if engname == "tensor":
                        t = eng
                        if j >= 2:
                            t.wait_ge(sem_se, items[j - 2]["se_e"])
                        ins = p["smm"](t, h, qb, kt, ps_s[par][:], meta, hd)
                        ins.then_inc(sem_pe, 1)
                        t.wait_ge(sem_se, e["se_e"])
                        if e["first"] and e["hq"] >= 2:
                            t.wait_ge(sem_ve, hqs[e["hq"] - 2]["norm"]["ve_o"])
                        if (e["first"] and sep_z and p.get("ps_z_n", 2) == 1
                                and e["hq"] >= 1):
                            t.wait_ge(sem_se, hqs[e["hq"] - 1]["norm"]["se_ln"])
                        ins = p["pv"](t, h, qb, kt, pt[j % 2][:],
                                      e["first"], e["last"], hd)
                        ins.then_inc(sem_pe, 1)
                    elif engname == "scalar":
                        s = eng
                        if e["dj"] is not None:
                            s.wait_ge(sem_ve, e["ve_d"])
                        else:
                            s.wait_ge(sem_pe, e["pe_s"])
                        if j >= 2:
                            s.wait_ge(sem_pe, items[j - 2]["pe_pv"])
                        kw = {}
                        sc = p["exp_scale"](h, qb, kt)
                        bi = p["exp_bias"](h, qb, kt)
                        if sc is not None:
                            kw["scale"] = sc
                        if bi is not None:
                            kw["bias"] = bi
                        s.activation(pt[j % 2][:], ps_s[par][:], AF.Exp, **kw)
                        s.drain().then_inc(sem_se, 1)
                    elif engname == "vector":
                        if e["dj"] is None:
                            continue
                        v = eng
                        v.wait_ge(sem_pe, e["pe_s"])
                        msk = (c.sb["cmask"] if p["mask_kind"] == "cmask"
                               else c.sb["dpatch"])
                        op = (ALU.add if p["mask_kind"] == "cmask"
                              else ALU.mult)
                        v.tensor_tensor(ps_s[par][:], ps_s[par][:],
                                        msk[:, e["dj"] * 512:(e["dj"] + 1) * 512],
                                        op)
                        v.drain().then_inc(sem_ve, 1)
                elif k == "norm":
                    hq, h = e["hq"], e["h"]
                    if engname == "tensor":
                        t = eng
                        if hq >= 1:
                            t.wait_ge(sem_se, hqs[hq - 1]["norm"]["se_rz"])
                        t.wait_ge(sem_se, e["se_ln"])
                        t.matmul(ps_b[:], c.sb["negones_f"][0:1, 0:o_rows],
                                 lnz[0][:], start=True, stop=False)
                        t.matmul(ps_b[:], c.sb["zeroblk"][:, 0:o_rows],
                                 c.sb["zpad"][:, 0:512],
                                 start=False, stop=True).then_inc(sem_pe, 1)
                    elif engname == "scalar":
                        s = eng
                        s.wait_ge(sem_pe, items[e["last_j"]]["pe_pv"])
                        zsrc = (ps_z[hq % n_pz][0:1, :] if sep_z
                                else ps_oz[hq % 2][64:65, :])
                        if hq >= 1:
                            s.wait_ge(sem_pe, hqs[hq - 1]["norm"]["pe_b"])
                        s.activation(lnz[0][:], zsrc, AF.Ln)
                        s.drain().then_inc(sem_se, 1)
                        s.wait_ge(sem_pe, e["pe_b"])
                        s.activation(rz[hq % 2][:], ps_b[:], AF.Exp)
                        s.drain().then_inc(sem_se, 1)
                    elif engname == "vector":
                        v = eng
                        v.wait_ge(sem_se, e["se_rz"])
                        if o_rows == 64:
                            dsl = obv[(h % 2) * 64:(h % 2) * 64 + 64, h // 2, :]
                        else:
                            dsl = obv[:, h, :]
                        v.tensor_tensor(dsl, ps_oz[hq % 2][0:o_rows, :],
                                        rz[hq % 2][:], ALU.mult)
                        v.drain().then_inc(sem_ve, 1)
                elif k == "oproj":
                    qb, t4 = e["qb"], e["t4"]
                    b = t4 % 2
                    tglob = qb * 4 + t4
                    if engname == "tensor":
                        t = eng
                        t.wait_ge(sem_ve, hqs[e["hq_end"]]["norm"]["ve_o"])
                        if p["oproj_mode"] == "mm":
                            for ob in range(2):
                                g = e["g"][ob]
                                if g["idx"] >= n_py:
                                    t.wait_ge(sem_ve,
                                              ygroups[g["idx"] - n_py]["ve"])
                                hasb = p["out_bias_row"] is not None
                                if hasb:
                                    t.matmul(
                                        ps_y[g["idx"] % n_py][:],
                                        c.sb["ones1"][0:1, 0:128],
                                        p["out_bias_row"][0:1,
                                                          ob * 512:(ob + 1) * 512],
                                        start=True, stop=False)
                                for oc in range(8):
                                    ins = t.matmul(
                                        ps_y[g["idx"] % n_py][:],
                                        obv[:, oc, t4 * 128:(t4 + 1) * 128],
                                        p["out_w"][:, oc,
                                                   ob * 512:(ob + 1) * 512],
                                        start=(oc == 0 and not hasb),
                                        stop=(oc == 7))
                                ins.then_inc(sem_pe, 1)
                        else:
                            g = e["g"][0]
                            if g["idx"] >= 1:
                                t.wait_ge(sem_ve, ygroups[g["idx"] - 1]["ve"])
                            for oc in range(8):
                                ins = t.transpose(
                                    ps_yt[:, oc * 128:(oc + 1) * 128],
                                    obv[:, oc, t4 * 128:(t4 + 1) * 128],
                                    c.sb["ident_bf"][:])
                            ins.then_inc(sem_pe, 1)
                    elif engname == "vector":
                        v = eng
                        if p["oproj_mode"] == "mm":
                            for ob in range(2):
                                g = e["g"][ob]
                                v.wait_ge(sem_pe, g["pe"])
                                v.wait_ge(sem_xl, e["xld"])
                                v.tensor_add(
                                    xres[b][:, ob * 512:(ob + 1) * 512],
                                    ps_y[g["idx"] % n_py][:],
                                    xres[b][:, ob * 512:(ob + 1) * 512])
                                v.drain().then_inc(sem_ve, 1)
                        else:
                            g = e["g"][0]
                            v.wait_ge(sem_pe, g["pe"])
                            v.wait_ge(sem_xl, e["xld"])
                            v.tensor_add(xres[b][:], ps_yt[:], xres[b][:])
                            v.drain().then_inc(sem_ve, 1)
                    elif engname == "sync":
                        sy = eng
                        if tglob >= 2:
                            sy.wait_ge(sem_st, oprojs[tglob - 2]["st"])
                        sy.dma_start(
                            out=xres[b][:],
                            in_=p["x_src"][tglob * 128:(tglob + 1) * 128, :]
                            ).then_inc(sem_xl, 16)
                        sy.wait_ge(sem_ve, e["g"][1]["ve"])
                        sy.dma_start(
                            out=p["x_dst"][tglob * 128:(tglob + 1) * 128, :],
                            in_=xres[b][:]).then_inc(sem_st, 16)

        for engname in ("tensor", "scalar", "vector", "sync", "gpsimd"):
            @getattr(c.block, engname)
            def _(eng, engname=engname):
                run(engname, eng)
                if engname == "sync":
                    eng.wait_ge(sem_st, CNT["a_st"][0])

        for sn in semnames:
            c.cnt[sn] = CNT[sn][0]
    c.barrier()
def emit_wload(c, dsts):
    ld = c.sem("w_ld")
    @c.block.gpsimd
    def _(g):
        for sb, dn in dsts:
            src = c.dram[dn]
            g.dma_start(out=sb[:].rearrange("p (dc f) -> p dc f",
                                            dc=src.shape[0] // 128),
                        in_=src.rearrange("(dc p) f -> p dc f", p=128)
                        ).then_inc(ld, 16)
    c.commit("w_ld", 16 * len(dsts))


def emit_kq(c, arena, karena, w_sb, bk_fm, bq_fm, qt_h, wq_dram):
    nc, st = c.nc, c.stack
    xv = arena[:].rearrange("p (dc q) -> p dc q", q=4096)
    kv = karena[:].rearrange("p (cc q) -> p cc q", q=4096)
    wv = w_sb[:].rearrange("p (dc f) -> p dc f", f=1024)
    pe = c.sem("kq_pe"); se = c.sem("kq_se"); hb = c.sem("kq_hb")
    wl = c.sem("w_ld")
    pe0 = c.base("kq_pe"); se0 = c.base("kq_se"); hb0 = c.base("kq_hb")
    wl_k = c.cnt["w_ld"]
    wl_q = wl_k + 16

    with contextlib.ExitStack() as ph:
        ps = [ph.enter_context(nc.psum_tensor(f"kqp{i}", [128, 512], F32))
              for i in range(8)]
        qstage = ph.enter_context(nc.sbuf_tensor("kqs", [128, 4096], BF16))

        def pe_at(isq, oc, tb):
            return pe0 + isq * 64 + oc * 8 + tb + 1

        def se_at(isq, oc, tb):
            return se0 + isq * 64 + oc * 8 + tb + 1

        @c.block.tensor
        def _(t):
            for isq in range(2):
                t.wait_ge(wl, wl_k if isq == 0 else wl_q)
                for oc in range(8):
                    for tb in range(8):
                        gi = isq * 8 + oc
                        if gi >= 1:
                            poc = (isq, oc - 1) if oc >= 1 else (isq - 1, 7)
                            t.wait_ge(se, se_at(poc[0], poc[1], tb))
                        for dc in range(8):
                            ins = t.matmul(ps[tb][:],
                                           wv[:, dc, oc * 128:(oc + 1) * 128],
                                           xv[:, dc, tb * 512:(tb + 1) * 512],
                                           start=(dc == 0), stop=(dc == 7))
                        ins.then_inc(pe, 1)

        @c.block.scalar
        def _(s):
            for isq in range(2):
                for oc in range(8):
                    for tb in range(8):
                        s.wait_ge(pe, pe_at(isq, oc, tb))
                        if isq == 0:
                            dst = kv[:, oc, tb * 512:(tb + 1) * 512]
                            bias = bk_fm[:, oc:oc + 1]
                        else:
                            if tb == 0 and oc >= 1:
                                s.wait_ge(hb, hb0 + 16 * oc)
                            dst = qstage[:, tb * 512:(tb + 1) * 512]
                            bias = bq_fm[:, oc:oc + 1]
                        s.activation(dst, ps[tb][:], AF.Identity, bias=bias)
                        s.drain().then_inc(se, 1)

        @c.block.gpsimd
        def _(g):
            g.wait_ge(pe, pe0 + 64)
            g.dma_start(out=w_sb[:].rearrange("p (dc f) -> p dc f", dc=8),
                        in_=wq_dram.rearrange("(dc p) f -> p dc f", p=128)
                        ).then_inc(wl, 16)
            for oc in range(8):
                g.wait_ge(se, se0 + 64 + oc * 8 + 8)
                g.dma_start(out=qt_h[oc, :, :],
                            in_=qstage[:]).then_inc(hb, 16)
            g.wait_ge(hb, hb0 + 16 * 8)

        c.commit("kq_pe", 128)
        c.commit("kq_se", 128)
        c.commit("kq_hb", 8 * 16)
        c.commit("w_ld", 16)
    c.barrier()


def emit_v65(c, varena, wv_sb, bv_row, xnt_h, w_ld_target):
    nc, st = c.nc, c.stack
    vv = varena[:].rearrange("p (t f) -> p t f", f=1024)
    wvv = wv_sb[:].rearrange("p (dc f) -> p dc f", f=1024)
    ld = c.sem("v_ld"); pe = c.sem("v_pe"); se = c.sem("v_se")
    ld0 = c.base("v_ld"); pe0 = c.base("v_pe"); se0 = c.base("v_se")

    with contextlib.ExitStack() as ph:
        xts = [ph.enter_context(nc.sbuf_tensor(f"vx{i}", [128, 8 * 128], BF16))
               for i in range(2)]
        ps = [ph.enter_context(nc.psum_tensor(f"vp{i}", [128, 512], F32))
              for i in range(4)]

        @c.block.gpsimd
        def _(g):
            g.dma_start(out=c.sb["onescol"][:],
                        in_=c.dram["c_onescol"][:, :]).then_inc(ld, 16)
            for t in range(NT):
                if t >= 2:
                    g.wait_ge(pe, pe0 + 2 * (t - 1))
                g.dma_start(out=xts[t % 2][:].rearrange("p (dc q) -> p dc q",
                                                        q=128),
                            in_=xnt_h[:, :, t * 128:(t + 1) * 128]
                            .rearrange("dc p q -> p dc q")).then_inc(ld, 16)

        @c.block.tensor
        def _(t_):
            t_.wait_ge(c.sem("w_ld"), w_ld_target)
            xw = [xts[i][:].rearrange("p (dc q) -> p dc q", q=128)
                  for i in range(2)]
            for t in range(NT):
                t_.wait_ge(ld, ld0 + 16 * (t + 2))
                for ob in range(2):
                    bank = (t % 2) * 2 + ob
                    if t >= 2:
                        t_.wait_ge(se, se0 + 2 * (t - 2) + ob + 1)
                    t_.matmul(ps[bank][:], c.sb["ones1"][0:1, 0:128],
                              bv_row[0:1, ob * 512:(ob + 1) * 512],
                              start=True, stop=False)
                    for dc in range(8):
                        ins = t_.matmul(ps[bank][:], xw[t % 2][:, dc, :],
                                        wvv[:, dc, ob * 512:(ob + 1) * 512],
                                        start=False, stop=(dc == 7))
                    ins.then_inc(pe, 1)

        @c.block.scalar
        def _(s):
            for t in range(NT):
                for ob in range(2):
                    bank = (t % 2) * 2 + ob
                    s.wait_ge(pe, pe0 + 2 * t + ob + 1)
                    s.activation(vv[:, t, ob * 512:(ob + 1) * 512],
                                 ps[bank][:], AF.Copy)
                    s.drain().then_inc(se, 1)

        c.commit("v_ld", 16 * (NT + 1))
        c.commit("v_pe", 2 * NT)
        c.commit("v_se", 2 * NT)
    c.barrier()


def emit_absmax(c, arena, cbuf, n_cc=8):
    nc = c.nc
    se = c.sem("am_se"); ve = c.sem("am_ve")
    se0 = c.base("am_se"); ve0 = c.base("am_ve")
    av = arena[:].rearrange("p (cc q) -> p cc q", q=4096)
    with contextlib.ExitStack() as ph:
        scr = [ph.enter_context(nc.sbuf_tensor(f"am{i}", [128, 4096], BF16))
               for i in range(2)]

        @c.block.scalar
        def _(s):
            for cc in range(n_cc):
                if cc >= 2:
                    s.wait_ge(ve, ve0 + cc - 1)
                s.activation(scr[cc % 2][:], av[:, cc, :], AF.Abs)
                s.drain().then_inc(se, 1)

        @c.block.vector
        def _(v):
            for cc in range(n_cc):
                v.wait_ge(se, se0 + cc + 1)
                v.reduce_max(cbuf[:, cc:cc + 1], scr[cc % 2][:], axis=AX.X)
                v.drain().then_inc(ve, 1)

        c.commit("am_se", n_cc)
        c.commit("am_ve", n_cc)
    c.barrier()


def sublayer_l0aa(c, x_src, x_dst):
    nc = c.nc
    xnt_h = c.dram["xnt_h"]
    qt_h = c.dram["qt_h"]
    hql = [(qb, h) for qb in range(NQB) for h in range(16)]
    with contextlib.ExitStack() as s1:
        emit_sub_consts(c, s1, CAUSAL_CONSTS)
        c.sb["onescol"] = s1.enter_context(
            nc.sbuf_tensor("onescol", [128, 1], BF16))
        emit_bias_prep(c, [("l0aa_bk_fm", "l0aa_bk"),
                           ("l0aa_bq_fm", "l0aa_bq")],
                       [("l0aa_bv_row", "l0aa_bv"), ("l0aa_bo_row", "l0aa_bo")],
                       st=s1)
        karena = s1.enter_context(nc.sbuf_tensor("karena", [128, 8 * 4096], BF16))
        cbuf = s1.enter_context(nc.sbuf_tensor("cbuf", [128, 8], BF16))
        qhead = [s1.enter_context(nc.sbuf_tensor(f"qh{i}", [128, 512], BF16))
                 for i in range(2)]
        absb = [s1.enter_context(nc.sbuf_tensor(f"ab{i}", [128, 512], BF16))
                for i in range(2)]
        shrow = [s1.enter_context(nc.sbuf_tensor(f"sh{i}", [1, 512], BF16))
                 for i in range(2)]
        slqb = [s1.enter_context(nc.sbuf_tensor(f"sq{i}", [1, 512], BF16))
                for i in range(2)]

        with contextlib.ExitStack() as s2:
            arena = s2.enter_context(nc.sbuf_tensor("arena", [128, 8 * 4096],
                                                    BF16))
            av = arena[:].rearrange("p (cc q) -> p cc q", q=4096)
            emit_ln_transpose(c, x_src,
                              lambda i, t: av[:, :, t * 128:(t + 1) * 128],
                              arena_to_hbm=(arena, xnt_h))
            with contextlib.ExitStack() as s25:
                w_sb = s25.enter_context(nc.sbuf_tensor("w_sb",
                                                        [128, 8 * 1024], BF16))
                emit_wload(c, [(w_sb, "l0aa_wk")])
                emit_kq(c, arena, karena, w_sb, c.sb["l0aa_bk_fm"],
                        c.sb["l0aa_bq_fm"], qt_h, c.dram["l0aa_wq"])
        varena = s1.enter_context(nc.sbuf_tensor("varena",
                                                 [128, NT * 1024], BF16))
        with contextlib.ExitStack() as s3:
            wv_sb = s3.enter_context(nc.sbuf_tensor("wv_sb", [128, 8 * 1024],
                                                    BF16))
            emit_wload(c, [(wv_sb, "l0aa_wv")])
            emit_v65(c, varena, wv_sb, c.sb["l0aa_bv_row"][:], xnt_h,
                     c.cnt["w_ld"])
        emit_absmax(c, karena, cbuf)
        if DEBUG:
            dsem = c.sem("dbg")
            @c.block.gpsimd
            def _(g):
                g.dma_start(out=c.dram["dbg_k"][:, :],
                            in_=karena[:]).then_inc(dsem, 16)
                g.dma_start(out=c.dram["dbg_v"][:, :],
                            in_=varena[:]).then_inc(dsem, 16)
                g.dma_start(out=c.dram["dbg_c"][:, :],
                            in_=cbuf[:]).then_inc(dsem, 16)
                g.wait_ge(dsem, c.cnt["dbg"] + 48)
            c.commit("dbg", 48)
            c.barrier()

        wo_sb = s1.enter_context(nc.sbuf_tensor("wo_sb", [128, 8 * 1024], BF16))
        ps_m = s1.enter_context(nc.psum_tensor("apsm", [1, 512], F32))
        emit_wload(c, [(wo_sb, "l0aa_wo")])
        wlt_o = c.cnt["w_ld"]

        kvv = karena[:].rearrange("p (cc q) -> p cc q", q=4096)
        vvv = varena[:].rearrange("p (t f) -> p t f", f=1024)

        def hprep(eng, E, h, qb, hq, meta, hd):
            mk = meta["hq"][(qb, h)]["prep"]
            r0 = (h % 2) * 64
            if eng == "gpsimd":
                if hq >= 2:
                    pmk = meta["hq"][hql[hq - 2]]
                    E.wait_ge(c.sem("a_pe"), pmk["pe_s_last"])
                    E.wait_ge(c.sem("a_ve"), pmk["prep"]["a_ve"])
                E.dma_start(out=qhead[hq % 2][r0:r0 + 64, :],
                            in_=qt_h[h // 2, r0:r0 + 64,
                                     qb * 512:(qb + 1) * 512]
                            ).then_inc(c.sem("a_gld"), 16)
                E.dma_start(out=slqb[hq % 2][:],
                            in_=c.dram["c_slq8"][0:1,
                                                 h * 4096 + qb * 512:
                                                 h * 4096 + (qb + 1) * 512]
                            ).then_inc(c.sem("a_gld"), 16)
            elif eng == "scalar":
                E.wait_ge(c.sem("a_gld"), mk["a_gld"])
                if hq >= 2:
                    E.wait_ge(c.sem("a_pe"), meta["hq"][hql[hq - 2]]["prep"]["a_pe"])
                E.activation(absb[hq % 2][r0:r0 + 64, :],
                             qhead[hq % 2][r0:r0 + 64, :], AF.Abs)
                E.drain().then_inc(c.sem("a_se"), 1)
            elif eng == "tensor":
                E.wait_ge(c.sem("a_se"), mk["a_se"])
                if hq >= 1:
                    E.wait_ge(c.sem("a_ve"), meta["hq"][hql[hq - 1]]["prep"]["a_ve"])
                E.matmul(ps_m[:], cbuf[r0:r0 + 64, h // 2:h // 2 + 1],
                         absb[hq % 2][r0:r0 + 64, :],
                         start=True, stop=True).then_inc(c.sem("a_pe"), 1)
            elif eng == "vector":
                E.wait_ge(c.sem("a_pe"), mk["a_pe"])
                E.wait_ge(c.sem("a_gld"), mk["a_gld"])
                E.scalar_tensor_tensor(shrow[hq % 2][:], ps_m[:], -1.0,
                                       slqb[hq % 2][:], ALU.mult, ALU.subtract)
                E.drain().then_inc(c.sem("a_ve"), 1)

        def smm(t, h, qb, kt, ps_ap, meta, hd):
            hq = qb * 16 + h
            mk = meta["hq"][(qb, h)]["prep"]
            r0 = (h % 2) * 64
            t.wait_ge(c.sem("a_ve"), mk["a_ve"])
            t.matmul(ps_ap, c.sb["ones1"][0:1, 0:128],
                     shrow[hq % 2][0:1, :], start=True, stop=False)
            return t.matmul(ps_ap,
                            kvv[r0:r0 + 64, h // 2, kt * 128:(kt + 1) * 128],
                            qhead[hq % 2][r0:r0 + 64, :],
                            start=False, stop=True)

        def pv(t, h, qb, kt, pt_ap, start, stop, hd):
            hq = qb * 16 + h
            t.matmul(hd["ps_oz"][hq % 2][0:64, :],
                     vvv[:, kt, h * 64:(h + 1) * 64], pt_ap,
                     start=start, stop=stop)
            return t.matmul(hd["ps_z"][0][0:1, :], c.sb["onescol"][:, 0:1],
                            pt_ap, start=start, stop=stop)

        def qprep(eng, E, qb, meta, hd):
            if eng == "tensor" and qb == 0:
                E.wait_ge(c.sem("w_ld"), wlt_o)

        p = {
            "H": 16,
            "kts": lambda qb: list(range(4 * qb + 4)),
            "diag": lambda qb, kt: (kt - 4 * qb) if kt >= 4 * qb else None,
            "mask_kind": "cmask",
            "sep_z": True, "ps_z_n": 1, "oz_rows": 64, "o_rows": 64,
            "qprep": qprep, "qprep_incs": {},
            "hprep": hprep,
            "hprep_incs": {"a_gld": 32, "a_se": 1, "a_pe": 1, "a_ve": 1},
            "smm": smm, "pv": pv,
            "exp_scale": lambda h, qb, kt: 0.125,
            "exp_bias": lambda h, qb, kt:
                c.sb["alibi_k"][:, h * 32 + kt:h * 32 + kt + 1],
            "out_w": wo_sb[:].rearrange("p (dc f) -> p dc f", f=1024),
            "out_bias_row": c.sb["l0aa_bo_row"][:],
            "oproj_mode": "mm", "ps_y_n": 1,
            "x_src": x_src, "x_dst": x_dst,
        }
        emit_attention(c, p)




def sublayer_l1aa(c, x_src, x_dst):
    nc = c.nc
    hql = [(qb, h) for qb in range(NQB) for h in range(16)]
    with contextlib.ExitStack() as s1:
        emit_sub_consts(c, s1, CAUSAL_CONSTS)
        c.sb["onescol"] = s1.enter_context(
            nc.sbuf_tensor("onescol", [128, 1], BF16))
        arena = s1.enter_context(nc.sbuf_tensor("arena", [128, 8 * 4096], BF16))
        varena = s1.enter_context(nc.sbuf_tensor("varena",
                                                 [128, NT * 1024], BF16))
        cbuf = s1.enter_context(nc.sbuf_tensor("cbuf", [128, 8], BF16))
        absb = [s1.enter_context(nc.sbuf_tensor(f"ab{i}", [128, 512], BF16))
                for i in range(2)]
        shrow = [s1.enter_context(nc.sbuf_tensor(f"sh{i}", [1, 512], BF16))
                 for i in range(2)]
        av = arena[:].rearrange("p (cc q) -> p cc q", q=4096)
        vv = varena[:].rearrange("p (t f) -> p t f", f=1024)

        ld1 = c.sem("g_ld")
        @c.block.gpsimd
        def _(g):
            g.dma_start(out=c.sb["onescol"][:],
                        in_=c.dram["c_onescol"][:, :]).then_inc(ld1, 16)
            g.wait_ge(ld1, c.cnt["g_ld"] + 16)
        c.commit("g_ld", 16)
        ve_sem = c.sem("ln_ve")

        def xn_cb(v, i, t, xn_ap, stat):
            v.tensor_copy(vv[:, t, :], xn_ap[:]).then_inc(ve_sem, 1)

        emit_ln_transpose(c, x_src,
                          lambda i, t: av[:, :, t * 128:(t + 1) * 128],
                          xn_cb=xn_cb, n_cb_inc=1)
        emit_absmax(c, arena, cbuf)
        ps_m = s1.enter_context(nc.psum_tensor("apsm", [1, 512], F32))

        def hprep(eng, E, h, qb, hq, meta, hd):
            mk = meta["hq"][(qb, h)]["prep"]
            r0 = (h % 2) * 64
            if eng == "scalar":
                if hq >= 2:
                    E.wait_ge(c.sem("a_pe"),
                              meta["hq"][hql[hq - 2]]["prep"]["a_pe"])
                E.activation(absb[hq % 2][r0:r0 + 64, :],
                             av[r0:r0 + 64, h // 2,
                                qb * 512:(qb + 1) * 512], AF.Abs)
                E.drain().then_inc(c.sem("a_se"), 1)
            elif eng == "tensor":
                E.wait_ge(c.sem("a_se"), mk["a_se"])
                if hq >= 1:
                    E.wait_ge(c.sem("a_ve"),
                              meta["hq"][hql[hq - 1]]["prep"]["a_ve"])
                E.matmul(ps_m[:], cbuf[r0:r0 + 64, h // 2:h // 2 + 1],
                         absb[hq % 2][r0:r0 + 64, :],
                         start=True, stop=True).then_inc(c.sem("a_pe"), 1)
            elif eng == "vector":
                E.wait_ge(c.sem("a_pe"), mk["a_pe"])
                E.tensor_single_scalar(shrow[hq % 2][:], ps_m[:], -1.0,
                                       ALU.mult)
                E.drain().then_inc(c.sem("a_ve"), 1)

        def smm(t, h, qb, kt, ps_ap, meta, hd):
            hq = qb * 16 + h
            mk = meta["hq"][(qb, h)]["prep"]
            r0 = (h % 2) * 64
            t.wait_ge(c.sem("a_ve"), mk["a_ve"])
            t.matmul(ps_ap, c.sb["ones1"][0:1, 0:128],
                     shrow[hq % 2][0:1, :], start=True, stop=False)
            return t.matmul(ps_ap,
                            av[r0:r0 + 64, h // 2, kt * 128:(kt + 1) * 128],
                            av[r0:r0 + 64, h // 2, qb * 512:(qb + 1) * 512],
                            start=False, stop=True)

        def pv(t, h, qb, kt, pt_ap, start, stop, hd):
            hq = qb * 16 + h
            t.matmul(hd["ps_oz"][hq % 2][0:64, :],
                     vv[:, kt, h * 64:(h + 1) * 64], pt_ap,
                     start=start, stop=stop)
            return t.matmul(hd["ps_z"][0][0:1, :], c.sb["onescol"][:, 0:1],
                            pt_ap, start=start, stop=stop)

        p = {
            "H": 16,
            "kts": lambda qb: list(range(4 * qb + 4)),
            "diag": lambda qb, kt: (kt - 4 * qb) if kt >= 4 * qb else None,
            "mask_kind": "cmask",
            "sep_z": True, "ps_z_n": 1, "oz_rows": 64, "o_rows": 64,
            "qprep": None, "qprep_incs": {},
            "hprep": hprep,
            "hprep_incs": {"a_se": 1, "a_pe": 1, "a_ve": 1},
            "smm": smm, "pv": pv,
            "exp_scale": lambda h, qb, kt: 0.125,
            "exp_bias": lambda h, qb, kt: None,
            "out_w": None, "out_bias_row": None,
            "oproj_mode": "tr",
            "x_src": x_src, "x_dst": x_dst,
        }
        emit_attention(c, p)


def declare_l1aa(c, nc):
    pass


def prep_l1aa(c):
    pass


def declare_common(c, nc):
    for name, arr in host_consts().items():
        c.dram[name] = nc.dram_tensor(name, list(arr.shape), F32,
                                      kind="ExternalInput").ap()
    dbg = "ExternalOutput" if DEBUG else "Internal"
    c.dram["xnt_h"] = nc.dram_tensor("xnt_h", [8, 128, 4096], BF16,
                                     kind=dbg).ap()
    c.dram["qt_h"] = nc.dram_tensor("qt_h", [8, 128, 4096], BF16,
                                    kind=dbg).ap()
    c.dram["qtok_h"] = nc.dram_tensor("qtok_h", [32, 128, 1024], BF16,
                                      kind="Internal").ap()
    if DEBUG:
        c.dram["dbg_sc"] = nc.dram_tensor("dbg_sc", [128, 8 * 16], F32,
                                          kind="ExternalOutput").ap()
        c.dram["dbg_st"] = nc.dram_tensor("dbg_st", [128, 8], F32,
                                          kind="ExternalOutput").ap()
        c.dram["dbg_r"] = nc.dram_tensor("dbg_r", [128, 72], F32,
                                         kind="ExternalOutput").ap()
        c.dram["dbg_cmb"] = nc.dram_tensor("dbg_cmb", [128, 8 * 16], F32,
                                           kind="ExternalOutput").ap()
        c.dram["dbg_l2f"] = nc.dram_tensor("dbg_l2f", [128, 8], F32,
                                           kind="ExternalOutput").ap()
        c.dram["dbg_acc"] = nc.dram_tensor("dbg_acc", [128, 8 * 1024], F32,
                                           kind="ExternalOutput").ap()
        c.dram["dbg_h"] = nc.dram_tensor("dbg_h", [128, 4 * 1024], BF16,
                                         kind="ExternalOutput").ap()
        c.dram["dbg_q"] = nc.dram_tensor("dbg_q", [128, 8 * 1024], BF16,
                                         kind="ExternalOutput").ap()
        c.dram["dbg_k"] = nc.dram_tensor("dbg_k", [128, 8 * 4096], BF16,
                                         kind="ExternalOutput").ap()
        c.dram["dbg_v"] = nc.dram_tensor("dbg_v", [128, NT * 16 * 65], BF16,
                                         kind="ExternalOutput").ap()
        c.dram["dbg_c"] = nc.dram_tensor("dbg_c", [128, 8], BF16,
                                         kind="ExternalOutput").ap()
        c.dram["dbg_o"] = nc.dram_tensor("dbg_o", [8, 128, 8 * 512], BF16,
                                         kind="ExternalOutput").ap()
    rkind = "ExternalOutput" if DEBUG2 else "Internal"
    c.dram["resid0"] = nc.dram_tensor("resid0", [4096, 1024], F32,
                                      kind=rkind).ap()
    c.dram["resid1"] = nc.dram_tensor("resid1", [4096, 1024], F32,
                                      kind=rkind).ap()


def declare_l0aa(c, nc):
    for n_, sh in [("l0aa_wq", [1024, 1024]), ("l0aa_wk", [1024, 1024]),
                   ("l0aa_wv", [1024, 1024]), ("l0aa_wo", [1024, 1024]),
                   ("l0aa_bq", [1, 1024]), ("l0aa_bk", [1, 1024]),
                   ("l0aa_bv", [1, 1024]), ("l0aa_bo", [1, 1024])]:
        c.dram[n_] = nc.dram_tensor(n_, sh, F32, kind="ExternalInput").ap()


def prep_l0aa(c):
    pass


PART_DECL = {"l0aa": declare_l0aa, "l1aa": declare_l1aa}
PART_PREP = {"l0aa": prep_l0aa, "l1aa": prep_l1aa}
PART_EMIT = {"l0aa": sublayer_l0aa, "l1aa": sublayer_l1aa}




def host_inputs_per_core(inputs, core):
    m = {}
    m["x_in"] = np.ascontiguousarray(inputs["L0_attn_a_X"][core])
    wqkv = inputs["L0_attn_a_W_qkv"].reshape(1024, 16, 3, 64)
    bqkv = inputs["L0_attn_a_b_qkv"].reshape(16, 3, 64)
    m["l0aa_wq"] = np.ascontiguousarray(wqkv[:, :, 0, :].reshape(1024, 1024))
    m["l0aa_wk"] = np.ascontiguousarray(wqkv[:, :, 1, :].reshape(1024, 1024))
    m["l0aa_wv"] = np.ascontiguousarray(wqkv[:, :, 2, :].reshape(1024, 1024))
    m["l0aa_bq"] = np.ascontiguousarray(bqkv[:, 0, :].reshape(1, 1024))
    m["l0aa_bk"] = np.ascontiguousarray(bqkv[:, 1, :].reshape(1, 1024))
    m["l0aa_bv"] = np.ascontiguousarray(bqkv[:, 2, :].reshape(1, 1024))
    m["l0aa_wo"] = inputs["L0_attn_a_W_out"]
    m["l0aa_bo"] = inputs["L0_attn_a_b_out"].reshape(1, 1024)
    host_inputs_ffn(inputs, m)
    host_inputs_l0ab(inputs, m)
    host_inputs_l1ab(inputs, m)
    for cname, arr in host_consts().items():
        m[cname] = arr
    return m


def build_nc(parts):
    nc = bass.Bass("TRN2", target_bir_lowering=False)
    x_in = nc.dram_tensor("x_in", [4096, 1024], F32, kind="ExternalInput").ap()
    x_out = nc.dram_tensor("x_out", [4096, 1024], F32,
                           kind="ExternalOutput").ap()
    stack = contextlib.ExitStack()
    with stack:
        c = Ctx(nc, stack)
        declare_common(c, nc)
        for pn in parts:
            PART_DECL[pn](c, nc)
        emit_consts(c)
        for pn in parts:
            PART_PREP[pn](c)
        for i, pn in enumerate(parts):
            src = x_in if i == 0 else c.dram[f"resid{i % 2}"]
            dst = (x_out if i == len(parts) - 1
                   else c.dram[f"resid{(i + 1) % 2}"])
            PART_EMIT[pn](c, src, dst)
    return nc


def kernel(**inputs):
    maps = [host_inputs_per_core(inputs, core) for core in range(N_CORES)]
    parts = ["l0aa", "l0f1", "l0ab", "l0f2", "l1aa", "l1f1", "l1ab", "l1f2"]
    nc = build_nc(parts)
    prefixes = {PART_PREFIX[p] for p in parts}
    in_maps = []
    for core in range(N_CORES):
        m = {"x_in": maps[core]["x_in"]}
        for k, v in maps[core].items():
            if k.startswith("c_") or any(k.startswith(p) for p in prefixes):
                m[k] = v
        in_maps.append(m)
    res = run_bass_kernel_spmd(nc, in_maps, core_ids=list(range(N_CORES)))
    return np.stack([r["x_out"] for r in res.results], axis=0)


def emit_gwn_prep_l0(c):
    nc, st = c.nc, c.stack
    out = st.enter_context(nc.sbuf_tensor("l0f_gwnT", [128, 8 * 16], BF16))
    c.sb["l0f_gwnT"] = out
    ld = c.sem("g_ld"); pe = c.sem("g_pe"); ve = c.sem("g_ve"); se = c.sem("g_se")
    ld0 = c.base("g_ld"); pe0 = c.base("g_pe"); ve0 = c.base("g_ve")
    se0 = c.base("g_se")
    with contextlib.ExitStack() as ph:
        gw = ph.enter_context(nc.sbuf_tensor("gwp", [16, 1024], F32))
        gs = ph.enter_context(nc.sbuf_tensor("gwst", [16, 4], F32))
        gwn = ph.enter_context(nc.sbuf_tensor("gwnb", [16, 1024], BF16))
        scr = ph.enter_context(nc.sbuf_tensor("gwsc", [16, 1024], F32))
        ps = ph.enter_context(nc.psum_tensor("gwps", [128, 8 * 16], BF16))

        @c.block.gpsimd
        def _(g):
            g.dma_start(out=gw[:], in_=c.dram["l0f_gate_w"][:, :]).then_inc(ld, 16)
            g.wait_ge(ld, ld0 + 16)

        @c.block.scalar
        def _(s):
            s.wait_ge(ld, ld0 + 16)
            s.activation(scr[:], gw[:], AF.Square, accum_out=gs[:, 0:1])
            s.activation(gs[:, 1:2], gs[:, 0:1], AF.Ln)
            s.activation(gs[:, 2:3], gs[:, 1:2], AF.Exp, scale=-0.5)
            s.drain().then_inc(se, 1)

        @c.block.vector
        def _(v):
            v.wait_ge(se, se0 + 1)
            v.tensor_scalar(gwn[:], gw[:], gs[:, 2:3], None, ALU.mult)
            v.drain().then_inc(ve, 1)

        @c.block.tensor
        def _(t):
            t.wait_ge(ve, ve0 + 1)
            for cc in range(8):
                ins = t.transpose(ps[:, cc * 16:(cc + 1) * 16],
                                  gwn[:, cc * 128:(cc + 1) * 128],
                                  c.sb["ident_bf"][0:16, 0:16])
            ins.then_inc(pe, 1)

        @c.block.vector
        def _(v):
            v.wait_ge(pe, pe0 + 1)
            v.tensor_copy(out[:], ps[:])
            v.drain().then_inc(ve, 1)

        c.commit("g_ld", 16)
        c.commit("g_se", 1)
        c.commit("g_pe", 1)
        c.commit("g_ve", 2)
    c.barrier()


def _emit_routing(c, L, bufs):
    nc = c.nc
    ve = c.sem("f_ve"); se = c.sem("f_se")
    ve0 = c.base("f_ve"); se0 = c.base("f_se")
    s3, w3, c3, k3 = bufs["s3"], bufs["w3"], bufs["c3"], bufs["k3"]
    mred = bufs["mred"]
    m83 = mred[:, 0:8].rearrange("p (t o) -> p t o", o=1)
    rw3 = mred[:, 8:16].rearrange("p (t o) -> p t o", o=1)
    junk = mred[:, 8:9]

    def bc(ap, like):
        a, _ = bass.broadcast_tensor_aps(ap, like)
        return a

    @c.block.vector
    def _(v):
        def pad(n=4):
            for _i in range(n):
                v.tensor_single_scalar(bufs["mred"][:, 15:16],
                                       bufs["s3"][:, 0, 0:1], 1.0, ALU.mult)
        nrounds = 8 if L["route"] == "l0" else 4
        if L["route"] == "l0":
            v.tensor_copy(w3, s3)
            v.tensor_copy(k3, s3)
        else:
            scores2 = bufs["scores2"]
            wrk2 = bufs["wrk2"]
            gm = bufs["gm4"]
            gs = bufs["gs4"]
            gm2 = bufs["gm2"]
            gs2 = bufs["gs2"]
            ta, tb_, tc, td = bufs["tt4"]
            ta2, tb2, tc2, td2 = bufs["tt2"]
            A = scores2[:].rearrange("p (n e) -> p n e", e=4)[:, :, 0:1]
            B = scores2[:].rearrange("p (n e) -> p n e", e=4)[:, :, 1:2]
            Cc = scores2[:].rearrange("p (n e) -> p n e", e=4)[:, :, 2:3]
            Dd = scores2[:].rearrange("p (n e) -> p n e", e=4)[:, :, 3:4]
            t_a = ta2.rearrange("p (n o) -> p n o", o=1)
            t_b = tb2.rearrange("p (n o) -> p n o", o=1)
            t_c = tc2.rearrange("p (n o) -> p n o", o=1)
            t_d = td2.rearrange("p (n o) -> p n o", o=1)
            v.tensor_max(t_a, A, B)
            v.tensor_tensor(t_b, A, B, ALU.min)
            v.tensor_max(t_c, Cc, Dd)
            v.tensor_tensor(t_d, Cc, Dd, ALU.min)
            pad()
            v.tensor_tensor(gm2, ta2, tc2, ALU.min)
            v.tensor_max(gs2, tb2, td2)
            pad()
            v.tensor_max(gm2, gm2, gs2)
            v.tensor_max(gs2, ta2, tc2)
            pad()
            v.tensor_add(gs2, gs2, gm2)
            pad()
            gsv = gs2.rearrange("p (t g) -> p t g", g=4)
            v.tensor_max(ta[:, :, 0:1], gsv[:, :, 0:1], gsv[:, :, 1:2])
            v.tensor_tensor(tb_[:, :, 0:1], gsv[:, :, 0:1], gsv[:, :, 1:2],
                            ALU.min)
            v.tensor_max(tc[:, :, 0:1], gsv[:, :, 2:3], gsv[:, :, 3:4])
            v.tensor_tensor(td[:, :, 0:1], gsv[:, :, 2:3], gsv[:, :, 3:4],
                            ALU.min)
            pad()
            v.tensor_tensor(ta[:, :, 0:1], ta[:, :, 0:1], tc[:, :, 0:1],
                            ALU.min)
            v.tensor_max(tb_[:, :, 0:1], tb_[:, :, 0:1], td[:, :, 0:1])
            pad()
            v.tensor_max(m83, ta[:, :, 0:1], tb_[:, :, 0:1])
            pad()
            v.tensor_tensor(gm, gs, bc(m83, gs), ALU.is_ge)
            pad()
            if "dbg_gm" in bufs:
                v.tensor_copy(bufs["dbg_gm"], gm)
                v.tensor_copy(bufs["dbg_g2"], m83)
                v.tensor_copy(bufs["dbg_gs"], gs)
            gmb = bc(gm.rearrange("p t (g o) -> p t g o", o=1), bufs["s4"])
            v.tensor_tensor(bufs["w4"], bufs["s4"], gmb, ALU.mult)
            v.tensor_copy(k3, w3)
        for r in range(nrounds):
            v.reduce_max(m83, w3, axis=AX.X)
            v.reduce_sum(junk[:, 0:1], s3[:, 0, 0:2], axis=AX.X)
            if r < nrounds - 1:
                v.tensor_tensor(c3, w3, bc(m83, w3), ALU.is_ge)
                v.scalar_tensor_tensor(w3, c3, -1e30, w3, ALU.mult, ALU.add)
        v.tensor_tensor(c3, k3, bc(m83, k3), ALU.is_ge)
        v.tensor_mul(w3, k3, c3)
        v.reduce_sum(m83, w3, axis=AX.X)
        v.reduce_sum(junk[:, 0:1], s3[:, 0, 0:2], axis=AX.X)
        v.drain().then_inc(ve, 1)
        v.wait_ge(se, se0 + 1)
        v.tensor_tensor(c3, w3, bc(rw3, w3), ALU.mult)
        v.drain().then_inc(ve, 1)

    @c.block.scalar
    def _(s):
        s.wait_ge(ve, ve0 + 1)
        if L["route"] == "l0":
            s.activation(mred[:, 8:16], mred[:, 0:8], AF.Ln)
            s.activation(mred[:, 8:16], mred[:, 8:16], AF.Exp, scale=-1.0)
        else:
            s.activation(mred[:, 8:16], mred[:, 0:8], AF.Ln,
                         bias=c.sb["kconst"][:, 4:5])
            s.activation(mred[:, 8:16], mred[:, 8:16], AF.Exp, scale=-1.0,
                         bias=c.sb["kconst"][:, 5:6])
        s.drain().then_inc(se, 1)

    c.commit("f_ve", 2)
    c.commit("f_se", 1)
    c.barrier()


def _emit_experts(c, L, units, qv, av, hv, gbuf, c3, wgb, wub, wdb):
    nc = c.nc
    has_gate = L["has_gate"]
    pe = c.sem("f_pe"); se = c.sem("f_se"); ve = c.sem("f_ve")
    gl = c.sem("f_gl")
    pe0 = c.base("f_pe"); se0 = c.base("f_se"); ve0 = c.base("f_ve")
    gl0 = c.base("f_gl")

    ngrp_hc = 2 * (2 if has_gate else 1)
    pe_u = 4 * ngrp_hc + 16
    se_u = 8
    ve_u = (8 if has_gate else 0) + 16
    ld_u = 16 * (3 if has_gate else 2)

    def pe_gate(u, hc, tb):
        return pe0 + u * pe_u + hc * ngrp_hc + tb + 1

    def pe_up(u, hc, tb):
        return pe0 + u * pe_u + hc * ngrp_hc + 2 + tb + 1

    def pe_down(u, g16):
        return pe0 + u * pe_u + 4 * ngrp_hc + g16 + 1

    def se_silu(u, hc, tb):
        return se0 + u * se_u + hc * 2 + tb + 1

    def ve_hm(u, hc, tb):
        return ve0 + u * ve_u + hc * 2 + tb + 1

    def ve_acc(u, g16):
        return ve0 + u * ve_u + (8 if has_gate else 0) + g16 + 1

    NU = len(units)
    with contextlib.ExitStack() as ph:
        ps_g = [ph.enter_context(nc.psum_tensor(f"fpg{i}", [128, 512], F32))
                for i in range(2)]
        ps_u = ([ph.enter_context(nc.psum_tensor(f"fpu{i}", [128, 512], F32))
                 for i in range(2)] if has_gate else ps_g)
        ps_y = [ph.enter_context(nc.psum_tensor(f"fpy{i}", [128, 512], F32))
                for i in range(2)]

        @c.block.gpsimd
        def _(g):
            for u, un in enumerate(units):
                b = u % 2
                if u >= 2:
                    g.wait_ge(pe, pe0 + (u - 1) * pe_u)
                g.dma_start(out=wgb[b][:].rearrange("p (dc f) -> p dc f", f=512),
                            in_=un["wg"].rearrange("(dc p) f -> p dc f", p=128)
                            ).then_inc(gl, 16)
                if has_gate:
                    g.dma_start(out=wub[b][:].rearrange("p (dc f) -> p dc f",
                                                        f=512),
                                in_=un["wu"].rearrange("(dc p) f -> p dc f",
                                                       p=128)).then_inc(gl, 16)
                g.dma_start(out=wdb[b][:].rearrange("p (hc f) -> p hc f",
                                                    f=1024),
                            in_=un["wd"].rearrange("(hc p) f -> p hc f", p=128)
                            ).then_inc(gl, 16)
            g.wait_ge(gl, gl0 + ld_u * NU)

        @c.block.tensor
        def _(t):
            for u, un in enumerate(units):
                b = u % 2
                wgv = wgb[b][:].rearrange("p (dc f) -> p dc f", f=512)
                wuv = (wub[b][:].rearrange("p (dc f) -> p dc f", f=512)
                       if has_gate else None)
                wdv = wdb[b][:].rearrange("p (hc f) -> p hc f", f=1024)
                t.wait_ge(gl, gl0 + ld_u * (u + 1))
                for hc in range(4):
                    for srcsel in range(2 if has_gate else 1):
                        psx = ps_g if srcsel == 0 else ps_u
                        wv = wgv if srcsel == 0 else wuv
                        if u > 0 or hc > 0:
                            phc, pu = (hc - 1, u) if hc > 0 else (3, u - 1)
                            for tb in range(2):
                                if srcsel == 0 or not has_gate:
                                    t.wait_ge(se, se_silu(pu, phc, tb))
                                else:
                                    t.wait_ge(ve, ve_hm(pu, phc, tb))
                        last = {}
                        for dc in range(8):
                            for tb in range(2):
                                ins = t.matmul(
                                    psx[tb][:],
                                    wv[:, dc, hc * 128:(hc + 1) * 128],
                                    qv[:, dc, tb * 512:(tb + 1) * 512],
                                    start=(dc == 0), stop=(dc == 7))
                                if dc == 7:
                                    last[tb] = ins
                        last[0].then_inc(pe, 1)
                        last[1].then_inc(pe, 1)
                for g16 in range(16):
                    t8, ob = divmod(g16, 2)
                    gg = u * 16 + g16
                    if gg >= 2:
                        pu2, pg2 = divmod(gg - 2, 16)
                        t.wait_ge(ve, ve_acc(pu2, pg2))
                    if g16 == 0:
                        if has_gate:
                            t.wait_ge(ve, ve_hm(u, 3, 1))
                        else:
                            t.wait_ge(se, se_silu(u, 3, 1))
                    for hcc in range(4):
                        ins = t.matmul(ps_y[gg % 2][:],
                                       hv[:, hcc, t8 * 128:(t8 + 1) * 128],
                                       wdv[:, hcc, ob * 512:(ob + 1) * 512],
                                       start=(hcc == 0), stop=(hcc == 3))
                    ins.then_inc(pe, 1)

        @c.block.scalar
        def _(s):
            for u, un in enumerate(units):
                for hc in range(4):
                    for tb in range(2):
                        s.wait_ge(pe, pe_gate(u, hc, tb))
                        if has_gate:
                            if u > 0 or hc > 0:
                                phc, pu = (hc - 1, u) if hc > 0 else (3, u - 1)
                                s.wait_ge(ve, ve_hm(pu, phc, tb))
                            s.activation(gbuf[:, tb * 512:(tb + 1) * 512],
                                         ps_g[tb][:], AF.Silu)
                        else:
                            s.activation(hv[:, hc, tb * 512:(tb + 1) * 512],
                                         ps_g[tb][:], AF.Silu)
                        s.drain().then_inc(se, 1)

        @c.block.vector
        def _(v):
            for u, un in enumerate(units):
                if has_gate:
                    for hc in range(4):
                        for tb in range(2):
                            v.wait_ge(se, se_silu(u, hc, tb))
                            v.wait_ge(pe, pe_up(u, hc, tb))
                            v.tensor_mul(hv[:, hc, tb * 512:(tb + 1) * 512],
                                         gbuf[:, tb * 512:(tb + 1) * 512],
                                         ps_u[tb][:])
                            v.drain().then_inc(ve, 1)
                for g16 in range(16):
                    t8, ob = divmod(g16, 2)
                    gg = u * 16 + g16
                    v.wait_ge(pe, pe_down(u, g16))
                    dst = av[:, t8, ob * 512:(ob + 1) * 512]
                    if un["cmb"] is None:
                        if u == 0:
                            v.tensor_copy(dst, ps_y[gg % 2][:])
                        else:
                            v.tensor_add(dst, ps_y[gg % 2][:], dst)
                    else:
                        col = c3[:, t8, un["cmb"]:un["cmb"] + 1]
                        if u == 0:
                            v.tensor_scalar(dst, ps_y[gg % 2][:], col, None,
                                            ALU.mult)
                        else:
                            v.scalar_tensor_tensor(dst, ps_y[gg % 2][:], col,
                                                   dst, ALU.mult, ALU.add)
                    v.drain().then_inc(ve, 1)

        c.commit("f_gl", ld_u * NU)
        c.commit("f_pe", pe_u * NU)
        c.commit("f_se", se_u * NU)
        c.commit("f_ve", ve_u * NU)
    c.barrier()


def sublayer_ffn(c, x_src, x_dst, L):
    nc = c.nc
    px = L["prefix"]
    has_gate = L["has_gate"]
    with contextlib.ExitStack() as s1:
        qarena = s1.enter_context(nc.sbuf_tensor("qarena", [128, 8 * 1024], BF16))
        acc = s1.enter_context(nc.sbuf_tensor("facc", [128, 8 * 1024], F32))
        harena = s1.enter_context(nc.sbuf_tensor("harena", [128, 4 * 1024], BF16))
        scores = s1.enter_context(nc.sbuf_tensor("fsc", [128, 8 * 16], F32))
        wrk = s1.enter_context(nc.sbuf_tensor("fwrk", [128, 8 * 16], F32))
        keep = s1.enter_context(nc.sbuf_tensor("fkp", [128, 8 * 16], F32))
        mred = s1.enter_context(nc.sbuf_tensor("fm", [128, 16], F32))
        cmb = s1.enter_context(nc.sbuf_tensor("fcmb", [128, 8 * 16], F32))
        gm4 = s1.enter_context(nc.sbuf_tensor("fgm", [128, 8 * 4], F32))
        gs4 = s1.enter_context(nc.sbuf_tensor("fgs", [128, 8 * 4], F32))
        tt4 = s1.enter_context(nc.sbuf_tensor("ftt", [128, 8 * 16], F32))
        gbuf = s1.enter_context(nc.sbuf_tensor("fgb", [128, 1024], BF16))
        l2fb = s1.enter_context(nc.sbuf_tensor("fl2f", [128, 8], F32))
        dbgst = s1.enter_context(nc.sbuf_tensor("fdbgst", [128, 8], F32))
        dbgr = s1.enter_context(nc.sbuf_tensor("fdbgr", [128, 72], F32))
        xres = [s1.enter_context(nc.sbuf_tensor(f"fxr{i}", [128, 1024], F32))
                for i in range(2)]
        wgb = [s1.enter_context(nc.sbuf_tensor(f"fwg{i}", [128, 8 * 512], BF16))
               for i in range(2)]
        wub = ([s1.enter_context(nc.sbuf_tensor(f"fwu{i}", [128, 8 * 512],
                                                BF16)) for i in range(2)]
               if has_gate else None)
        wdb = [s1.enter_context(nc.sbuf_tensor(f"fwd{i}", [128, 4 * 1024], BF16))
               for i in range(2)]
        qv = qarena[:].rearrange("p (dc q) -> p dc q", q=1024)
        av = acc[:].rearrange("p (t f) -> p t f", f=1024)
        hv = harena[:].rearrange("p (hc f) -> p hc f", f=1024)
        s3 = scores[:].rearrange("p (t e) -> p t e", e=16)
        w3 = wrk[:].rearrange("p (t e) -> p t e", e=16)
        c3 = cmb[:].rearrange("p (t e) -> p t e", e=16)
        k3 = keep[:].rearrange("p (t e) -> p t e", e=16)
        bufs = {"s3": s3, "w3": w3, "c3": c3, "k3": k3, "mred": mred[:],
                "scores2": scores, "wrk2": wrk,
                "gm2": gm4[:], "gs2": gs4[:],
                "tt2": [tt4[:, i * 32:(i + 1) * 32] for i in range(4)],

                "s4": scores[:].rearrange("p (t g e) -> p t g e", g=4, e=4),
                "w4": wrk[:].rearrange("p (t g e) -> p t g e", g=4, e=4),
                "gm4": gm4[:].rearrange("p (t g) -> p t g", g=4),
                "gs4": gs4[:].rearrange("p (t g) -> p t g", g=4),
                "tt4": [tt4[:].rearrange("p (i t g) -> i p t g", i=4)[i]
                        if False else
                        tt4[:, i * 32:(i + 1) * 32].rearrange(
                            "p (t g) -> p t g", g=4) for i in range(4)]}

        def mk_units():
            units = []
            if L["route"] == "l0":
                for e in range(16):
                    units.append({"wg": c.dram[f"{px}_w_gate"][e],
                                  "wu": c.dram[f"{px}_w_up"][e],
                                  "wd": c.dram[f"{px}_w_down"][e], "cmb": e})
                units.append({"wg": c.dram[f"{px}_sw_gate"],
                              "wu": c.dram[f"{px}_sw_up"],
                              "wd": c.dram[f"{px}_sw_down"], "cmb": None})
            else:
                for e in range(16):
                    units.append({"wg": c.dram[f"{px}_expert_up"][e],
                                  "wu": None,
                                  "wd": c.dram[f"{px}_expert_down"][e],
                                  "cmb": e})
                for g in range(4):
                    units.append({"wg": c.dram[f"{px}_shared_up"]
                                  [:, g * 512:(g + 1) * 512],
                                  "wu": None,
                                  "wd": c.dram[f"{px}_shared_down"]
                                  [g * 512:(g + 1) * 512, :], "cmb": None})
            return units

        for q in range(4):
            tiles = list(range(q * 8, q * 8 + 8))
            if L["route"] == "l0":
                ln_sc = c.sem("ln_sc")
                sc0 = c.base("ln_sc")

                def act_cb(s, i, t, stat, xn_ap, xsq, q=q):
                    s.activation(xsq[:], xn_ap[:], AF.Square,
                                 accum_out=stat[:, 8:9])
                    s.activation(stat[:, 9:10], stat[:, 8:9], AF.Ln)
                    s.activation(l2fb[:, i:i + 1], stat[:, 9:10], AF.Exp,
                                 scale=-0.5)
                    s.drain().then_inc(ln_sc, 1)

                def pre_xn(v, i, sc0=sc0):
                    if i >= 2:
                        v.wait_ge(ln_sc, sc0 + i - 1)

                emit_ln_transpose(c, x_src,
                                  lambda i, t: qv[:, :, i * 128:(i + 1) * 128],
                                  tiles=tiles, act_cb=act_cb, n_acb_inc=1,
                                  pre_xn=pre_xn)
                c.commit("ln_sc", 8)
            else:
                emit_ln_transpose(c, x_src,
                                  lambda i, t: qv[:, :, i * 128:(i + 1) * 128],
                                  tiles=tiles)

            pe = c.sem("f_pe"); se = c.sem("f_se")
            pe0 = c.base("f_pe"); se0 = c.base("f_se")
            gwT = (c.sb["l0f_gwnT"] if L["route"] == "l0" else c.sb["l1f_gwT"])
            gv = gwT[:].rearrange("p (dc e) -> p dc e", e=16)
            with contextlib.ExitStack() as phb:
                ps_r = phb.enter_context(nc.psum_tensor("fpr", [128, 16], F32))

                @c.block.tensor
                def _(t):
                    for i in range(8):
                        if i >= 1:
                            t.wait_ge(se, se0 + i)
                        for dc in range(8):
                            ins = t.matmul(ps_r[:],
                                           qv[:, dc, i * 128:(i + 1) * 128],
                                           gv[:, dc, :], start=(dc == 0),
                                           stop=(dc == 7))
                        ins.then_inc(pe, 1)

                @c.block.scalar
                def _(s):
                    for i in range(8):
                        s.wait_ge(pe, pe0 + i + 1)
                        kw = {}
                        if L["route"] == "l0":
                            kw["scale"] = l2fb[:, i:i + 1]
                        s.activation(s3[:, i, :], ps_r[:], AF.Sigmoid, **kw)
                        s.drain().then_inc(se, 1)

                c.commit("f_pe", 8)
                c.commit("f_se", 8)
            c.barrier()

            if DEBUG and q == 0 and L["route"] == "l1":
                bufs["dbg_gm"] = dbgr[:, 0:32].rearrange("p (t g) -> p t g", g=4)
                bufs["dbg_g2"] = dbgr[:, 32:40].rearrange("p (t o) -> p t o", o=1)
                bufs["dbg_gs"] = dbgr[:, 40:72].rearrange("p (t g) -> p t g", g=4)
            _emit_routing(c, L, bufs)
            bufs.pop("dbg_gm", None)
            bufs.pop("dbg_g2", None)
            bufs.pop("dbg_gs", None)
            if DEBUG and q == 0:
                dsem = c.sem("dbg")
                d0 = c.base("dbg")
                @c.block.gpsimd
                def _(g):
                    g.dma_start(out=c.dram["dbg_sc"][:, :],
                                in_=scores[:]).then_inc(dsem, 16)
                    g.dma_start(out=c.dram["dbg_cmb"][:, :],
                                in_=cmb[:]).then_inc(dsem, 16)
                    g.dma_start(out=c.dram["dbg_l2f"][:, :],
                                in_=l2fb[:]).then_inc(dsem, 16)
                    g.dma_start(out=c.dram["dbg_q"][:, :],
                                in_=qarena[:]).then_inc(dsem, 16)
                    g.dma_start(out=c.dram["dbg_st"][:, :],
                                in_=dbgst[:]).then_inc(dsem, 16)
                    g.dma_start(out=c.dram["dbg_r"][:, :],
                                in_=dbgr[:]).then_inc(dsem, 16)
                    g.wait_ge(dsem, d0 + 96)
                c.commit("dbg", 96)
                c.barrier()

            _emit_experts(c, L, mk_units(), qv, av, hv, gbuf, c3, wgb, wub, wdb)

            if DEBUG and q == 0:
                dsem = c.sem("dbg")
                d0 = c.base("dbg")
                @c.block.gpsimd
                def _(g):
                    g.dma_start(out=c.dram["dbg_acc"][:, :],
                                in_=acc[:]).then_inc(dsem, 16)
                    g.dma_start(out=c.dram["dbg_h"][:, :],
                                in_=harena[:]).then_inc(dsem, 16)
                    g.wait_ge(dsem, d0 + 32)
                c.commit("dbg", 32)
                c.barrier()

            ve = c.sem("f_ve"); xl = c.sem("f_xl"); sts = c.sem("f_st")
            ve0 = c.base("f_ve"); xl0 = c.base("f_xl"); st0 = c.base("f_st")

            @c.block.sync
            def _(sy):
                for i, t in enumerate(tiles):
                    b = i % 2
                    if i >= 2:
                        sy.wait_ge(sts, st0 + 16 * (i - 1))
                    sy.dma_start(out=xres[b][:],
                                 in_=x_src[t * 128:(t + 1) * 128, :]
                                 ).then_inc(xl, 16)
                    sy.wait_ge(ve, ve0 + i + 1)
                    sy.dma_start(out=x_dst[t * 128:(t + 1) * 128, :],
                                 in_=xres[b][:]).then_inc(sts, 16)
                sy.wait_ge(sts, st0 + 16 * 8)

            @c.block.vector
            def _(v):
                for i, t in enumerate(tiles):
                    b = i % 2
                    v.wait_ge(xl, xl0 + 16 * (i + 1))
                    v.tensor_add(xres[b][:], av[:, i, :], xres[b][:])
                    v.drain().then_inc(ve, 1)

            c.commit("f_xl", 16 * 8)
            c.commit("f_st", 16 * 8)
            c.commit("f_ve", 8)
            c.barrier()


def sublayer_l0f(c, x_src, x_dst):
    sublayer_ffn(c, x_src, x_dst,
                 {"prefix": "l0f", "has_gate": True, "route": "l0"})


def sublayer_l1f(c, x_src, x_dst):
    sublayer_ffn(c, x_src, x_dst,
                 {"prefix": "l1f", "has_gate": False, "route": "l1"})


def declare_l0f(c, nc):
    if "l0f_gate_w" in c.dram:
        return
    for n_, sh in [("l0f_gate_w", [16, 1024]),
                   ("l0f_w_gate", [16, 1024, 512]),
                   ("l0f_w_up", [16, 1024, 512]),
                   ("l0f_w_down", [16, 512, 1024]),
                   ("l0f_sw_gate", [1024, 512]), ("l0f_sw_up", [1024, 512]),
                   ("l0f_sw_down", [512, 1024])]:
        c.dram[n_] = nc.dram_tensor(n_, sh, F32, kind="ExternalInput").ap()


def declare_l1f(c, nc):
    if "l1f_gwT" in c.dram:
        return
    for n_, sh in [("l1f_gwT", [1024, 16]),
                   ("l1f_expert_up", [16, 1024, 512]),
                   ("l1f_expert_down", [16, 512, 1024]),
                   ("l1f_shared_up", [1024, 2048]),
                   ("l1f_shared_down", [2048, 1024])]:
        c.dram[n_] = nc.dram_tensor(n_, sh, F32, kind="ExternalInput").ap()


def prep_l0f(c):
    if "l0f_gwnT" in c.sb:
        return
    emit_gwn_prep_l0(c)


def prep_l1f(c):
    if "l1f_gwT" in c.sb:
        return
    nc, st = c.nc, c.stack
    out = st.enter_context(nc.sbuf_tensor("l1f_gwT", [128, 8 * 16], BF16))
    c.sb["l1f_gwT"] = out
    ld = c.sem("g_ld")
    ld0 = c.base("g_ld")

    @c.block.gpsimd
    def _(g):
        g.dma_start(out=out[:].rearrange("p (dc e) -> p dc e", e=16),
                    in_=c.dram["l1f_gwT"].rearrange("(dc p) e -> p dc e",
                                                    p=128)).then_inc(ld, 16)
        g.wait_ge(ld, ld0 + 16)
    c.commit("g_ld", 16)
    c.barrier()


PART_PREFIX = {"l0aa": "l0aa", "l1aa": "l1aa", "l0f1": "l0f", "l0f2": "l0f",
               "l1f1": "l1f", "l1f2": "l1f", "l0ab": "l0ab", "l1ab": "l1ab"}
PART_DECL.update({"l0f1": declare_l0f, "l0f2": declare_l0f,
                  "l1f1": declare_l1f, "l1f2": declare_l1f})
PART_PREP.update({"l0f1": prep_l0f, "l0f2": prep_l0f,
                  "l1f1": prep_l1f, "l1f2": prep_l1f})
PART_EMIT.update({"l0f1": sublayer_l0f, "l0f2": sublayer_l0f,
                  "l1f1": sublayer_l1f, "l1f2": sublayer_l1f})


def host_inputs_ffn(inputs, m):
    m["l0f_gate_w"] = inputs["L0_ffn_gate_w"]
    m["l0f_w_gate"] = inputs["L0_ffn_w_gate"]
    m["l0f_w_up"] = inputs["L0_ffn_w_up"]
    m["l0f_w_down"] = inputs["L0_ffn_w_down"]
    m["l0f_sw_gate"] = inputs["L0_ffn_sw_gate"]
    m["l0f_sw_up"] = inputs["L0_ffn_sw_up"]
    m["l0f_sw_down"] = inputs["L0_ffn_sw_down"]
    m["l1f_gwT"] = np.ascontiguousarray(inputs["L1_ffn_gate_weight"].T)
    m["l1f_expert_up"] = inputs["L1_ffn_expert_up"]
    m["l1f_expert_down"] = inputs["L1_ffn_expert_down"]
    m["l1f_shared_up"] = inputs["L1_ffn_shared_up"]
    m["l1f_shared_down"] = inputs["L1_ffn_shared_down"]


def emit_kvs(c, arena, w_sb, sT_arena):
    nc = c.nc
    xv = arena[:].rearrange("p (dc q) -> p dc q", q=4096)
    sv = sT_arena[:].rearrange("p (cc q) -> p cc q", q=4096)
    wv = w_sb[:].rearrange("p (dc f) -> p dc f", f=2048)
    pe = c.sem("kv_pe"); ve = c.sem("kv_ve"); ev = c.sem("kv_ev")
    pe0 = c.base("kv_pe"); ve0 = c.base("kv_ve"); ev0 = c.base("kv_ev")
    with contextlib.ExitStack() as ph:
        ps = [ph.enter_context(nc.psum_tensor(f"kvp{i}", [128, 512], F32))
              for i in range(4)]
        ps_t = [ph.enter_context(nc.psum_tensor(f"kvt{i}", [128, 1024], BF16))
                for i in range(2)]
        ksb = ph.enter_context(nc.sbuf_tensor("kvk", [128, 1024], F32))
        sbf = [ph.enter_context(nc.sbuf_tensor(f"kvs{i}", [128, 1024], BF16))
               for i in range(2)]

        @c.block.tensor
        def _(t):
            t.wait_ge(c.sem("w_ld"), c.cnt["w_ld"])
            for ti in range(NT):
                if ti >= 1:
                    t.wait_ge(ve, ve0 + 2 * ti)
                for dc in range(8):
                    for j in range(4):
                        ins = t.matmul(ps[j][:],
                                       xv[:, dc, ti * 128:(ti + 1) * 128],
                                       wv[:, dc, j * 512:(j + 1) * 512],
                                       start=(dc == 0), stop=(dc == 7))
                        if dc == 7:
                            ins.then_inc(pe, 1)
                t.wait_ge(ve, ve0 + 2 * ti + 2)
                if ti >= 2:
                    t.wait_ge(ev, ev0 + ti - 1)
                for cc in range(8):
                    ins = t.transpose(ps_t[ti % 2][:, cc * 128:(cc + 1) * 128],
                                      sbf[ti % 2][:, cc * 128:(cc + 1) * 128],
                                      c.sb["ident_bf"][:])
                ins.then_inc(pe, 1)

        @c.block.vector
        def _(v):
            for ti in range(NT):
                v.wait_ge(pe, pe0 + 5 * ti + 4)
                v.tensor_copy(ksb[:, 0:512], ps[0][:])
                v.tensor_copy(ksb[:, 512:1024], ps[1][:])
                v.drain().then_inc(ve, 1)
                v.tensor_add(sbf[ti % 2][:, 0:512], ps[2][:], ksb[:, 0:512])
                v.tensor_add(sbf[ti % 2][:, 512:1024], ps[3][:],
                             ksb[:, 512:1024])
                v.drain().then_inc(ve, 1)
                v.wait_ge(pe, pe0 + 5 * ti + 5)
                v.tensor_copy(sv[:, :, ti * 128:(ti + 1) * 128],
                              ps_t[ti % 2][:].rearrange("p (cc q) -> p cc q",
                                                        q=128))
                v.drain().then_inc(ev, 1)

        c.commit("kv_pe", 5 * NT)
        c.commit("kv_ve", 2 * NT)
        c.commit("kv_ev", NT)
    c.barrier()


def emit_latent(c, sT_arena, wd_sb, latT):
    nc = c.nc
    sv = sT_arena[:].rearrange("p (dc q) -> p dc q", q=4096)
    wv = wd_sb[:].rearrange("p (dc f) -> p dc f", f=256)
    lv = latT[:].rearrange("p (lc q) -> p lc q", q=4096)
    pe = c.sem("lt_pe"); se = c.sem("lt_se")
    pe0 = c.base("lt_pe"); se0 = c.base("lt_se")
    with contextlib.ExitStack() as ph:
        ps = [ph.enter_context(nc.psum_tensor(f"ltp{i}", [128, 512], F32))
              for i in range(8)]

        @c.block.tensor
        def _(t):
            t.wait_ge(c.sem("w_ld"), c.cnt["w_ld"])
            for oc in range(2):
                for tb in range(8):
                    if oc == 1:
                        t.wait_ge(se, se0 + tb + 1)
                    for dc in range(8):
                        ins = t.matmul(ps[tb][:],
                                       wv[:, dc, oc * 128:(oc + 1) * 128],
                                       sv[:, dc, tb * 512:(tb + 1) * 512],
                                       start=(dc == 0), stop=(dc == 7))
                    ins.then_inc(pe, 1)

        @c.block.scalar
        def _(s):
            for oc in range(2):
                for tb in range(8):
                    s.wait_ge(pe, pe0 + oc * 8 + tb + 1)
                    s.activation(lv[:, oc, tb * 512:(tb + 1) * 512],
                                 ps[tb][:], AF.Copy)
                    s.drain().then_inc(se, 1)

        c.commit("lt_pe", 16)
        c.commit("lt_se", 16)
    c.barrier()


def emit_recup(c, latT, w_sb, karena, varena):
    nc = c.nc
    lv = latT[:].rearrange("p (lc q) -> p lc q", q=4096)
    wv = w_sb[:].rearrange("p (lc f) -> p lc f", f=2048)
    kv = karena[:].rearrange("p (cc q) -> p cc q", q=4096)
    vv = varena[:].rearrange("p (t f) -> p t f", f=1024)
    pe = c.sem("ru_pe"); se = c.sem("ru_se"); ve = c.sem("ru_ve")
    ev = c.sem("ru_ev")
    pe0 = c.base("ru_pe"); se0 = c.base("ru_se"); ve0 = c.base("ru_ve")
    ev0 = c.base("ru_ev")
    with contextlib.ExitStack() as ph:
        ps = [ph.enter_context(nc.psum_tensor(f"rup{i}", [128, 512], F32))
              for i in range(4)]
        ps_t = [ph.enter_context(nc.psum_tensor(f"rut{i}", [128, 1024], BF16))
                for i in range(2)]
        ktok = [ph.enter_context(nc.sbuf_tensor(f"ruk{i}", [128, 1024], BF16))
                for i in range(2)]

        @c.block.tensor
        def _(t):
            t.wait_ge(c.sem("w_ld"), c.cnt["w_ld"])
            for ti in range(NT):
                if ti >= 1:
                    t.wait_ge(se, se0 + 4 * ti)
                for lc in range(2):
                    for j in range(4):
                        ins = t.matmul(ps[j][:],
                                       lv[:, lc, ti * 128:(ti + 1) * 128],
                                       wv[:, lc, j * 512:(j + 1) * 512],
                                       start=(lc == 0), stop=(lc == 1))
                        if lc == 1:
                            ins.then_inc(pe, 1)
                t.wait_ge(se, se0 + 4 * ti + 2)
                if ti >= 2:
                    t.wait_ge(ev, ev0 + ti - 1)
                for cc in range(8):
                    ins = t.transpose(ps_t[ti % 2][:, cc * 128:(cc + 1) * 128],
                                      ktok[ti % 2][:, cc * 128:(cc + 1) * 128],
                                      c.sb["ident_bf"][:])
                ins.then_inc(pe, 1)

        @c.block.scalar
        def _(s):
            for ti in range(NT):
                for j in range(2):
                    s.wait_ge(pe, pe0 + 5 * ti + j + 1)
                    s.activation(ktok[ti % 2][:, j * 512:(j + 1) * 512],
                                 ps[j][:], AF.Copy)
                    s.drain().then_inc(se, 1)
                for j in range(2):
                    s.wait_ge(pe, pe0 + 5 * ti + j + 3)
                    s.activation(vv[:, ti, j * 512:(j + 1) * 512],
                                 ps[2 + j][:], AF.Copy)
                    s.drain().then_inc(se, 1)

        @c.block.vector
        def _(v):
            for ti in range(NT):
                v.wait_ge(pe, pe0 + 5 * ti + 5)
                v.tensor_copy(kv[:, :, ti * 128:(ti + 1) * 128],
                              ps_t[ti % 2][:].rearrange("p (cc q) -> p cc q",
                                                        q=128))
                v.drain().then_inc(ev, 1)

        c.commit("ru_pe", 5 * NT)
        c.commit("ru_se", 4 * NT)
        c.commit("ru_ev", NT)
    c.barrier()


def emit_qtok(c, arena, wq_sb, qtok_h):
    nc = c.nc
    xv = arena[:].rearrange("p (dc q) -> p dc q", q=4096)
    wv = wq_sb[:].rearrange("p (dc f) -> p dc f", f=1024)
    pe = c.sem("qt_pe"); se = c.sem("qt_se"); hb = c.sem("qt_hb")
    pe0 = c.base("qt_pe"); se0 = c.base("qt_se"); hb0 = c.base("qt_hb")
    with contextlib.ExitStack() as ph:
        ps = [ph.enter_context(nc.psum_tensor(f"qtp{i}", [128, 512], F32))
              for i in range(2)]
        stage = [ph.enter_context(nc.sbuf_tensor(f"qts{i}", [128, 1024], BF16))
                 for i in range(2)]

        @c.block.tensor
        def _(t):
            t.wait_ge(c.sem("w_ld"), c.cnt["w_ld"])
            for ti in range(NT):
                for ob in range(2):
                    if ti >= 1:
                        t.wait_ge(se, se0 + 2 * (ti - 1) + ob + 1)
                    for dc in range(8):
                        ins = t.matmul(ps[ob][:],
                                       xv[:, dc, ti * 128:(ti + 1) * 128],
                                       wv[:, dc, ob * 512:(ob + 1) * 512],
                                       start=(dc == 0), stop=(dc == 7))
                    ins.then_inc(pe, 1)

        @c.block.scalar
        def _(s):
            for ti in range(NT):
                for ob in range(2):
                    s.wait_ge(pe, pe0 + 2 * ti + ob + 1)
                    if ti >= 2:
                        s.wait_ge(hb, hb0 + 16 * (ti - 1))
                    s.activation(stage[ti % 2][:, ob * 512:(ob + 1) * 512],
                                 ps[ob][:], AF.Copy)
                    s.drain().then_inc(se, 1)

        @c.block.gpsimd
        def _(g):
            for ti in range(NT):
                g.wait_ge(se, se0 + 2 * ti + 2)
                g.dma_start(out=qtok_h[ti, :, :],
                            in_=stage[ti % 2][:]).then_inc(hb, 16)
            g.wait_ge(hb, hb0 + 16 * NT)

        c.commit("qt_pe", 2 * NT)
        c.commit("qt_se", 2 * NT)
        c.commit("qt_hb", 16 * NT)
    c.barrier()


def sublayer_l0ab(c, x_src, x_dst):
    nc = c.nc
    xnt_h = c.dram["xnt_h"]
    qtok_h = c.dram["qtok_h"]
    with contextlib.ExitStack() as s1:
        emit_sub_consts(c, s1, DECAY_CONSTS)
        karena = s1.enter_context(nc.sbuf_tensor("karena", [128, 8 * 4096],
                                                 BF16))
        with contextlib.ExitStack() as s2:
            arena = s2.enter_context(nc.sbuf_tensor("arena", [128, 8 * 4096],
                                                    BF16))
            av = arena[:].rearrange("p (cc q) -> p cc q", q=4096)
            emit_ln_transpose(c, x_src,
                              lambda i, t: av[:, :, t * 128:(t + 1) * 128])
            with contextlib.ExitStack() as s25:
                w_sb = s25.enter_context(nc.sbuf_tensor("wkv",
                                                        [128, 8 * 2048], BF16))
                emit_wload(c, [(w_sb, "l0ab_wkv")])
                emit_kvs(c, arena, w_sb, karena)
            with contextlib.ExitStack() as s26:
                wq_sb = s26.enter_context(nc.sbuf_tensor("wqq",
                                                         [128, 8 * 1024],
                                                         BF16))
                emit_wload(c, [(wq_sb, "l0ab_wq")])
                emit_qtok(c, arena, wq_sb, qtok_h)
        varena = s1.enter_context(nc.sbuf_tensor("varena",
                                                 [128, NT * 1024], BF16))
        with contextlib.ExitStack() as s3:
            latT = s3.enter_context(nc.sbuf_tensor("latT", [128, 2 * 4096],
                                                   BF16))
            with contextlib.ExitStack() as s35:
                wd_sb = s35.enter_context(nc.sbuf_tensor("wdkv",
                                                         [128, 8 * 256], BF16))
                emit_wload(c, [(wd_sb, "l0ab_wdown")])
                emit_latent(c, karena, wd_sb, latT)
            with contextlib.ExitStack() as s36:
                wu_sb = s36.enter_context(nc.sbuf_tensor("wup",
                                                         [128, 2 * 2048],
                                                         BF16))
                emit_wload(c, [(wu_sb, "l0ab_wup")])
                emit_recup(c, latT, wu_sb, karena, varena)
        wo_sb = s1.enter_context(nc.sbuf_tensor("wo_sb", [128, 8 * 1024],
                                                BF16))
        emit_wload(c, [(wo_sb, "l0ab_wo")])
        qlo = s1.enter_context(nc.sbuf_tensor("qlo", [128, 8 * 512], BF16))
        qhi = s1.enter_context(nc.sbuf_tensor("qhi", [128, 8 * 512], BF16))
        qbt = s1.enter_context(nc.sbuf_tensor("qbt", [128, 4 * 1024], BF16))
        qtk = [s1.enter_context(nc.sbuf_tensor(f"qtk{i}", [128, 1024], BF16))
               for i in range(2)]
        ps_qt = s1.enter_context(nc.psum_tensor("psqt", [128, 1024], BF16))

        kvv = karena[:].rearrange("p (cc q) -> p cc q", q=4096)
        vvv = varena[:].rearrange("p (t f) -> p t f", f=1024)
        qlov = qlo[:].rearrange("p (cc q) -> p cc q", q=512)
        qhiv = qhi[:].rearrange("p (cc q) -> p cc q", q=512)
        qbv = qbt[:].rearrange("p (t f) -> p t f", f=1024)
        aq = c.sem("aq_ld"); aqp = c.sem("aq_pe"); aqs = c.sem("aq_se")
        aqv = c.sem("aq_ve")
        aq0 = c.base("aq_ld"); aqp0 = c.base("aq_pe"); aqs0 = c.base("aq_se")
        aqv0 = c.base("aq_ve")

        def pe_at(qb, t4, i):
            return aqp0 + 8 * qb + 2 * t4 + i

        def se_at(qb, t4, i):
            return aqs0 + 8 * qb + 2 * t4 + i

        def ve_at(qb, t4, i):
            return aqv0 + 8 * qb + 2 * t4 + i

        def qprep(eng, E, qb, meta, hd):
            if eng == "gpsimd":
                if qb >= 1:
                    E.wait_ge(aqs, aqs0 + 8 * qb)
                E.dma_start(out=qbt[:].rearrange("p (t f) -> p t f", f=1024),
                            in_=qtok_h[qb * 4:(qb + 1) * 4, :, :]
                            .rearrange("t p f -> p t f")).then_inc(aq, 16)
            elif eng == "scalar":
                E.wait_ge(aq, aq0 + 16 * (qb + 1))
                for t4 in range(4):
                    tglob = qb * 4 + t4
                    for lohi in range(2):
                        if qb > 0 or t4 > 0:
                            pqb, pt4 = (qb, t4 - 1) if t4 > 0 else (qb - 1, 3)
                            E.wait_ge(aqp, pe_at(pqb, pt4, lohi + 1))
                        scl = (c.sb["dqlo"] if lohi == 0
                               else c.sb["dqhi"])[:, tglob:tglob + 1]
                        E.activation(qtk[lohi][:], qbv[:, t4, :],
                                     AF.Identity, scale=scl)
                        E.drain().then_inc(aqs, 1)
            elif eng == "tensor":
                for t4 in range(4):
                    for lohi in range(2):
                        E.wait_ge(aqs, se_at(qb, t4, lohi + 1))
                        if qb > 0 or t4 > 0 or lohi > 0:
                            pqb, pt4, pl = ((qb, t4, 0) if lohi == 1 else
                                            ((qb, t4 - 1, 1) if t4 > 0
                                             else (qb - 1, 3, 1)))
                            E.wait_ge(aqv, ve_at(pqb, pt4, pl + 1))
                        if qb >= 1 and t4 == 0 and lohi == 0:
                            E.wait_ge(c.sem("a_pe"),
                                      meta["hq"][(qb - 1, 15)]["pe_s_last"])
                        for cc in range(8):
                            ins = E.transpose(
                                ps_qt[:, cc * 128:(cc + 1) * 128],
                                qtk[lohi][:, cc * 128:(cc + 1) * 128],
                                c.sb["ident_bf"][:])
                        ins.then_inc(aqp, 1)
            elif eng == "vector":
                for t4 in range(4):
                    for lohi in range(2):
                        E.wait_ge(aqp, pe_at(qb, t4, lohi + 1))
                        dst = (qlov if lohi == 0 else qhiv)[
                            :, :, t4 * 128:(t4 + 1) * 128]
                        E.tensor_copy(dst, ps_qt[:].rearrange(
                            "p (cc q) -> p cc q", q=128))
                        E.drain().then_inc(aqv, 1)

        def smm(t, h, qb, kt, ps_ap, meta, hd):
            r0 = (h % 2) * 64
            upper = kt > 4 * qb + 3
            qsrc = (qhiv if upper else qlov)[r0:r0 + 64, h // 2, :]
            return t.matmul(ps_ap, kvv[r0:r0 + 64, h // 2,
                                       kt * 128:(kt + 1) * 128],
                            qsrc, start=True, stop=True)

        def pv(t, h, qb, kt, pt_ap, start, stop, hd):
            hq = qb * 16 + h
            t.matmul(hd["ps_oz"][hq % 2][0:64, :],
                     vvv[:, kt, h * 64:(h + 1) * 64], pt_ap,
                     start=start, stop=stop)
            return t.matmul(hd["ps_z"][0][0:1, :], c.sb["onescol"][:, 0:1],
                            pt_ap, start=start, stop=stop)

        def hprep(eng, E, h, qb, hq, meta, hd):
            if eng == "tensor" and h == 0:
                E.wait_ge(aqv, aqv0 + 8 * (qb + 1))

        p = {
            "H": 16,
            "kts": lambda qb: list(range(32)),
            "diag": lambda qb, kt: (kt - 4 * qb)
            if 4 * qb <= kt <= 4 * qb + 3 else None,
            "mask_kind": "patch",
            "sep_z": True, "ps_z_n": 1, "oz_rows": 64, "o_rows": 64,
            "qprep": qprep, "qprep_incs": {},
            "hprep": hprep, "hprep_incs": {},
            "smm": smm, "pv": pv,
            "exp_scale": lambda h, qb, kt:
                (c.sb["dcolinv"] if kt > 4 * qb + 3 else
                 c.sb["dcol"])[:, kt:kt + 1],
            "exp_bias": lambda h, qb, kt: c.sb["kconst"][:, 1:2],
            "out_w": wo_sb[:].rearrange("p (dc f) -> p dc f", f=1024),
            "out_bias_row": None,
            "oproj_mode": "mm", "ps_y_n": 1,
            "x_src": x_src, "x_dst": x_dst,
        }
        emit_attention(c, p)
        c.commit("aq_ld", 16 * NQB)
        c.commit("aq_pe", 8 * NQB)
        c.commit("aq_se", 8 * NQB)
        c.commit("aq_ve", 8 * NQB)


def declare_l0ab(c, nc):
    for n_, sh in [("l0ab_wkv", [1024, 2048]), ("l0ab_wdown", [1024, 256]),
                   ("l0ab_wup", [256, 2048]), ("l0ab_wq", [1024, 1024]),
                   ("l0ab_wo", [1024, 1024])]:
        c.dram[n_] = nc.dram_tensor(n_, sh, F32, kind="ExternalInput").ap()


def prep_l0ab(c):
    pass


PART_PREFIX["l0ab"] = "l0ab"
PART_DECL["l0ab"] = declare_l0ab
PART_PREP["l0ab"] = prep_l0ab
PART_EMIT["l0ab"] = sublayer_l0ab


def host_inputs_l0ab(inputs, m):
    m["l0ab_wkv"] = np.ascontiguousarray(
        np.concatenate([inputs["L0_attn_b_W_k"], inputs["L0_attn_b_W_v"]],
                       axis=1))
    m["l0ab_wdown"] = inputs["L0_attn_b_W_down_kv"]
    m["l0ab_wup"] = np.ascontiguousarray(
        np.concatenate([inputs["L0_attn_b_W_up_k"],
                        inputs["L0_attn_b_W_up_v"]], axis=1))
    m["l0ab_wq"] = inputs["L0_attn_b_W_q"]
    m["l0ab_wo"] = inputs["L0_attn_b_W_o"]


def emit_fmT(c, st, name, dram_name, r, cc):
    nc = c.nc
    out = st.enter_context(nc.sbuf_tensor(name, [r, cc], F32))
    c.sb[name] = out
    ld = c.sem("g_ld"); pe = c.sem("g_pe"); ve = c.sem("g_ve")
    ldt = c.base("g_ld") + 16
    pet = c.base("g_pe") + 1
    vet = c.base("g_ve") + 1
    with contextlib.ExitStack() as ph:
        tmp = ph.enter_context(nc.sbuf_tensor("fmt_t", [cc, r], F32))
        ps = ph.enter_context(nc.psum_tensor("fmt_p", [r, cc], F32))

        @c.block.gpsimd
        def _(g):
            g.dma_start(out=tmp[:], in_=c.dram[dram_name][:, :]).then_inc(ld, 16)
            g.wait_ge(ld, ldt)

        @c.block.tensor
        def _(t):
            t.wait_ge(ld, ldt)
            t.transpose(ps[:], tmp[:],
                        c.sb["ident_f"][0:cc, 0:cc]).then_inc(pe, 1)

        @c.block.vector
        def _(v):
            v.wait_ge(pe, pet)
            v.tensor_copy(out[:], ps[:])
            v.drain().then_inc(ve, 1)

        c.commit("g_ld", 16)
        c.commit("g_pe", 1)
        c.commit("g_ve", 1)
    c.barrier()


def emit_cproj(c, arena, w_sb, cq_h, ckvT, krT):
    nc = c.nc
    xv = arena[:].rearrange("p (dc q) -> p dc q", q=4096)
    wv = w_sb[:].rearrange("p (dc f) -> p dc f", f=96)
    pe = c.sem("cp_pe"); se = c.sem("cp_se"); hb = c.sem("cp_hb")
    pe0 = c.base("cp_pe"); se0 = c.base("cp_se"); hb0 = c.base("cp_hb")
    biases = ["l1ab_bdq_fm", "l1ab_bdkv_fm", "l1ab_bkr_fm"]
    with contextlib.ExitStack() as ph:
        ps = [ph.enter_context(nc.psum_tensor(f"cpp{i}", [32, 512], F32))
              for i in range(2)]
        cq_sb = ph.enter_context(nc.sbuf_tensor("cqsb", [32, 4096], BF16))
        dsts = [cq_sb, ckvT, krT]

        @c.block.tensor
        def _(t):
            t.wait_ge(c.sem("w_ld"), c.cnt["w_ld"])
            gi = 0
            for j in range(3):
                for tb in range(8):
                    if gi >= 2:
                        t.wait_ge(se, se0 + gi - 1)
                    for dc in range(8):
                        ins = t.matmul(ps[gi % 2][:],
                                       wv[:, dc, j * 32:(j + 1) * 32],
                                       xv[:, dc, tb * 512:(tb + 1) * 512],
                                       start=(dc == 0), stop=(dc == 7))
                    ins.then_inc(pe, 1)
                    gi += 1

        @c.block.scalar
        def _(s):
            gi = 0
            for j in range(3):
                for tb in range(8):
                    s.wait_ge(pe, pe0 + gi + 1)
                    s.activation(dsts[j][:, tb * 512:(tb + 1) * 512],
                                 ps[gi % 2][:], AF.Identity,
                                 bias=c.sb[biases[j]][:, 0:1])
                    s.drain().then_inc(se, 1)
                    gi += 1

        @c.block.gpsimd
        def _(g):
            g.wait_ge(se, se0 + 8)
            g.dma_start(out=cq_h[:, :], in_=cq_sb[:]).then_inc(hb, 16)
            g.wait_ge(hb, hb0 + 16)

        c.commit("cp_pe", 24)
        c.commit("cp_se", 24)
        c.commit("cp_hb", 16)
    c.barrier()


def emit_kcv(c, ckvT, wuk, wuv, buv_row, karena, varena):
    nc = c.nc
    kv = karena[:].rearrange("p (h q) -> p h q", q=4096)
    vv = varena[:].rearrange("p (t h d) -> p t h d", h=8, d=128)
    pe = c.sem("kc_pe"); se = c.sem("kc_se")
    pe0 = c.base("kc_pe"); se0 = c.base("kc_se")
    with contextlib.ExitStack() as ph:
        ps = [ph.enter_context(nc.psum_tensor(f"kcp{i}", [128, 512], F32))
              for i in range(2)]

        @c.block.tensor
        def _(t):
            t.wait_ge(c.sem("w_ld"), c.cnt["w_ld"])
            gi = 0
            for h in range(8):
                for tb in range(8):
                    if gi >= 2:
                        t.wait_ge(se, se0 + gi - 1)
                    t.matmul(ps[gi % 2][:], wuk[0:32, h * 128:(h + 1) * 128],
                             ckvT[0:32, tb * 512:(tb + 1) * 512],
                             start=True, stop=True).then_inc(pe, 1)
                    gi += 1
            for ti in range(NT):
                for ob in range(2):
                    if gi >= 2:
                        t.wait_ge(se, se0 + gi - 1)
                    t.matmul(ps[gi % 2][:], c.sb["ones1"][0:1, 0:128],
                             buv_row[0:1, ob * 512:(ob + 1) * 512],
                             start=True, stop=False)
                    ins = t.matmul(ps[gi % 2][:],
                                   ckvT[0:32, ti * 128:(ti + 1) * 128],
                                   wuv[0:32, ob * 512:(ob + 1) * 512],
                                   start=False, stop=True)
                    ins.then_inc(pe, 1)
                    gi += 1

        @c.block.scalar
        def _(s):
            gi = 0
            for h in range(8):
                for tb in range(8):
                    s.wait_ge(pe, pe0 + gi + 1)
                    s.activation(kv[:, h, tb * 512:(tb + 1) * 512],
                                 ps[gi % 2][:], AF.Identity,
                                 bias=c.sb["l1ab_buk_fm"][:, h:h + 1])
                    s.drain().then_inc(se, 1)
                    gi += 1
            for ti in range(NT):
                for ob in range(2):
                    s.wait_ge(pe, pe0 + gi + 1)
                    s.activation(vv[:, ti, ob * 4:(ob + 1) * 4, :],
                                 ps[gi % 2][:].rearrange("p (h d) -> p h d",
                                                         d=128), AF.Copy)
                    s.drain().then_inc(se, 1)
                    gi += 1

        c.commit("kc_pe", 64 + 64)
        c.commit("kc_se", 64 + 64)
    c.barrier()


def sublayer_l1ab(c, x_src, x_dst):
    nc = c.nc
    cq_h = c.dram["cq_h"]
    with contextlib.ExitStack() as s1:
        emit_sub_consts(c, s1, [("onescol", "c_onescol", [128, 1], BF16)])
        emit_bias_prep(c, [("l1ab_buq_fm", "l1ab_buq"),
                           ("l1ab_buk_fm", "l1ab_buk")],
                       [("l1ab_buv_row", "l1ab_buv"),
                        ("l1ab_bo_row", "l1ab_bo")], st=s1)
        emit_fmT(c, s1, "l1ab_bdq_fm", "l1ab_bdq", 32, 1)
        emit_fmT(c, s1, "l1ab_bdkv_fm", "l1ab_bdkv", 32, 1)
        emit_fmT(c, s1, "l1ab_bkr_fm", "l1ab_bkr", 32, 1)
        emit_fmT(c, s1, "l1ab_bqr_fm", "l1ab_bqr", 32, 8)
        karena = s1.enter_context(nc.sbuf_tensor("karena", [128, 8 * 4096],
                                                 BF16))
        ckvT = s1.enter_context(nc.sbuf_tensor("ckvT", [32, 4096], BF16))
        krT = s1.enter_context(nc.sbuf_tensor("krT", [32, 4096], BF16))
        with contextlib.ExitStack() as s2:
            arena = s2.enter_context(nc.sbuf_tensor("arena", [128, 8 * 4096],
                                                    BF16))
            av = arena[:].rearrange("p (cc q) -> p cc q", q=4096)
            emit_ln_transpose(c, x_src,
                              lambda i, t: av[:, :, t * 128:(t + 1) * 128])
            with contextlib.ExitStack() as s25:
                w_sb = s25.enter_context(nc.sbuf_tensor("w3d",
                                                        [128, 8 * 96], BF16))
                emit_wload(c, [(w_sb, "l1ab_wdown3")])
                emit_cproj(c, arena, w_sb, cq_h, ckvT, krT)
        varena = s1.enter_context(nc.sbuf_tensor("varena",
                                                 [128, NT * 8 * 128], BF16))
        wuk = s1.enter_context(nc.sbuf_tensor("wuk", [32, 1024], BF16))
        wuv = s1.enter_context(nc.sbuf_tensor("wuv", [32, 1024], BF16))
        wuq = s1.enter_context(nc.sbuf_tensor("wuq", [32, 1024], BF16))
        wqr = s1.enter_context(nc.sbuf_tensor("wqr", [32, 256], BF16))
        ld = c.sem("w_ld")

        @c.block.gpsimd
        def _(g):
            for sb, dn in [(wuk, "l1ab_wuk"), (wuv, "l1ab_wuv"),
                           (wuq, "l1ab_wuq"), (wqr, "l1ab_wqr")]:
                g.dma_start(out=sb[:], in_=c.dram[dn][:, :]).then_inc(ld, 16)
        c.commit("w_ld", 16 * 4)
        emit_kcv(c, ckvT, wuk[:], wuv[:], c.sb["l1ab_buv_row"][:],
                 karena, varena)

        wo_sb = s1.enter_context(nc.sbuf_tensor("wo_sb", [128, 8 * 1024],
                                                BF16))
        emit_wload(c, [(wo_sb, "l1ab_wo")])
        wlt = c.cnt["w_ld"]
        cqb = [s1.enter_context(nc.sbuf_tensor(f"cqb{i}", [32, 512], BF16))
               for i in range(2)]
        qch = [s1.enter_context(nc.sbuf_tensor(f"qch{i}", [128, 512], BF16))
               for i in range(2)]
        qrh = [s1.enter_context(nc.sbuf_tensor(f"qrh{i}", [32, 512], BF16))
               for i in range(2)]

        kvv = karena[:].rearrange("p (h q) -> p h q", q=4096)
        vvv = varena[:].rearrange("p (t h d) -> p t h d", h=8, d=128)
        aq = c.sem("aq_ld")
        aq0 = c.base("aq_ld")
        hql = [(qb, h) for qb in range(NQB) for h in range(8)]

        def qprep(eng, E, qb, meta, hd):
            if eng != "gpsimd":
                return
            if qb >= 2:
                E.wait_ge(c.sem("a_pe"),
                          meta["hq"][(qb - 2, 7)]["prep"]["a_pe"])
            E.dma_start(out=cqb[qb % 2][:],
                        in_=cq_h[:, qb * 512:(qb + 1) * 512]).then_inc(aq, 16)

        def hprep(eng, E, h, qb, hq, meta, hd):
            mk = meta["hq"][(qb, h)]["prep"]
            ps_hp = hd["ps_b"]
            if eng == "tensor":
                if hq == 0:
                    E.wait_ge(c.sem("w_ld"), wlt)
                E.wait_ge(aq, aq0 + 16 * (qb + 1))
                if hq >= 1:
                    E.wait_ge(c.sem("a_se"),
                              meta["hqs"][hq - 1]["norm"]["se_rz"])
                    E.wait_ge(c.sem("a_se"),
                              meta["hq"][hql[hq - 1]]["prep"]["a_se"])
                if hq >= 2:
                    E.wait_ge(c.sem("a_pe"),
                              meta["hq"][hql[hq - 2]]["pe_s_last"])
                E.matmul(ps_hp[:], wuq[0:32, h * 128:(h + 1) * 128],
                         cqb[qb % 2][:], start=True,
                         stop=True).then_inc(c.sem("a_pe"), 1)
                E.wait_ge(c.sem("a_se"), mk["a_se"] - 1)
                E.matmul(ps_hp[0:32, :], wqr[0:32, h * 32:(h + 1) * 32],
                         cqb[qb % 2][:], start=True,
                         stop=True).then_inc(c.sem("a_pe"), 1)
            elif eng == "scalar":
                E.wait_ge(c.sem("a_pe"), mk["a_pe"] - 1)
                E.activation(qch[hq % 2][:], ps_hp[:], AF.Identity,
                             bias=c.sb["l1ab_buq_fm"][:, h:h + 1])
                E.drain().then_inc(c.sem("a_se"), 1)
                E.wait_ge(c.sem("a_pe"), mk["a_pe"])
                E.activation(qrh[hq % 2][:], ps_hp[0:32, :], AF.Identity,
                             bias=c.sb["l1ab_bqr_fm"][:, h:h + 1])
                E.drain().then_inc(c.sem("a_se"), 1)

        def smm(t, h, qb, kt, ps_ap, meta, hd):
            hq = qb * 8 + h
            mk = meta["hq"][(qb, h)]["prep"]
            t.wait_ge(c.sem("a_se"), mk["a_se"])
            t.matmul(ps_ap, kvv[:, h, kt * 128:(kt + 1) * 128],
                     qch[hq % 2][:], start=True, stop=False)
            return t.matmul(ps_ap, krT[0:32, kt * 128:(kt + 1) * 128],
                            qrh[hq % 2][:], start=False, stop=True)

        def pv(t, h, qb, kt, pt_ap, start, stop, hd):
            hq = qb * 8 + h
            t.matmul(hd["ps_oz"][hq % 2][:], vvv[:, kt, h, :], pt_ap,
                     start=start, stop=stop)
            return t.matmul(hd["ps_z"][hq % 2][:], c.sb["onescol"][:, 0:1],
                            pt_ap, start=start, stop=stop)

        p = {
            "H": 8,
            "kts": lambda qb: list(range(32)),
            "diag": lambda qb, kt: None,
            "mask_kind": None,
            "sep_z": True, "oz_rows": 128, "o_rows": 128,
            "qprep": qprep, "qprep_incs": {},
            "hprep": hprep,
            "hprep_incs": {"a_pe": 2, "a_se": 2},
            "smm": smm, "pv": pv,
            "exp_scale": lambda h, qb, kt: 1.0 / math.sqrt(160.0),
            "exp_bias": lambda h, qb, kt: c.sb["kconst"][:, 2:3],
            "out_w": wo_sb[:].rearrange("p (dc f) -> p dc f", f=1024),
            "out_bias_row": c.sb["l1ab_bo_row"][:],
            "oproj_mode": "mm", "ps_y_n": 1,
            "x_src": x_src, "x_dst": x_dst,
        }
        emit_attention(c, p)
        c.commit("aq_ld", 16 * NQB)


def declare_l1ab(c, nc):
    for n_, sh in [("l1ab_wdown3", [1024, 96]), ("l1ab_wuk", [32, 1024]),
                   ("l1ab_wuv", [32, 1024]), ("l1ab_wuq", [32, 1024]),
                   ("l1ab_wqr", [32, 256]), ("l1ab_wo", [1024, 1024]),
                   ("l1ab_bdq", [1, 32]), ("l1ab_bdkv", [1, 32]),
                   ("l1ab_bkr", [1, 32]), ("l1ab_buq", [1, 1024]),
                   ("l1ab_buk", [1, 1024]), ("l1ab_buv", [1, 1024]),
                   ("l1ab_bqr", [8, 32]), ("l1ab_bo", [1, 1024])]:
        c.dram[n_] = nc.dram_tensor(n_, sh, F32, kind="ExternalInput").ap()
    c.dram["cq_h"] = nc.dram_tensor("cq_h", [32, 4096], BF16,
                                    kind="Internal").ap()


def prep_l1ab(c):
    pass


PART_PREFIX["l1ab"] = "l1ab"
PART_DECL["l1ab"] = declare_l1ab
PART_PREP["l1ab"] = prep_l1ab
PART_EMIT["l1ab"] = sublayer_l1ab


def host_inputs_l1ab(inputs, m):
    g = lambda n: inputs["L1_attn_b_" + n]
    m["l1ab_wdown3"] = np.ascontiguousarray(
        np.concatenate([g("W_DQ"), g("W_DKV"), g("W_KR")], axis=1))
    m["l1ab_wuk"] = g("W_UK")
    m["l1ab_wuv"] = g("W_UV")
    m["l1ab_wuq"] = g("W_UQ")
    m["l1ab_wqr"] = g("W_QR")
    m["l1ab_wo"] = g("W_O")
    m["l1ab_bdq"] = g("b_DQ").reshape(1, 32)
    m["l1ab_bdkv"] = g("b_DKV").reshape(1, 32)
    m["l1ab_bkr"] = g("b_KR").reshape(1, 32)
    m["l1ab_buq"] = g("b_UQ").reshape(1, 1024)
    m["l1ab_buk"] = g("b_UK").reshape(1, 1024)
    m["l1ab_buv"] = g("b_UV").reshape(1, 1024)
    m["l1ab_bqr"] = np.ascontiguousarray(g("b_QR").reshape(8, 32))
    m["l1ab_bo"] = g("b_O").reshape(1, 1024)


# version 05
# speedup vs baseline: 1.2614x; 1.2614x over previous
import contextlib
import math
import numpy as np
import ml_dtypes
import concourse.bass as bass
import concourse.mybir as mybir
from concourse.bass_utils import run_bass_kernel_spmd

F32 = mybir.dt.float32
BF16 = mybir.dt.bfloat16
AF = mybir.ActivationFunctionType
ALU = mybir.AluOpType
AX = mybir.AxisListType
BFNP = ml_dtypes.bfloat16

DEBUG = False
DEBUG2 = False
S, D = 4096, 1024
NT = S // 128
NQB = S // 512
NDC = D // 128
LAM = 0.01
N_CORES = 8


def host_consts():
    c = {}
    c["c_ident"] = np.eye(128, dtype=np.float32)
    ql = np.arange(512, dtype=np.float64)[None, :]
    kl = np.arange(128, dtype=np.float64)[:, None]
    c["c_cmask"] = np.concatenate(
        [np.where(kl + 128 * j > ql, np.float64(-1e30), 0.0) for j in range(4)],
        axis=1).astype(np.float32)
    slopes = 2.0 ** (-np.linspace(1.0, 8.0, 16))
    p = np.arange(128, dtype=np.float64)
    bias_k = np.zeros((128, 16 * 32))
    for h in range(16):
        for kt in range(32):
            bias_k[:, h * 32 + kt] = slopes[h] * (kt * 128 + p)
    c["c_alibi_k"] = bias_k.astype(np.float32)
    q = np.arange(4096, dtype=np.float64)
    c["c_slq8"] = np.concatenate(
        [8.0 * slopes[h] * q for h in range(16)])[None, :].astype(np.float32)
    k = np.arange(4096, dtype=np.float64)
    c["c_dcol"] = np.exp(LAM * k).reshape(32, 128).T.astype(np.float32)
    c["c_dcolinv"] = np.exp(-LAM * k).reshape(32, 128).T.astype(np.float32)
    c["c_dscale"] = np.exp(-2 * LAM * k).reshape(32, 128).T.astype(np.float32)
    c["c_dqlo"] = (np.exp(-LAM * k) / 32.0).reshape(32, 128).T.astype(np.float32)
    c["c_dqhi"] = (np.exp(LAM * k) / 32.0).reshape(32, 128).T.astype(np.float32)
    c["c_dpatch"] = np.concatenate(
        [np.exp(-2 * LAM * np.maximum(kl + 128 * j - ql, 0.0)) for j in range(4)],
        axis=1).astype(np.float32)
    c["c_ones512"] = np.ones((1, 512), np.float32)
    c["c_negones"] = np.full((1, 128), -1.0, np.float32)
    kc = np.zeros((128, 8), np.float32)
    kc[:, 0] = 1e-5
    kc[:, 1] = -25.0
    kc[:, 2] = -15.0
    kc[:, 3] = -np.log(32.0)
    kc[:, 4] = 1e-20
    kc[:, 5] = np.log(2.5)
    c["c_kconst"] = kc
    c["c_onesp"] = np.ones((128, 1024), np.float32)
    c["c_onescol"] = np.ones((128, 1), np.float32)
    c["c_zeroblk"] = np.zeros((128, 128), np.float32)
    c["c_zpad"] = np.ones((128, 512), np.float32)
    return c


class NcProxy:
    def __init__(self, nc):
        self._nc = nc
        self._uid = 0

    def __getattr__(self, k):
        return getattr(self._nc, k)

    def sbuf_tensor(self, name, shape, dt):
        self._uid += 1
        return self._nc.sbuf_tensor(f"{name}_u{self._uid}", shape, dt)

    def psum_tensor(self, name, shape, dt):
        self._uid += 1
        return self._nc.psum_tensor(f"{name}_u{self._uid}", shape, dt)


class Ctx:
    def __init__(self, nc, stack):
        self.nc = NcProxy(nc)
        self.stack = stack
        self.block = stack.enter_context(nc.Block())
        self.sems = {}
        self.cnt = {}
        self.sb = {}
        self.dram = {}
        self._bar_n = 0

    def sem(self, name):
        if name not in self.sems:
            self.sems[name] = self.stack.enter_context(
                self.nc._nc.semaphore(name))
            self.cnt[name] = 0
        return self.sems[name]

    def base(self, name):
        self.sem(name)
        return self.cnt[name]

    def commit(self, name, total):
        self.cnt[name] += total

    def barrier(self):
        bar = self.sem("bar")
        self._bar_n += 1
        tgt = 5 * self._bar_n
        self.cnt["bar"] = tgt
        for eng in ("tensor", "scalar", "vector", "sync", "gpsimd"):
            @getattr(self.block, eng)
            def _(e, tgt=tgt, bar=bar):
                e.drain().then_inc(bar, 1)
                e.wait_ge(bar, tgt)


def emit_consts(c):
    nc, st = c.nc, c.stack
    def sbt(name, shape, dt):
        h = st.enter_context(nc.sbuf_tensor(name, shape, dt))
        c.sb[name] = h
        return h

    names = [("ident_bf", "c_ident", [128, 128], BF16),
             ("ident_f", "c_ident", [128, 128], F32),
             ("ones1", "c_ones512", [1, 512], BF16),
             ("negones_f", "c_negones", [1, 128], F32),
             ("zeroblk", "c_zeroblk", [128, 128], BF16),
             ("zpad", "c_zpad", [128, 512], BF16),
             ("kconst", "c_kconst", [128, 8], F32)]
    for n_, _, sh, dt in names:
        sbt(n_, sh, dt)

    ld = c.sem("g_ld")
    ld0 = c.base("g_ld")

    @c.block.gpsimd
    def _(g):
        for n_, dn, sh, dt in names:
            g.dma_start(out=c.sb[n_][:], in_=c.dram[dn][:, :]).then_inc(ld, 16)
        g.wait_ge(ld, ld0 + 16 * len(names))

    c.commit("g_ld", 16 * len(names))
    c.barrier()


def emit_sub_consts(c, st, names):
    nc = c.nc
    ld = c.sem("g_ld")
    ld0 = c.base("g_ld")
    for n_, dn, sh, dt in names:
        c.sb[n_] = st.enter_context(nc.sbuf_tensor(n_, sh, dt))

    @c.block.gpsimd
    def _(g):
        for n_, dn, sh, dt in names:
            g.dma_start(out=c.sb[n_][:], in_=c.dram[dn][:, :]).then_inc(ld, 16)
        g.wait_ge(ld, ld0 + 16 * len(names))
    c.commit("g_ld", 16 * len(names))
    c.barrier()


CAUSAL_CONSTS = [("cmask", "c_cmask", [128, 4 * 512], BF16),
                 ("alibi_k", "c_alibi_k", [128, 16 * 32], F32)]
DECAY_CONSTS = [("dcol", "c_dcol", [128, 32], F32),
                ("dcolinv", "c_dcolinv", [128, 32], F32),
                ("dqlo", "c_dqlo", [128, 32], F32),
                ("dqhi", "c_dqhi", [128, 32], F32),
                ("dpatch", "c_dpatch", [128, 4 * 512], F32),
                ("onescol", "c_onescol", [128, 1], BF16)]


def emit_bias_prep(c, fm_specs, row_specs, st=None):
    nc = c.nc
    st = st if st is not None else c.stack
    for name, dn in fm_specs:
        F = c.dram[dn].shape[-1]
        c.sb[name] = st.enter_context(nc.sbuf_tensor(name, [128, F // 128], F32))
    for name, dn in row_specs:
        F = c.dram[dn].shape[-1]
        c.sb[name] = st.enter_context(nc.sbuf_tensor(name, [1, F], BF16))
    ld = c.sem("g_ld"); pe = c.sem("g_pe"); ve = c.sem("g_ve")
    ld0 = c.base("g_ld"); pe0 = c.base("g_pe"); ve0 = c.base("g_ve")
    n = len(fm_specs)
    with contextlib.ExitStack() as loc:
        tmp = loc.enter_context(nc.sbuf_tensor("bp_t", [16, 128], F32))
        ps = loc.enter_context(nc.psum_tensor("bp_p", [128, 16], F32))

        @c.block.gpsimd
        def _(g):
            for i, (name, dn) in enumerate(fm_specs):
                if i >= 1:
                    g.wait_ge(pe, pe0 + i)
                g.dma_start(out=tmp[0:c.dram[dn].shape[-1] // 128, :],
                            in_=c.dram[dn].rearrange("o (cc p) -> (o cc) p",
                                                     p=128)).then_inc(ld, 16)
            for name, dn in row_specs:
                g.dma_start(out=c.sb[name][:],
                            in_=c.dram[dn][:, :]).then_inc(ld, 16)
            g.wait_ge(ld, ld0 + 16 * (len(fm_specs) + len(row_specs)))

        @c.block.tensor
        def _(t):
            for i, (name, dn) in enumerate(fm_specs):
                nc_ = c.dram[dn].shape[-1] // 128
                t.wait_ge(ld, ld0 + 16 * (i + 1))
                if i >= 1:
                    t.wait_ge(ve, ve0 + i)
                t.transpose(ps[:, 0:nc_], tmp[0:nc_, :],
                            c.sb["ident_f"][0:nc_, 0:nc_]).then_inc(pe, 1)

        @c.block.vector
        def _(v):
            for i, (name, dn) in enumerate(fm_specs):
                nc_ = c.dram[dn].shape[-1] // 128
                v.wait_ge(pe, pe0 + i + 1)
                v.tensor_copy(c.sb[name][:, 0:nc_], ps[:, 0:nc_])
                v.drain().then_inc(ve, 1)

        c.commit("g_ld", 16 * (n + len(row_specs)))
        c.commit("g_pe", n)
        c.commit("g_ve", n)
    c.barrier()


def emit_ln_transpose(c, x_src, xnt_dst, tiles=None, xn_cb=None, n_cb_inc=0,
                      act_cb=None, n_acb_inc=0, pre_xn=None,
                      arena_to_hbm=None):
    nc = c.nc
    tiles = list(range(NT)) if tiles is None else list(tiles)
    n = len(tiles)
    ld = c.sem("ln_ld"); va = c.sem("ln_va"); sa = c.sem("ln_sa")
    vb = c.sem("ln_vb"); sb = c.sem("ln_sb"); ve = c.sem("ln_ve")
    pe = c.sem("ln_pe"); ev = c.sem("ln_ev"); hb = c.sem("ln_hb")
    ld0 = c.base("ln_ld"); va0 = c.base("ln_va"); sa0 = c.base("ln_sa")
    vb0 = c.base("ln_vb"); sb0 = c.base("ln_sb"); ve0 = c.base("ln_ve")
    pe0 = c.base("ln_pe"); ev0 = c.base("ln_ev"); hb0 = c.base("ln_hb")
    vpt = 1 + n_cb_inc

    with contextlib.ExitStack() as ph:
        xtile = [ph.enter_context(nc.sbuf_tensor(f"lnx{i}", [128, 1024], F32))
                 for i in range(2)]
        xn_bf = [ph.enter_context(nc.sbuf_tensor(f"lnn{i}", [128, 1024], BF16))
                 for i in range(2)]
        stat = ph.enter_context(nc.sbuf_tensor("lnst", [128, 12], F32))
        xsq = ph.enter_context(nc.sbuf_tensor("lnsq", [128, 1024], F32))
        ps_tr = [ph.enter_context(nc.psum_tensor(f"lnp{i}", [128, 1024], BF16))
                 for i in range(2)]

        @c.block.sync
        def _(sy):
            for i, t in enumerate(tiles):
                b = i % 2
                if i >= 2:
                    sy.wait_ge(ve, ve0 + vpt * (i - 1))
                    sy.wait_ge(sa, sa0 + (i - 2) + 1)
                sy.dma_start(out=xtile[b][:],
                             in_=x_src[t * 128:(t + 1) * 128, :]).then_inc(ld, 16)

        def evac(v, i):
            b = i % 2
            v.wait_ge(pe, pe0 + i + 1)
            src = ps_tr[b][:].rearrange("p (cc q) -> p cc q", q=128)
            v.tensor_copy(xnt_dst(i, tiles[i]), src)
            v.drain().then_inc(ev, 1)

        @c.block.vector
        def _(v):
            for i, t in enumerate(tiles):
                b = i % 2
                v.wait_ge(ld, ld0 + 16 * (i + 1))
                if i >= 1:
                    v.wait_ge(sa, sa0 + i)
                v.reduce_sum(stat[:, 0:1], xtile[b][:], axis=AX.X)
                v.drain().then_inc(va, 1)
                v.wait_ge(sa, sa0 + i + 1)
                v.tensor_sub(stat[:, 5:6], stat[:, 3:4], stat[:, 4:5])
                v.drain().then_inc(vb, 1)
                v.wait_ge(sb, sb0 + i + 1)
                if i >= 2:
                    v.wait_ge(pe, pe0 + i - 1)
                if pre_xn is not None:
                    pre_xn(v, i)
                v.tensor_scalar(xn_bf[b][:], xtile[b][:], stat[:, 2:3],
                                stat[:, 7:8], ALU.subtract,
                                ALU.mult).then_inc(ve, 1)
                if xn_cb is not None:
                    xn_cb(v, i, t, xn_bf[b], stat)
                if i >= 1:
                    evac(v, i - 1)
            evac(v, n - 1)

        @c.block.scalar
        def _(s):
            for i, t in enumerate(tiles):
                b = i % 2
                s.wait_ge(ld, ld0 + 16 * (i + 1))
                s.activation(xsq[:], xtile[b][:], AF.Square,
                             accum_out=stat[:, 1:2])
                s.wait_ge(va, va0 + i + 1)
                s.activation(stat[:, 2:3], stat[:, 0:1], AF.Copy,
                             scale=1.0 / 1024)
                s.activation(stat[:, 4:5], stat[:, 2:3], AF.Square)
                s.activation(stat[:, 3:4], stat[:, 1:2], AF.Copy,
                             scale=1.0 / 1024)
                s.drain().then_inc(sa, 1)
                s.wait_ge(vb, vb0 + i + 1)
                s.activation(stat[:, 6:7], stat[:, 5:6], AF.Ln,
                             bias=c.sb["kconst"][:, 0:1])
                s.activation(stat[:, 7:8], stat[:, 6:7], AF.Exp, scale=-0.5)
                s.drain().then_inc(sb, 1)
                if act_cb is not None:
                    s.wait_ge(ve, ve0 + vpt * i + vpt)
                    act_cb(s, i, t, stat, xn_bf[b], xsq)

        @c.block.tensor
        def _(t_):
            for i, t in enumerate(tiles):
                b = i % 2
                t_.wait_ge(ve, ve0 + vpt * i + 1)
                if i >= 2:
                    t_.wait_ge(ev, ev0 + (i - 1))
                for cc in range(8):
                    ins = t_.transpose(ps_tr[b][:, cc * 128:(cc + 1) * 128],
                                       xn_bf[b][:, cc * 128:(cc + 1) * 128],
                                       c.sb["ident_bf"][:])
                ins.then_inc(pe, 1)

        if arena_to_hbm is not None:
            arena, hbm_ap = arena_to_hbm
            @c.block.gpsimd
            def _(g):
                g.wait_ge(ev, ev0 + n)
                for cc in range(8):
                    g.dma_start(out=hbm_ap[cc, :, :],
                                in_=arena[:].rearrange(
                                    "p (cc q) -> p cc q", q=4096)[:, cc, :]
                                ).then_inc(hb, 16)
                g.wait_ge(hb, hb0 + 16 * 8)

        c.commit("ln_ld", 16 * n)
        c.commit("ln_va", n)
        c.commit("ln_sa", n)
        c.commit("ln_vb", n)
        c.commit("ln_sb", n)
        c.commit("ln_ve", vpt * n)
        c.commit("ln_pe", n)
        c.commit("ln_ev", n)
        if arena_to_hbm is not None:
            c.commit("ln_hb", 16 * 8)
    c.barrier()


def emit_proj_fm(c, w_sb, xnt_arena, out_cb, oc_list, bias_fm=None,
                 scale=None, n_dc=8, tok=4096):
    nc = c.nc
    pe = c.sem("pj_pe"); ev = c.sem("pj_ev")
    pe0 = c.base("pj_pe"); ev0 = c.base("pj_ev")
    ntb = tok // 512
    xv = xnt_arena[:].rearrange("p (dc q) -> p dc q", q=tok)

    with contextlib.ExitStack() as ph:
        ps = [ph.enter_context(nc.psum_tensor(f"pjp{i}", [128, 512], F32))
              for i in range(8)]

        @c.block.tensor
        def _(t):
            for j, oc in enumerate(oc_list):
                for dc in range(n_dc):
                    for tb in range(ntb):
                        if j >= 1 and dc == 0:
                            t.wait_ge(ev, ev0 + (j - 1) * ntb + tb + 1)
                        ins = t.matmul(ps[tb][:],
                                       w_sb[:, dc, oc * 128:(oc + 1) * 128],
                                       xv[:, dc, tb * 512:(tb + 1) * 512],
                                       start=(dc == 0), stop=(dc == n_dc - 1))
                        if dc == n_dc - 1:
                            ins.then_inc(pe, 1)

        @c.block.scalar
        def _(s):
            for j, oc in enumerate(oc_list):
                for tb in range(ntb):
                    s.wait_ge(pe, pe0 + j * ntb + tb + 1)
                    out_cb(s, oc, tb, ps[tb][:])

        c.commit("pj_pe", len(oc_list) * ntb)
        c.commit("pj_ev", len(oc_list) * ntb)


def attn_core(c, cfg):
    raise NotImplementedError


def emit_attention(c, p):
    nc, st = c.nc, c.stack
    H = p["H"]
    sep_z = p.get("sep_z", False)
    oz_rows = p["oz_rows"]
    o_rows = p["o_rows"]
    hinc = p.get("hprep_incs", {})
    qinc = p.get("qprep_incs", {})

    semnames = ["a_pe", "a_se", "a_ve", "a_gld", "a_xld", "a_st"]
    CNT = {sn: [c.base(sn)] for sn in semnames}

    def bump(sn, n=1):
        CNT[sn][0] += n
        return CNT[sn][0]

    sched = []
    items = []
    hqs = []
    ygroups = []
    meta = {"hq": {}, "qb": {}}
    for qb in range(NQB):
        kts = p["kts"](qb)
        e = {"k": "qprep", "qb": qb,
             "marks": {sn: bump(sn, qinc.get(sn, 0)) for sn in semnames}}
        sched.append(e)
        meta["qb"][qb] = e["marks"]
        for h in range(H):
            hq = len(hqs)
            e = {"k": "hprep", "qb": qb, "h": h, "hq": hq,
                 "marks": {sn: bump(sn, hinc.get(sn, 0)) for sn in semnames}}
            sched.append(e)
            mk = {"prep": e["marks"]}
            for i, kt in enumerate(kts):
                j = len(items)
                dj = p["diag"](qb, kt)
                e = {"k": "item", "qb": qb, "h": h, "kt": kt, "j": j,
                     "hq": hq, "dj": dj,
                     "pe_s": bump("a_pe"),
                     "ve_d": bump("a_ve") if dj is not None else None,
                     "se_e": bump("a_se"), "pe_pv": bump("a_pe"),
                     "first": i == 0, "last": i == len(kts) - 1}
                sched.append(e)
                items.append(e)
            mk["pe_s_last"] = items[-1]["pe_s"]
            mk["pe_pv_last"] = items[-1]["pe_pv"]
            e = {"k": "norm", "qb": qb, "h": h, "hq": hq,
                 "se_ln": bump("a_se"), "pe_b": bump("a_pe"),
                 "se_rz": bump("a_se"),
                 "ve_o": bump("a_ve"), "last_j": len(items) - 1}
            sched.append(e)
            mk["norm"] = e
            hqs.append(mk)
            meta["hq"][(qb, h)] = mk
        for t4 in range(4):
            e = {"k": "oproj", "qb": qb, "t4": t4, "hq_end": len(hqs) - 1,
                 "xld": bump("a_xld", 16), "g": [None, None],
                 "st": bump("a_st", 16)}
            if p["oproj_mode"] == "mm":
                for ob in range(2):
                    g = {"pe": bump("a_pe"), "ve": bump("a_ve"),
                         "idx": len(ygroups)}
                    e["g"][ob] = g
                    ygroups.append(g)
            else:
                g = {"pe": bump("a_pe"), "ve": bump("a_ve"),
                     "idx": len(ygroups)}
                e["g"][0] = e["g"][1] = g
                ygroups.append(g)
            sched.append(e)
    oprojs = [e for e in sched if e["k"] == "oproj"]
    meta["items"] = items
    meta["hqs"] = hqs

    with contextlib.ExitStack() as ph:
        ps_s = [ph.enter_context(nc.psum_tensor(f"aps{i}", [128, 512], F32))
                for i in range(2)]
        ps_oz = [ph.enter_context(nc.psum_tensor(f"aoz{i}", [oz_rows, 512], F32))
                 for i in range(2)]
        ps_b = ph.enter_context(nc.psum_tensor("abc", [o_rows, 512], F32))
        hd = {}
        hd["ps_b"] = ps_b
        n_py = p.get("ps_y_n", 2)
        if p["oproj_mode"] == "mm":
            ps_y = [ph.enter_context(nc.psum_tensor(f"apy{i}", [128, 512], F32))
                    for i in range(n_py)]
        else:
            ps_yt = ph.enter_context(nc.psum_tensor("apyt", [128, 1024], BF16))
        if sep_z:
            n_pz = p.get("ps_z_n", 2)
            ps_z = [ph.enter_context(nc.psum_tensor(f"apz{i}", [1, 512], F32))
                    for i in range(n_pz)]
            hd["ps_z"] = ps_z
            hd["n_pz"] = n_pz
        pt = [ph.enter_context(nc.sbuf_tensor(f"apt{i}", [128, 512], BF16))
              for i in range(2)]
        lnz = [ph.enter_context(nc.sbuf_tensor("alnz", [1, 512], F32))]
        rz = [ph.enter_context(nc.sbuf_tensor(f"arz{i}", [o_rows, 512], BF16))
              for i in range(2)]
        obuf = ph.enter_context(nc.sbuf_tensor("aobuf", [128, 8 * 512], BF16))
        xres = [ph.enter_context(nc.sbuf_tensor(f"axr{i}", [128, 1024], F32))
                for i in range(2)]
        hd["ps_oz"] = ps_oz
        hd["ps_s"] = ps_s
        obv = obuf[:].rearrange("p (cc q) -> p cc q", q=512)
        hd["obv"] = obv

        sem = {sn: c.sem(sn) for sn in semnames}
        sem_pe = sem["a_pe"]; sem_se = sem["a_se"]; sem_ve = sem["a_ve"]
        sem_xl = sem["a_xld"]; sem_st = sem["a_st"]

        def run(engname, eng):
            for e in sched:
                k = e["k"]
                if k == "qprep":
                    if p["qprep"] is not None:
                        p["qprep"](engname, eng, e["qb"], meta, hd)
                elif k == "hprep":
                    if p["hprep"] is not None:
                        p["hprep"](engname, eng, e["h"], e["qb"], e["hq"],
                                   meta, hd)
                elif k == "item":
                    j, h, qb, kt = e["j"], e["h"], e["qb"], e["kt"]
                    par = j % 2
                    if engname == "tensor":
                        t = eng
                        if j >= 2:
                            t.wait_ge(sem_se, items[j - 2]["se_e"])
                        ins = p["smm"](t, h, qb, kt, ps_s[par][:], meta, hd)
                        ins.then_inc(sem_pe, 1)
                        t.wait_ge(sem_se, e["se_e"])
                        if e["first"] and e["hq"] >= 2:
                            t.wait_ge(sem_ve, hqs[e["hq"] - 2]["norm"]["ve_o"])
                        if (e["first"] and sep_z and p.get("ps_z_n", 2) == 1
                                and e["hq"] >= 1):
                            t.wait_ge(sem_se, hqs[e["hq"] - 1]["norm"]["se_ln"])
                        ins = p["pv"](t, h, qb, kt, pt[j % 2][:],
                                      e["first"], e["last"], hd)
                        ins.then_inc(sem_pe, 1)
                    elif engname == "scalar":
                        s = eng
                        if e["dj"] is not None:
                            s.wait_ge(sem_ve, e["ve_d"])
                        else:
                            s.wait_ge(sem_pe, e["pe_s"])
                        if j >= 2:
                            s.wait_ge(sem_pe, items[j - 2]["pe_pv"])
                        kw = {}
                        sc = p["exp_scale"](h, qb, kt)
                        bi = p["exp_bias"](h, qb, kt)
                        if sc is not None:
                            kw["scale"] = sc
                        if bi is not None:
                            kw["bias"] = bi
                        s.activation(pt[j % 2][:], ps_s[par][:], AF.Exp, **kw)
                        s.drain().then_inc(sem_se, 1)
                    elif engname == "vector":
                        if e["dj"] is None:
                            continue
                        v = eng
                        v.wait_ge(sem_pe, e["pe_s"])
                        msk = (c.sb["cmask"] if p["mask_kind"] == "cmask"
                               else c.sb["dpatch"])
                        op = (ALU.add if p["mask_kind"] == "cmask"
                              else ALU.mult)
                        v.tensor_tensor(ps_s[par][:], ps_s[par][:],
                                        msk[:, e["dj"] * 512:(e["dj"] + 1) * 512],
                                        op)
                        v.drain().then_inc(sem_ve, 1)
                elif k == "norm":
                    hq, h = e["hq"], e["h"]
                    if engname == "tensor":
                        t = eng
                        if hq >= 1:
                            t.wait_ge(sem_se, hqs[hq - 1]["norm"]["se_rz"])
                        t.wait_ge(sem_se, e["se_ln"])
                        t.matmul(ps_b[:], c.sb["negones_f"][0:1, 0:o_rows],
                                 lnz[0][:], start=True, stop=False)
                        t.matmul(ps_b[:], c.sb["zeroblk"][:, 0:o_rows],
                                 c.sb["zpad"][:, 0:512],
                                 start=False, stop=True).then_inc(sem_pe, 1)
                    elif engname == "scalar":
                        s = eng
                        s.wait_ge(sem_pe, items[e["last_j"]]["pe_pv"])
                        zsrc = (ps_z[hq % n_pz][0:1, :] if sep_z
                                else ps_oz[hq % 2][64:65, :])
                        if hq >= 1:
                            s.wait_ge(sem_pe, hqs[hq - 1]["norm"]["pe_b"])
                        s.activation(lnz[0][:], zsrc, AF.Ln)
                        s.drain().then_inc(sem_se, 1)
                        s.wait_ge(sem_pe, e["pe_b"])
                        s.activation(rz[hq % 2][:], ps_b[:], AF.Exp)
                        s.drain().then_inc(sem_se, 1)
                    elif engname == "vector":
                        v = eng
                        v.wait_ge(sem_se, e["se_rz"])
                        if o_rows == 64:
                            dsl = obv[(h % 2) * 64:(h % 2) * 64 + 64, h // 2, :]
                        else:
                            dsl = obv[:, h, :]
                        v.tensor_tensor(dsl, ps_oz[hq % 2][0:o_rows, :],
                                        rz[hq % 2][:], ALU.mult)
                        v.drain().then_inc(sem_ve, 1)
                elif k == "oproj":
                    qb, t4 = e["qb"], e["t4"]
                    b = t4 % 2
                    tglob = qb * 4 + t4
                    if engname == "tensor":
                        t = eng
                        t.wait_ge(sem_ve, hqs[e["hq_end"]]["norm"]["ve_o"])
                        if p["oproj_mode"] == "mm":
                            for ob in range(2):
                                g = e["g"][ob]
                                if g["idx"] >= n_py:
                                    t.wait_ge(sem_ve,
                                              ygroups[g["idx"] - n_py]["ve"])
                                hasb = p["out_bias_row"] is not None
                                if hasb:
                                    t.matmul(
                                        ps_y[g["idx"] % n_py][:],
                                        c.sb["ones1"][0:1, 0:128],
                                        p["out_bias_row"][0:1,
                                                          ob * 512:(ob + 1) * 512],
                                        start=True, stop=False)
                                for oc in range(8):
                                    ins = t.matmul(
                                        ps_y[g["idx"] % n_py][:],
                                        obv[:, oc, t4 * 128:(t4 + 1) * 128],
                                        p["out_w"][:, oc,
                                                   ob * 512:(ob + 1) * 512],
                                        start=(oc == 0 and not hasb),
                                        stop=(oc == 7))
                                ins.then_inc(sem_pe, 1)
                        else:
                            g = e["g"][0]
                            if g["idx"] >= 1:
                                t.wait_ge(sem_ve, ygroups[g["idx"] - 1]["ve"])
                            for oc in range(8):
                                ins = t.transpose(
                                    ps_yt[:, oc * 128:(oc + 1) * 128],
                                    obv[:, oc, t4 * 128:(t4 + 1) * 128],
                                    c.sb["ident_bf"][:])
                            ins.then_inc(sem_pe, 1)
                    elif engname == "vector":
                        v = eng
                        if p["oproj_mode"] == "mm":
                            for ob in range(2):
                                g = e["g"][ob]
                                v.wait_ge(sem_pe, g["pe"])
                                v.wait_ge(sem_xl, e["xld"])
                                v.tensor_add(
                                    xres[b][:, ob * 512:(ob + 1) * 512],
                                    ps_y[g["idx"] % n_py][:],
                                    xres[b][:, ob * 512:(ob + 1) * 512])
                                v.drain().then_inc(sem_ve, 1)
                        else:
                            g = e["g"][0]
                            v.wait_ge(sem_pe, g["pe"])
                            v.wait_ge(sem_xl, e["xld"])
                            v.tensor_add(xres[b][:], ps_yt[:], xres[b][:])
                            v.drain().then_inc(sem_ve, 1)
                    elif engname == "sync":
                        sy = eng
                        if tglob >= 2:
                            sy.wait_ge(sem_st, oprojs[tglob - 2]["st"])
                        sy.dma_start(
                            out=xres[b][:],
                            in_=p["x_src"][tglob * 128:(tglob + 1) * 128, :]
                            ).then_inc(sem_xl, 16)
                        sy.wait_ge(sem_ve, e["g"][1]["ve"])
                        sy.dma_start(
                            out=p["x_dst"][tglob * 128:(tglob + 1) * 128, :],
                            in_=xres[b][:]).then_inc(sem_st, 16)

        for engname in ("tensor", "scalar", "vector", "sync", "gpsimd"):
            @getattr(c.block, engname)
            def _(eng, engname=engname):
                run(engname, eng)
                if engname == "sync":
                    eng.wait_ge(sem_st, CNT["a_st"][0])

        for sn in semnames:
            c.cnt[sn] = CNT[sn][0]
    c.barrier()
def emit_wload(c, dsts):
    ld = c.sem("w_ld")
    @c.block.gpsimd
    def _(g):
        for sb, dn in dsts:
            src = c.dram[dn]
            g.dma_start(out=sb[:].rearrange("p (dc f) -> p dc f",
                                            dc=src.shape[0] // 128),
                        in_=src.rearrange("(dc p) f -> p dc f", p=128)
                        ).then_inc(ld, 16)
    c.commit("w_ld", 16 * len(dsts))


def emit_kq(c, arena, karena, w_sb, bk_fm, bq_fm, qt_h, wq_dram):
    nc, st = c.nc, c.stack
    xv = arena[:].rearrange("p (dc q) -> p dc q", q=4096)
    kv = karena[:].rearrange("p (cc q) -> p cc q", q=4096)
    wv = w_sb[:].rearrange("p (dc f) -> p dc f", f=1024)
    pe = c.sem("kq_pe"); se = c.sem("kq_se"); hb = c.sem("kq_hb")
    wl = c.sem("w_ld")
    pe0 = c.base("kq_pe"); se0 = c.base("kq_se"); hb0 = c.base("kq_hb")
    wl_k = c.cnt["w_ld"]
    wl_q = wl_k + 16

    with contextlib.ExitStack() as ph:
        ps = [ph.enter_context(nc.psum_tensor(f"kqp{i}", [128, 512], F32))
              for i in range(8)]
        qstage = ph.enter_context(nc.sbuf_tensor("kqs", [128, 4096], BF16))

        def pe_at(isq, oc, tb):
            return pe0 + isq * 64 + oc * 8 + tb + 1

        def se_at(isq, oc, tb):
            return se0 + isq * 64 + oc * 8 + tb + 1

        @c.block.tensor
        def _(t):
            for isq in range(2):
                t.wait_ge(wl, wl_k if isq == 0 else wl_q)
                for oc in range(8):
                    for tb in range(8):
                        gi = isq * 8 + oc
                        if gi >= 1:
                            poc = (isq, oc - 1) if oc >= 1 else (isq - 1, 7)
                            t.wait_ge(se, se_at(poc[0], poc[1], tb))
                        for dc in range(8):
                            ins = t.matmul(ps[tb][:],
                                           wv[:, dc, oc * 128:(oc + 1) * 128],
                                           xv[:, dc, tb * 512:(tb + 1) * 512],
                                           start=(dc == 0), stop=(dc == 7))
                        ins.then_inc(pe, 1)

        @c.block.scalar
        def _(s):
            for isq in range(2):
                for oc in range(8):
                    for tb in range(8):
                        s.wait_ge(pe, pe_at(isq, oc, tb))
                        if isq == 0:
                            dst = kv[:, oc, tb * 512:(tb + 1) * 512]
                            bias = bk_fm[:, oc:oc + 1]
                        else:
                            if tb == 0 and oc >= 1:
                                s.wait_ge(hb, hb0 + 16 * oc)
                            dst = qstage[:, tb * 512:(tb + 1) * 512]
                            bias = bq_fm[:, oc:oc + 1]
                        s.activation(dst, ps[tb][:], AF.Identity, bias=bias)
                        s.drain().then_inc(se, 1)

        @c.block.gpsimd
        def _(g):
            g.wait_ge(pe, pe0 + 64)
            g.dma_start(out=w_sb[:].rearrange("p (dc f) -> p dc f", dc=8),
                        in_=wq_dram.rearrange("(dc p) f -> p dc f", p=128)
                        ).then_inc(wl, 16)
            for oc in range(8):
                g.wait_ge(se, se0 + 64 + oc * 8 + 8)
                g.dma_start(out=qt_h[oc, :, :],
                            in_=qstage[:]).then_inc(hb, 16)
            g.wait_ge(hb, hb0 + 16 * 8)

        c.commit("kq_pe", 128)
        c.commit("kq_se", 128)
        c.commit("kq_hb", 8 * 16)
        c.commit("w_ld", 16)
    c.barrier()


def emit_ones_prefill(c, varena):
    vo = varena[:].rearrange("p (x e) -> p x e", e=65)
    sm = c.sem("vones")

    @c.block.sync
    def _(sy):
        sy.dma_start(out=vo[:, :, 64:65],
                     in_=c.sb["zpad"][:, 0:512]
                     .rearrange("p (x o) -> p x o", o=1),
                     allow_slow_non_contiguous=True).then_inc(sm, 16)
        sy.wait_ge(sm, c.cnt["vones"] + 16)
    c.commit("vones", 16)


def emit_v65(c, varena, wv_sb, bv_row, xnt_h, w_ld_target):
    nc, st = c.nc, c.stack
    vv = varena[:].rearrange("p (t h e) -> p t h e", h=16, e=65)
    wvv = wv_sb[:].rearrange("p (dc f) -> p dc f", f=1024)
    ld = c.sem("v_ld"); pe = c.sem("v_pe"); se = c.sem("v_se")
    ld0 = c.base("v_ld"); pe0 = c.base("v_pe"); se0 = c.base("v_se")

    with contextlib.ExitStack() as ph:
        xts = [ph.enter_context(nc.sbuf_tensor(f"vx{i}", [128, 8 * 128], BF16))
               for i in range(2)]
        ps = [ph.enter_context(nc.psum_tensor(f"vp{i}", [128, 512], F32))
              for i in range(4)]

        @c.block.gpsimd
        def _(g):
            for t in range(NT):
                if t >= 2:
                    g.wait_ge(pe, pe0 + 2 * (t - 1))
                g.dma_start(out=xts[t % 2][:].rearrange("p (dc q) -> p dc q",
                                                        q=128),
                            in_=xnt_h[:, :, t * 128:(t + 1) * 128]
                            .rearrange("dc p q -> p dc q")).then_inc(ld, 16)

        @c.block.tensor
        def _(t_):
            t_.wait_ge(c.sem("w_ld"), w_ld_target)
            xw = [xts[i][:].rearrange("p (dc q) -> p dc q", q=128)
                  for i in range(2)]
            for t in range(NT):
                t_.wait_ge(ld, ld0 + 16 * (t + 1))
                for ob in range(2):
                    bank = (t % 2) * 2 + ob
                    if t >= 2:
                        t_.wait_ge(se, se0 + 2 * (t - 2) + ob + 1)
                    t_.matmul(ps[bank][:], c.sb["ones1"][0:1, 0:128],
                              bv_row[0:1, ob * 512:(ob + 1) * 512],
                              start=True, stop=False)
                    for dc in range(8):
                        ins = t_.matmul(ps[bank][:], xw[t % 2][:, dc, :],
                                        wvv[:, dc, ob * 512:(ob + 1) * 512],
                                        start=False, stop=(dc == 7))
                    ins.then_inc(pe, 1)

        @c.block.scalar
        def _(s):
            for t in range(NT):
                for ob in range(2):
                    bank = (t % 2) * 2 + ob
                    s.wait_ge(pe, pe0 + 2 * t + ob + 1)
                    for hh in range(8):
                        s.activation(vv[:, t, ob * 8 + hh, 0:64],
                                     ps[bank][:, hh * 64:(hh + 1) * 64],
                                     AF.Copy)
                    s.drain().then_inc(se, 1)

        c.commit("v_ld", 16 * NT)
        c.commit("v_pe", 2 * NT)
        c.commit("v_se", 2 * NT)
    c.barrier()


def emit_absmax(c, arena, cbuf, n_cc=8):
    nc = c.nc
    se = c.sem("am_se"); ve = c.sem("am_ve")
    se0 = c.base("am_se"); ve0 = c.base("am_ve")
    av = arena[:].rearrange("p (cc q) -> p cc q", q=4096)
    with contextlib.ExitStack() as ph:
        scr = [ph.enter_context(nc.sbuf_tensor(f"am{i}", [128, 4096], BF16))
               for i in range(2)]

        @c.block.scalar
        def _(s):
            for cc in range(n_cc):
                if cc >= 2:
                    s.wait_ge(ve, ve0 + cc - 1)
                s.activation(scr[cc % 2][:], av[:, cc, :], AF.Abs)
                s.drain().then_inc(se, 1)

        @c.block.vector
        def _(v):
            for cc in range(n_cc):
                v.wait_ge(se, se0 + cc + 1)
                v.reduce_max(cbuf[:, cc:cc + 1], scr[cc % 2][:], axis=AX.X)
                v.drain().then_inc(ve, 1)

        c.commit("am_se", n_cc)
        c.commit("am_ve", n_cc)
    c.barrier()


def sublayer_l0aa(c, x_src, x_dst):
    nc = c.nc
    xnt_h = c.dram["xnt_h"]
    qt_h = c.dram["qt_h"]
    hql = [(qb, h) for qb in range(NQB) for h in range(16)]
    with contextlib.ExitStack() as s1:
        emit_sub_consts(c, s1, CAUSAL_CONSTS)
        c.sb["onescol"] = s1.enter_context(
            nc.sbuf_tensor("onescol", [128, 1], BF16))
        emit_bias_prep(c, [("l0aa_bk_fm", "l0aa_bk"),
                           ("l0aa_bq_fm", "l0aa_bq")],
                       [("l0aa_bv_row", "l0aa_bv"), ("l0aa_bo_row", "l0aa_bo")],
                       st=s1)
        karena = s1.enter_context(nc.sbuf_tensor("karena", [128, 8 * 4096], BF16))
        cbuf = s1.enter_context(nc.sbuf_tensor("cbuf", [128, 8], BF16))
        qhead = [s1.enter_context(nc.sbuf_tensor(f"qh{i}", [128, 512], BF16))
                 for i in range(2)]
        absb = [s1.enter_context(nc.sbuf_tensor(f"ab{i}", [128, 512], BF16))
                for i in range(2)]
        shrow = [s1.enter_context(nc.sbuf_tensor(f"sh{i}", [1, 512], BF16))
                 for i in range(2)]
        slqb = [s1.enter_context(nc.sbuf_tensor(f"sq{i}", [1, 512], BF16))
                for i in range(2)]

        with contextlib.ExitStack() as s2:
            arena = s2.enter_context(nc.sbuf_tensor("arena", [128, 8 * 4096],
                                                    BF16))
            av = arena[:].rearrange("p (cc q) -> p cc q", q=4096)
            emit_ln_transpose(c, x_src,
                              lambda i, t: av[:, :, t * 128:(t + 1) * 128],
                              arena_to_hbm=(arena, xnt_h))
            with contextlib.ExitStack() as s25:
                w_sb = s25.enter_context(nc.sbuf_tensor("w_sb",
                                                        [128, 8 * 1024], BF16))
                emit_wload(c, [(w_sb, "l0aa_wk")])
                emit_kq(c, arena, karena, w_sb, c.sb["l0aa_bk_fm"],
                        c.sb["l0aa_bq_fm"], qt_h, c.dram["l0aa_wq"])
        varena = s1.enter_context(nc.sbuf_tensor("varena",
                                                 [128, NT * 1040], BF16))
        emit_ones_prefill(c, varena)
        with contextlib.ExitStack() as s3:
            wv_sb = s3.enter_context(nc.sbuf_tensor("wv_sb", [128, 8 * 1024],
                                                    BF16))
            emit_wload(c, [(wv_sb, "l0aa_wv")])
            emit_v65(c, varena, wv_sb, c.sb["l0aa_bv_row"][:], xnt_h,
                     c.cnt["w_ld"])
        emit_absmax(c, karena, cbuf)
        if DEBUG:
            dsem = c.sem("dbg")
            @c.block.gpsimd
            def _(g):
                g.dma_start(out=c.dram["dbg_k"][:, :],
                            in_=karena[:]).then_inc(dsem, 16)
                g.dma_start(out=c.dram["dbg_v"][:, :],
                            in_=varena[:]).then_inc(dsem, 16)
                g.dma_start(out=c.dram["dbg_c"][:, :],
                            in_=cbuf[:]).then_inc(dsem, 16)
                g.wait_ge(dsem, c.cnt["dbg"] + 48)
            c.commit("dbg", 48)
            c.barrier()

        wo_sb = s1.enter_context(nc.sbuf_tensor("wo_sb", [128, 8 * 1024], BF16))
        ps_m = s1.enter_context(nc.psum_tensor("apsm", [1, 512], F32))
        emit_wload(c, [(wo_sb, "l0aa_wo")])
        wlt_o = c.cnt["w_ld"]

        kvv = karena[:].rearrange("p (cc q) -> p cc q", q=4096)
        vvv = varena[:].rearrange("p (t h e) -> p t h e", h=16, e=65)

        def hprep(eng, E, h, qb, hq, meta, hd):
            mk = meta["hq"][(qb, h)]["prep"]
            r0 = (h % 2) * 64
            if eng == "gpsimd":
                if hq >= 2:
                    pmk = meta["hq"][hql[hq - 2]]
                    E.wait_ge(c.sem("a_pe"), pmk["pe_s_last"])
                    E.wait_ge(c.sem("a_ve"), pmk["prep"]["a_ve"])
                E.dma_start(out=qhead[hq % 2][r0:r0 + 64, :],
                            in_=qt_h[h // 2, r0:r0 + 64,
                                     qb * 512:(qb + 1) * 512]
                            ).then_inc(c.sem("a_gld"), 16)
                E.dma_start(out=slqb[hq % 2][:],
                            in_=c.dram["c_slq8"][0:1,
                                                 h * 4096 + qb * 512:
                                                 h * 4096 + (qb + 1) * 512]
                            ).then_inc(c.sem("a_gld"), 16)
            elif eng == "scalar":
                E.wait_ge(c.sem("a_gld"), mk["a_gld"])
                if hq >= 2:
                    E.wait_ge(c.sem("a_pe"), meta["hq"][hql[hq - 2]]["prep"]["a_pe"])
                E.activation(absb[hq % 2][r0:r0 + 64, :],
                             qhead[hq % 2][r0:r0 + 64, :], AF.Abs)
                E.drain().then_inc(c.sem("a_se"), 1)
            elif eng == "tensor":
                E.wait_ge(c.sem("a_se"), mk["a_se"])
                if hq >= 1:
                    E.wait_ge(c.sem("a_ve"), meta["hq"][hql[hq - 1]]["prep"]["a_ve"])
                E.matmul(ps_m[:], cbuf[r0:r0 + 64, h // 2:h // 2 + 1],
                         absb[hq % 2][r0:r0 + 64, :],
                         start=True, stop=True).then_inc(c.sem("a_pe"), 1)
            elif eng == "vector":
                E.wait_ge(c.sem("a_pe"), mk["a_pe"])
                E.wait_ge(c.sem("a_gld"), mk["a_gld"])
                E.scalar_tensor_tensor(shrow[hq % 2][:], ps_m[:], -1.0,
                                       slqb[hq % 2][:], ALU.mult, ALU.subtract)
                E.drain().then_inc(c.sem("a_ve"), 1)

        def smm(t, h, qb, kt, ps_ap, meta, hd):
            hq = qb * 16 + h
            mk = meta["hq"][(qb, h)]["prep"]
            r0 = (h % 2) * 64
            t.wait_ge(c.sem("a_ve"), mk["a_ve"])
            t.matmul(ps_ap, c.sb["ones1"][0:1, 0:128],
                     shrow[hq % 2][0:1, :], start=True, stop=False)
            return t.matmul(ps_ap,
                            kvv[r0:r0 + 64, h // 2, kt * 128:(kt + 1) * 128],
                            qhead[hq % 2][r0:r0 + 64, :],
                            start=False, stop=True)

        def pv(t, h, qb, kt, pt_ap, start, stop, hd):
            hq = qb * 16 + h
            return t.matmul(hd["ps_oz"][hq % 2][0:65, :],
                            vvv[:, kt, h, 0:65], pt_ap,
                            start=start, stop=stop)

        def qprep(eng, E, qb, meta, hd):
            if eng == "tensor" and qb == 0:
                E.wait_ge(c.sem("w_ld"), wlt_o)

        p = {
            "H": 16,
            "kts": lambda qb: list(range(4 * qb + 4)),
            "diag": lambda qb, kt: (kt - 4 * qb) if kt >= 4 * qb else None,
            "mask_kind": "cmask",
            "sep_z": False, "oz_rows": 65, "o_rows": 64,
            "qprep": qprep, "qprep_incs": {},
            "hprep": hprep,
            "hprep_incs": {"a_gld": 32, "a_se": 1, "a_pe": 1, "a_ve": 1},
            "smm": smm, "pv": pv,
            "exp_scale": lambda h, qb, kt: 0.125,
            "exp_bias": lambda h, qb, kt:
                c.sb["alibi_k"][:, h * 32 + kt:h * 32 + kt + 1],
            "out_w": wo_sb[:].rearrange("p (dc f) -> p dc f", f=1024),
            "out_bias_row": c.sb["l0aa_bo_row"][:],
            "oproj_mode": "mm", "ps_y_n": 1,
            "x_src": x_src, "x_dst": x_dst,
        }
        emit_attention(c, p)




def sublayer_l1aa(c, x_src, x_dst):
    nc = c.nc
    hql = [(qb, h) for qb in range(NQB) for h in range(16)]
    with contextlib.ExitStack() as s1:
        emit_sub_consts(c, s1, CAUSAL_CONSTS)
        c.sb["onescol"] = s1.enter_context(
            nc.sbuf_tensor("onescol", [128, 1], BF16))
        arena = s1.enter_context(nc.sbuf_tensor("arena", [128, 8 * 4096], BF16))
        varena = s1.enter_context(nc.sbuf_tensor("varena",
                                                 [128, NT * 1040], BF16))
        cbuf = s1.enter_context(nc.sbuf_tensor("cbuf", [128, 8], BF16))
        absb = [s1.enter_context(nc.sbuf_tensor(f"ab{i}", [128, 512], BF16))
                for i in range(2)]
        shrow = [s1.enter_context(nc.sbuf_tensor(f"sh{i}", [1, 512], BF16))
                 for i in range(2)]
        av = arena[:].rearrange("p (cc q) -> p cc q", q=4096)
        vv = varena[:].rearrange("p (t h e) -> p t h e", h=16, e=65)
        emit_ones_prefill(c, varena)
        ve_sem = c.sem("ln_ve")

        def xn_cb(v, i, t, xn_ap, stat):
            for hh in range(15):
                v.tensor_copy(vv[:, t, hh, 0:64],
                              xn_ap[:, hh * 64:(hh + 1) * 64])
            v.tensor_copy(vv[:, t, 15, 0:64],
                          xn_ap[:, 960:1024]).then_inc(ve_sem, 1)

        emit_ln_transpose(c, x_src,
                          lambda i, t: av[:, :, t * 128:(t + 1) * 128],
                          xn_cb=xn_cb, n_cb_inc=1)
        emit_absmax(c, arena, cbuf)
        ps_m = s1.enter_context(nc.psum_tensor("apsm", [1, 512], F32))

        def hprep(eng, E, h, qb, hq, meta, hd):
            mk = meta["hq"][(qb, h)]["prep"]
            r0 = (h % 2) * 64
            if eng == "scalar":
                if hq >= 2:
                    E.wait_ge(c.sem("a_pe"),
                              meta["hq"][hql[hq - 2]]["prep"]["a_pe"])
                E.activation(absb[hq % 2][r0:r0 + 64, :],
                             av[r0:r0 + 64, h // 2,
                                qb * 512:(qb + 1) * 512], AF.Abs)
                E.drain().then_inc(c.sem("a_se"), 1)
            elif eng == "tensor":
                E.wait_ge(c.sem("a_se"), mk["a_se"])
                if hq >= 1:
                    E.wait_ge(c.sem("a_ve"),
                              meta["hq"][hql[hq - 1]]["prep"]["a_ve"])
                E.matmul(ps_m[:], cbuf[r0:r0 + 64, h // 2:h // 2 + 1],
                         absb[hq % 2][r0:r0 + 64, :],
                         start=True, stop=True).then_inc(c.sem("a_pe"), 1)
            elif eng == "vector":
                E.wait_ge(c.sem("a_pe"), mk["a_pe"])
                E.tensor_single_scalar(shrow[hq % 2][:], ps_m[:], -1.0,
                                       ALU.mult)
                E.drain().then_inc(c.sem("a_ve"), 1)

        def smm(t, h, qb, kt, ps_ap, meta, hd):
            hq = qb * 16 + h
            mk = meta["hq"][(qb, h)]["prep"]
            r0 = (h % 2) * 64
            t.wait_ge(c.sem("a_ve"), mk["a_ve"])
            t.matmul(ps_ap, c.sb["ones1"][0:1, 0:128],
                     shrow[hq % 2][0:1, :], start=True, stop=False)
            return t.matmul(ps_ap,
                            av[r0:r0 + 64, h // 2, kt * 128:(kt + 1) * 128],
                            av[r0:r0 + 64, h // 2, qb * 512:(qb + 1) * 512],
                            start=False, stop=True)

        def pv(t, h, qb, kt, pt_ap, start, stop, hd):
            hq = qb * 16 + h
            return t.matmul(hd["ps_oz"][hq % 2][0:65, :],
                            vv[:, kt, h, 0:65], pt_ap,
                            start=start, stop=stop)

        p = {
            "H": 16,
            "kts": lambda qb: list(range(4 * qb + 4)),
            "diag": lambda qb, kt: (kt - 4 * qb) if kt >= 4 * qb else None,
            "mask_kind": "cmask",
            "sep_z": False, "oz_rows": 65, "o_rows": 64,
            "qprep": None, "qprep_incs": {},
            "hprep": hprep,
            "hprep_incs": {"a_se": 1, "a_pe": 1, "a_ve": 1},
            "smm": smm, "pv": pv,
            "exp_scale": lambda h, qb, kt: 0.125,
            "exp_bias": lambda h, qb, kt: None,
            "out_w": None, "out_bias_row": None,
            "oproj_mode": "tr",
            "x_src": x_src, "x_dst": x_dst,
        }
        emit_attention(c, p)


def declare_l1aa(c, nc):
    pass


def prep_l1aa(c):
    pass


def declare_common(c, nc):
    for name, arr in host_consts().items():
        c.dram[name] = nc.dram_tensor(name, list(arr.shape), F32,
                                      kind="ExternalInput").ap()
    dbg = "ExternalOutput" if DEBUG else "Internal"
    c.dram["xnt_h"] = nc.dram_tensor("xnt_h", [8, 128, 4096], BF16,
                                     kind=dbg).ap()
    c.dram["qt_h"] = nc.dram_tensor("qt_h", [8, 128, 4096], BF16,
                                    kind=dbg).ap()
    c.dram["qtok_h"] = nc.dram_tensor("qtok_h", [32, 128, 1024], BF16,
                                      kind="Internal").ap()
    if DEBUG:
        c.dram["dbg_sc"] = nc.dram_tensor("dbg_sc", [128, 8 * 16], F32,
                                          kind="ExternalOutput").ap()
        c.dram["dbg_st"] = nc.dram_tensor("dbg_st", [128, 8], F32,
                                          kind="ExternalOutput").ap()
        c.dram["dbg_r"] = nc.dram_tensor("dbg_r", [128, 72], F32,
                                         kind="ExternalOutput").ap()
        c.dram["dbg_cmb"] = nc.dram_tensor("dbg_cmb", [128, 8 * 16], F32,
                                           kind="ExternalOutput").ap()
        c.dram["dbg_l2f"] = nc.dram_tensor("dbg_l2f", [128, 8], F32,
                                           kind="ExternalOutput").ap()
        c.dram["dbg_acc"] = nc.dram_tensor("dbg_acc", [128, 8 * 1024], F32,
                                           kind="ExternalOutput").ap()
        c.dram["dbg_h"] = nc.dram_tensor("dbg_h", [128, 4 * 1024], BF16,
                                         kind="ExternalOutput").ap()
        c.dram["dbg_q"] = nc.dram_tensor("dbg_q", [128, 8 * 1024], BF16,
                                         kind="ExternalOutput").ap()
        c.dram["dbg_k"] = nc.dram_tensor("dbg_k", [128, 8 * 4096], BF16,
                                         kind="ExternalOutput").ap()
        c.dram["dbg_v"] = nc.dram_tensor("dbg_v", [128, NT * 16 * 65], BF16,
                                         kind="ExternalOutput").ap()
        c.dram["dbg_c"] = nc.dram_tensor("dbg_c", [128, 8], BF16,
                                         kind="ExternalOutput").ap()
        c.dram["dbg_o"] = nc.dram_tensor("dbg_o", [8, 128, 8 * 512], BF16,
                                         kind="ExternalOutput").ap()
    rkind = "ExternalOutput" if DEBUG2 else "Internal"
    c.dram["resid0"] = nc.dram_tensor("resid0", [4096, 1024], F32,
                                      kind=rkind).ap()
    c.dram["resid1"] = nc.dram_tensor("resid1", [4096, 1024], F32,
                                      kind=rkind).ap()


def declare_l0aa(c, nc):
    for n_, sh in [("l0aa_wq", [1024, 1024]), ("l0aa_wk", [1024, 1024]),
                   ("l0aa_wv", [1024, 1024]), ("l0aa_wo", [1024, 1024]),
                   ("l0aa_bq", [1, 1024]), ("l0aa_bk", [1, 1024]),
                   ("l0aa_bv", [1, 1024]), ("l0aa_bo", [1, 1024])]:
        c.dram[n_] = nc.dram_tensor(n_, sh, F32, kind="ExternalInput").ap()


def prep_l0aa(c):
    pass


PART_DECL = {"l0aa": declare_l0aa, "l1aa": declare_l1aa}
PART_PREP = {"l0aa": prep_l0aa, "l1aa": prep_l1aa}
PART_EMIT = {"l0aa": sublayer_l0aa, "l1aa": sublayer_l1aa}




def host_inputs_per_core(inputs, core):
    m = {}
    m["x_in"] = np.ascontiguousarray(inputs["L0_attn_a_X"][core])
    wqkv = inputs["L0_attn_a_W_qkv"].reshape(1024, 16, 3, 64)
    bqkv = inputs["L0_attn_a_b_qkv"].reshape(16, 3, 64)
    m["l0aa_wq"] = np.ascontiguousarray(wqkv[:, :, 0, :].reshape(1024, 1024))
    m["l0aa_wk"] = np.ascontiguousarray(wqkv[:, :, 1, :].reshape(1024, 1024))
    m["l0aa_wv"] = np.ascontiguousarray(wqkv[:, :, 2, :].reshape(1024, 1024))
    m["l0aa_bq"] = np.ascontiguousarray(bqkv[:, 0, :].reshape(1, 1024))
    m["l0aa_bk"] = np.ascontiguousarray(bqkv[:, 1, :].reshape(1, 1024))
    m["l0aa_bv"] = np.ascontiguousarray(bqkv[:, 2, :].reshape(1, 1024))
    m["l0aa_wo"] = inputs["L0_attn_a_W_out"]
    m["l0aa_bo"] = inputs["L0_attn_a_b_out"].reshape(1, 1024)
    host_inputs_ffn(inputs, m)
    host_inputs_l0ab(inputs, m)
    host_inputs_l1ab(inputs, m)
    for cname, arr in host_consts().items():
        m[cname] = arr
    return m


def build_nc(parts):
    nc = bass.Bass("TRN2", target_bir_lowering=False)
    x_in = nc.dram_tensor("x_in", [4096, 1024], F32, kind="ExternalInput").ap()
    x_out = nc.dram_tensor("x_out", [4096, 1024], F32,
                           kind="ExternalOutput").ap()
    stack = contextlib.ExitStack()
    with stack:
        c = Ctx(nc, stack)
        declare_common(c, nc)
        for pn in parts:
            PART_DECL[pn](c, nc)
        emit_consts(c)
        for pn in parts:
            PART_PREP[pn](c)
        for i, pn in enumerate(parts):
            src = x_in if i == 0 else c.dram[f"resid{i % 2}"]
            dst = (x_out if i == len(parts) - 1
                   else c.dram[f"resid{(i + 1) % 2}"])
            PART_EMIT[pn](c, src, dst)
    return nc


def kernel(**inputs):
    maps = [host_inputs_per_core(inputs, core) for core in range(N_CORES)]
    parts = ["l0aa", "l0f1", "l0ab", "l0f2", "l1aa", "l1f1", "l1ab", "l1f2"]
    nc = build_nc(parts)
    prefixes = {PART_PREFIX[p] for p in parts}
    in_maps = []
    for core in range(N_CORES):
        m = {"x_in": maps[core]["x_in"]}
        for k, v in maps[core].items():
            if k.startswith("c_") or any(k.startswith(p) for p in prefixes):
                m[k] = v
        in_maps.append(m)
    res = run_bass_kernel_spmd(nc, in_maps, core_ids=list(range(N_CORES)))
    return np.stack([r["x_out"] for r in res.results], axis=0)


def emit_gwn_prep_l0(c):
    nc, st = c.nc, c.stack
    out = st.enter_context(nc.sbuf_tensor("l0f_gwnT", [128, 8 * 16], BF16))
    c.sb["l0f_gwnT"] = out
    ld = c.sem("g_ld"); pe = c.sem("g_pe"); ve = c.sem("g_ve"); se = c.sem("g_se")
    ld0 = c.base("g_ld"); pe0 = c.base("g_pe"); ve0 = c.base("g_ve")
    se0 = c.base("g_se")
    with contextlib.ExitStack() as ph:
        gw = ph.enter_context(nc.sbuf_tensor("gwp", [16, 1024], F32))
        gs = ph.enter_context(nc.sbuf_tensor("gwst", [16, 4], F32))
        gwn = ph.enter_context(nc.sbuf_tensor("gwnb", [16, 1024], BF16))
        scr = ph.enter_context(nc.sbuf_tensor("gwsc", [16, 1024], F32))
        ps = ph.enter_context(nc.psum_tensor("gwps", [128, 8 * 16], BF16))

        @c.block.gpsimd
        def _(g):
            g.dma_start(out=gw[:], in_=c.dram["l0f_gate_w"][:, :]).then_inc(ld, 16)
            g.wait_ge(ld, ld0 + 16)

        @c.block.scalar
        def _(s):
            s.wait_ge(ld, ld0 + 16)
            s.activation(scr[:], gw[:], AF.Square, accum_out=gs[:, 0:1])
            s.activation(gs[:, 1:2], gs[:, 0:1], AF.Ln)
            s.activation(gs[:, 2:3], gs[:, 1:2], AF.Exp, scale=-0.5)
            s.drain().then_inc(se, 1)

        @c.block.vector
        def _(v):
            v.wait_ge(se, se0 + 1)
            v.tensor_scalar(gwn[:], gw[:], gs[:, 2:3], None, ALU.mult)
            v.drain().then_inc(ve, 1)

        @c.block.tensor
        def _(t):
            t.wait_ge(ve, ve0 + 1)
            for cc in range(8):
                ins = t.transpose(ps[:, cc * 16:(cc + 1) * 16],
                                  gwn[:, cc * 128:(cc + 1) * 128],
                                  c.sb["ident_bf"][0:16, 0:16])
            ins.then_inc(pe, 1)

        @c.block.vector
        def _(v):
            v.wait_ge(pe, pe0 + 1)
            v.tensor_copy(out[:], ps[:])
            v.drain().then_inc(ve, 1)

        c.commit("g_ld", 16)
        c.commit("g_se", 1)
        c.commit("g_pe", 1)
        c.commit("g_ve", 2)
    c.barrier()


def _emit_routing(c, L, bufs):
    nc = c.nc
    ve = c.sem("f_ve"); se = c.sem("f_se")
    ve0 = c.base("f_ve"); se0 = c.base("f_se")
    s3, w3, c3, k3 = bufs["s3"], bufs["w3"], bufs["c3"], bufs["k3"]
    mred = bufs["mred"]
    m83 = mred[:, 0:8].rearrange("p (t o) -> p t o", o=1)
    rw3 = mred[:, 8:16].rearrange("p (t o) -> p t o", o=1)
    junk = mred[:, 8:9]

    def bc(ap, like):
        a, _ = bass.broadcast_tensor_aps(ap, like)
        return a

    @c.block.vector
    def _(v):
        def pad(n=4):
            for _i in range(n):
                v.tensor_single_scalar(bufs["mred"][:, 15:16],
                                       bufs["s3"][:, 0, 0:1], 1.0, ALU.mult)
        nrounds = 8 if L["route"] == "l0" else 4
        if L["route"] == "l0":
            v.tensor_copy(w3, s3)
            v.tensor_copy(k3, s3)
        else:
            scores2 = bufs["scores2"]
            wrk2 = bufs["wrk2"]
            gm = bufs["gm4"]
            gs = bufs["gs4"]
            gm2 = bufs["gm2"]
            gs2 = bufs["gs2"]
            ta, tb_, tc, td = bufs["tt4"]
            ta2, tb2, tc2, td2 = bufs["tt2"]
            A = scores2[:].rearrange("p (n e) -> p n e", e=4)[:, :, 0:1]
            B = scores2[:].rearrange("p (n e) -> p n e", e=4)[:, :, 1:2]
            Cc = scores2[:].rearrange("p (n e) -> p n e", e=4)[:, :, 2:3]
            Dd = scores2[:].rearrange("p (n e) -> p n e", e=4)[:, :, 3:4]
            t_a = ta2.rearrange("p (n o) -> p n o", o=1)
            t_b = tb2.rearrange("p (n o) -> p n o", o=1)
            t_c = tc2.rearrange("p (n o) -> p n o", o=1)
            t_d = td2.rearrange("p (n o) -> p n o", o=1)
            v.tensor_max(t_a, A, B)
            v.tensor_tensor(t_b, A, B, ALU.min)
            v.tensor_max(t_c, Cc, Dd)
            v.tensor_tensor(t_d, Cc, Dd, ALU.min)
            pad()
            v.tensor_tensor(gm2, ta2, tc2, ALU.min)
            v.tensor_max(gs2, tb2, td2)
            pad()
            v.tensor_max(gm2, gm2, gs2)
            v.tensor_max(gs2, ta2, tc2)
            pad()
            v.tensor_add(gs2, gs2, gm2)
            pad()
            gsv = gs2.rearrange("p (t g) -> p t g", g=4)
            v.tensor_max(ta[:, :, 0:1], gsv[:, :, 0:1], gsv[:, :, 1:2])
            v.tensor_tensor(tb_[:, :, 0:1], gsv[:, :, 0:1], gsv[:, :, 1:2],
                            ALU.min)
            v.tensor_max(tc[:, :, 0:1], gsv[:, :, 2:3], gsv[:, :, 3:4])
            v.tensor_tensor(td[:, :, 0:1], gsv[:, :, 2:3], gsv[:, :, 3:4],
                            ALU.min)
            pad()
            v.tensor_tensor(ta[:, :, 0:1], ta[:, :, 0:1], tc[:, :, 0:1],
                            ALU.min)
            v.tensor_max(tb_[:, :, 0:1], tb_[:, :, 0:1], td[:, :, 0:1])
            pad()
            v.tensor_max(m83, ta[:, :, 0:1], tb_[:, :, 0:1])
            pad()
            v.tensor_tensor(gm, gs, bc(m83, gs), ALU.is_ge)
            pad()
            if "dbg_gm" in bufs:
                v.tensor_copy(bufs["dbg_gm"], gm)
                v.tensor_copy(bufs["dbg_g2"], m83)
                v.tensor_copy(bufs["dbg_gs"], gs)
            gmb = bc(gm.rearrange("p t (g o) -> p t g o", o=1), bufs["s4"])
            v.tensor_tensor(bufs["w4"], bufs["s4"], gmb, ALU.mult)
            v.tensor_copy(k3, w3)
        for r in range(nrounds):
            v.reduce_max(m83, w3, axis=AX.X)
            v.reduce_sum(junk[:, 0:1], s3[:, 0, 0:2], axis=AX.X)
            if r < nrounds - 1:
                v.tensor_tensor(c3, w3, bc(m83, w3), ALU.is_ge)
                v.scalar_tensor_tensor(w3, c3, -1e30, w3, ALU.mult, ALU.add)
        v.tensor_tensor(c3, k3, bc(m83, k3), ALU.is_ge)
        v.tensor_mul(w3, k3, c3)
        v.reduce_sum(m83, w3, axis=AX.X)
        v.reduce_sum(junk[:, 0:1], s3[:, 0, 0:2], axis=AX.X)
        v.drain().then_inc(ve, 1)
        v.wait_ge(se, se0 + 1)
        v.tensor_tensor(c3, w3, bc(rw3, w3), ALU.mult)
        v.drain().then_inc(ve, 1)

    @c.block.scalar
    def _(s):
        s.wait_ge(ve, ve0 + 1)
        if L["route"] == "l0":
            s.activation(mred[:, 8:16], mred[:, 0:8], AF.Ln)
            s.activation(mred[:, 8:16], mred[:, 8:16], AF.Exp, scale=-1.0)
        else:
            s.activation(mred[:, 8:16], mred[:, 0:8], AF.Ln,
                         bias=c.sb["kconst"][:, 4:5])
            s.activation(mred[:, 8:16], mred[:, 8:16], AF.Exp, scale=-1.0,
                         bias=c.sb["kconst"][:, 5:6])
        s.drain().then_inc(se, 1)

    c.commit("f_ve", 2)
    c.commit("f_se", 1)
    c.barrier()


def _emit_experts(c, L, units, qv, av, hv, gbuf, c3, wgb, wub, wdb):
    nc = c.nc
    has_gate = L["has_gate"]
    pe = c.sem("f_pe"); se = c.sem("f_se"); ve = c.sem("f_ve")
    gl = c.sem("f_gl")
    pe0 = c.base("f_pe"); se0 = c.base("f_se"); ve0 = c.base("f_ve")
    gl0 = c.base("f_gl")

    ngrp_hc = 2 * (2 if has_gate else 1)
    pe_u = 4 * ngrp_hc + 16
    se_u = 8
    ve_u = (8 if has_gate else 0) + 16
    ld_u = 16 * (3 if has_gate else 2)

    def pe_gate(u, hc, tb):
        return pe0 + u * pe_u + hc * ngrp_hc + tb + 1

    def pe_up(u, hc, tb):
        return pe0 + u * pe_u + hc * ngrp_hc + 2 + tb + 1

    def pe_down(u, g16):
        return pe0 + u * pe_u + 4 * ngrp_hc + g16 + 1

    def se_silu(u, hc, tb):
        return se0 + u * se_u + hc * 2 + tb + 1

    def ve_hm(u, hc, tb):
        return ve0 + u * ve_u + hc * 2 + tb + 1

    def ve_acc(u, g16):
        return ve0 + u * ve_u + (8 if has_gate else 0) + g16 + 1

    NU = len(units)
    with contextlib.ExitStack() as ph:
        ps_g = [ph.enter_context(nc.psum_tensor(f"fpg{i}", [128, 512], F32))
                for i in range(2)]
        ps_u = ([ph.enter_context(nc.psum_tensor(f"fpu{i}", [128, 512], F32))
                 for i in range(2)] if has_gate else ps_g)
        ps_y = [ph.enter_context(nc.psum_tensor(f"fpy{i}", [128, 512], F32))
                for i in range(2)]

        @c.block.gpsimd
        def _(g):
            for u, un in enumerate(units):
                b = u % 2
                if u >= 2:
                    g.wait_ge(pe, pe0 + (u - 1) * pe_u)
                g.dma_start(out=wgb[b][:].rearrange("p (dc f) -> p dc f", f=512),
                            in_=un["wg"].rearrange("(dc p) f -> p dc f", p=128)
                            ).then_inc(gl, 16)
                if has_gate:
                    g.dma_start(out=wub[b][:].rearrange("p (dc f) -> p dc f",
                                                        f=512),
                                in_=un["wu"].rearrange("(dc p) f -> p dc f",
                                                       p=128)).then_inc(gl, 16)
                g.dma_start(out=wdb[b][:].rearrange("p (hc f) -> p hc f",
                                                    f=1024),
                            in_=un["wd"].rearrange("(hc p) f -> p hc f", p=128)
                            ).then_inc(gl, 16)
            g.wait_ge(gl, gl0 + ld_u * NU)

        @c.block.tensor
        def _(t):
            for u, un in enumerate(units):
                b = u % 2
                wgv = wgb[b][:].rearrange("p (dc f) -> p dc f", f=512)
                wuv = (wub[b][:].rearrange("p (dc f) -> p dc f", f=512)
                       if has_gate else None)
                wdv = wdb[b][:].rearrange("p (hc f) -> p hc f", f=1024)
                t.wait_ge(gl, gl0 + ld_u * (u + 1))
                for hc in range(4):
                    for srcsel in range(2 if has_gate else 1):
                        psx = ps_g if srcsel == 0 else ps_u
                        wv = wgv if srcsel == 0 else wuv
                        if u > 0 or hc > 0:
                            phc, pu = (hc - 1, u) if hc > 0 else (3, u - 1)
                            for tb in range(2):
                                if srcsel == 0 or not has_gate:
                                    t.wait_ge(se, se_silu(pu, phc, tb))
                                else:
                                    t.wait_ge(ve, ve_hm(pu, phc, tb))
                        last = {}
                        for dc in range(8):
                            for tb in range(2):
                                ins = t.matmul(
                                    psx[tb][:],
                                    wv[:, dc, hc * 128:(hc + 1) * 128],
                                    qv[:, dc, tb * 512:(tb + 1) * 512],
                                    start=(dc == 0), stop=(dc == 7))
                                if dc == 7:
                                    last[tb] = ins
                        last[0].then_inc(pe, 1)
                        last[1].then_inc(pe, 1)
                for g16 in range(16):
                    t8, ob = divmod(g16, 2)
                    gg = u * 16 + g16
                    if gg >= 2:
                        pu2, pg2 = divmod(gg - 2, 16)
                        t.wait_ge(ve, ve_acc(pu2, pg2))
                    if g16 == 0:
                        if has_gate:
                            t.wait_ge(ve, ve_hm(u, 3, 1))
                        else:
                            t.wait_ge(se, se_silu(u, 3, 1))
                    for hcc in range(4):
                        ins = t.matmul(ps_y[gg % 2][:],
                                       hv[:, hcc, t8 * 128:(t8 + 1) * 128],
                                       wdv[:, hcc, ob * 512:(ob + 1) * 512],
                                       start=(hcc == 0), stop=(hcc == 3))
                    ins.then_inc(pe, 1)

        @c.block.scalar
        def _(s):
            for u, un in enumerate(units):
                for hc in range(4):
                    for tb in range(2):
                        s.wait_ge(pe, pe_gate(u, hc, tb))
                        if has_gate:
                            if u > 0 or hc > 0:
                                phc, pu = (hc - 1, u) if hc > 0 else (3, u - 1)
                                s.wait_ge(ve, ve_hm(pu, phc, tb))
                            s.activation(gbuf[:, tb * 512:(tb + 1) * 512],
                                         ps_g[tb][:], AF.Silu)
                        else:
                            s.activation(hv[:, hc, tb * 512:(tb + 1) * 512],
                                         ps_g[tb][:], AF.Silu)
                        s.drain().then_inc(se, 1)

        @c.block.vector
        def _(v):
            for u, un in enumerate(units):
                if has_gate:
                    for hc in range(4):
                        for tb in range(2):
                            v.wait_ge(se, se_silu(u, hc, tb))
                            v.wait_ge(pe, pe_up(u, hc, tb))
                            v.tensor_mul(hv[:, hc, tb * 512:(tb + 1) * 512],
                                         gbuf[:, tb * 512:(tb + 1) * 512],
                                         ps_u[tb][:])
                            v.drain().then_inc(ve, 1)
                for g16 in range(16):
                    t8, ob = divmod(g16, 2)
                    gg = u * 16 + g16
                    v.wait_ge(pe, pe_down(u, g16))
                    dst = av[:, t8, ob * 512:(ob + 1) * 512]
                    if un["cmb"] is None:
                        if u == 0:
                            v.tensor_copy(dst, ps_y[gg % 2][:])
                        else:
                            v.tensor_add(dst, ps_y[gg % 2][:], dst)
                    else:
                        col = c3[:, t8, un["cmb"]:un["cmb"] + 1]
                        if u == 0:
                            v.tensor_scalar(dst, ps_y[gg % 2][:], col, None,
                                            ALU.mult)
                        else:
                            v.scalar_tensor_tensor(dst, ps_y[gg % 2][:], col,
                                                   dst, ALU.mult, ALU.add)
                    v.drain().then_inc(ve, 1)

        c.commit("f_gl", ld_u * NU)
        c.commit("f_pe", pe_u * NU)
        c.commit("f_se", se_u * NU)
        c.commit("f_ve", ve_u * NU)
    c.barrier()


def sublayer_ffn(c, x_src, x_dst, L):
    nc = c.nc
    px = L["prefix"]
    has_gate = L["has_gate"]
    with contextlib.ExitStack() as s1:
        qarena = s1.enter_context(nc.sbuf_tensor("qarena", [128, 8 * 1024], BF16))
        acc = s1.enter_context(nc.sbuf_tensor("facc", [128, 8 * 1024], F32))
        harena = s1.enter_context(nc.sbuf_tensor("harena", [128, 4 * 1024], BF16))
        scores = s1.enter_context(nc.sbuf_tensor("fsc", [128, 8 * 16], F32))
        wrk = s1.enter_context(nc.sbuf_tensor("fwrk", [128, 8 * 16], F32))
        keep = s1.enter_context(nc.sbuf_tensor("fkp", [128, 8 * 16], F32))
        mred = s1.enter_context(nc.sbuf_tensor("fm", [128, 16], F32))
        cmb = s1.enter_context(nc.sbuf_tensor("fcmb", [128, 8 * 16], F32))
        gm4 = s1.enter_context(nc.sbuf_tensor("fgm", [128, 8 * 4], F32))
        gs4 = s1.enter_context(nc.sbuf_tensor("fgs", [128, 8 * 4], F32))
        tt4 = s1.enter_context(nc.sbuf_tensor("ftt", [128, 8 * 16], F32))
        gbuf = s1.enter_context(nc.sbuf_tensor("fgb", [128, 1024], BF16))
        l2fb = s1.enter_context(nc.sbuf_tensor("fl2f", [128, 8], F32))
        dbgst = s1.enter_context(nc.sbuf_tensor("fdbgst", [128, 8], F32))
        dbgr = s1.enter_context(nc.sbuf_tensor("fdbgr", [128, 72], F32))
        xres = [s1.enter_context(nc.sbuf_tensor(f"fxr{i}", [128, 1024], F32))
                for i in range(2)]
        wgb = [s1.enter_context(nc.sbuf_tensor(f"fwg{i}", [128, 8 * 512], BF16))
               for i in range(2)]
        wub = ([s1.enter_context(nc.sbuf_tensor(f"fwu{i}", [128, 8 * 512],
                                                BF16)) for i in range(2)]
               if has_gate else None)
        wdb = [s1.enter_context(nc.sbuf_tensor(f"fwd{i}", [128, 4 * 1024], BF16))
               for i in range(2)]
        qv = qarena[:].rearrange("p (dc q) -> p dc q", q=1024)
        av = acc[:].rearrange("p (t f) -> p t f", f=1024)
        hv = harena[:].rearrange("p (hc f) -> p hc f", f=1024)
        s3 = scores[:].rearrange("p (t e) -> p t e", e=16)
        w3 = wrk[:].rearrange("p (t e) -> p t e", e=16)
        c3 = cmb[:].rearrange("p (t e) -> p t e", e=16)
        k3 = keep[:].rearrange("p (t e) -> p t e", e=16)
        bufs = {"s3": s3, "w3": w3, "c3": c3, "k3": k3, "mred": mred[:],
                "scores2": scores, "wrk2": wrk,
                "gm2": gm4[:], "gs2": gs4[:],
                "tt2": [tt4[:, i * 32:(i + 1) * 32] for i in range(4)],

                "s4": scores[:].rearrange("p (t g e) -> p t g e", g=4, e=4),
                "w4": wrk[:].rearrange("p (t g e) -> p t g e", g=4, e=4),
                "gm4": gm4[:].rearrange("p (t g) -> p t g", g=4),
                "gs4": gs4[:].rearrange("p (t g) -> p t g", g=4),
                "tt4": [tt4[:].rearrange("p (i t g) -> i p t g", i=4)[i]
                        if False else
                        tt4[:, i * 32:(i + 1) * 32].rearrange(
                            "p (t g) -> p t g", g=4) for i in range(4)]}

        def mk_units():
            units = []
            if L["route"] == "l0":
                for e in range(16):
                    units.append({"wg": c.dram[f"{px}_w_gate"][e],
                                  "wu": c.dram[f"{px}_w_up"][e],
                                  "wd": c.dram[f"{px}_w_down"][e], "cmb": e})
                units.append({"wg": c.dram[f"{px}_sw_gate"],
                              "wu": c.dram[f"{px}_sw_up"],
                              "wd": c.dram[f"{px}_sw_down"], "cmb": None})
            else:
                for e in range(16):
                    units.append({"wg": c.dram[f"{px}_expert_up"][e],
                                  "wu": None,
                                  "wd": c.dram[f"{px}_expert_down"][e],
                                  "cmb": e})
                for g in range(4):
                    units.append({"wg": c.dram[f"{px}_shared_up"]
                                  [:, g * 512:(g + 1) * 512],
                                  "wu": None,
                                  "wd": c.dram[f"{px}_shared_down"]
                                  [g * 512:(g + 1) * 512, :], "cmb": None})
            return units

        for q in range(4):
            tiles = list(range(q * 8, q * 8 + 8))
            if L["route"] == "l0":
                ln_sc = c.sem("ln_sc")
                sc0 = c.base("ln_sc")

                def act_cb(s, i, t, stat, xn_ap, xsq, q=q):
                    s.activation(xsq[:], xn_ap[:], AF.Square,
                                 accum_out=stat[:, 8:9])
                    s.activation(stat[:, 9:10], stat[:, 8:9], AF.Ln)
                    s.activation(l2fb[:, i:i + 1], stat[:, 9:10], AF.Exp,
                                 scale=-0.5)
                    s.drain().then_inc(ln_sc, 1)

                def pre_xn(v, i, sc0=sc0):
                    if i >= 2:
                        v.wait_ge(ln_sc, sc0 + i - 1)

                emit_ln_transpose(c, x_src,
                                  lambda i, t: qv[:, :, i * 128:(i + 1) * 128],
                                  tiles=tiles, act_cb=act_cb, n_acb_inc=1,
                                  pre_xn=pre_xn)
                c.commit("ln_sc", 8)
            else:
                emit_ln_transpose(c, x_src,
                                  lambda i, t: qv[:, :, i * 128:(i + 1) * 128],
                                  tiles=tiles)

            pe = c.sem("f_pe"); se = c.sem("f_se")
            pe0 = c.base("f_pe"); se0 = c.base("f_se")
            gwT = (c.sb["l0f_gwnT"] if L["route"] == "l0" else c.sb["l1f_gwT"])
            gv = gwT[:].rearrange("p (dc e) -> p dc e", e=16)
            with contextlib.ExitStack() as phb:
                ps_r = phb.enter_context(nc.psum_tensor("fpr", [128, 16], F32))

                @c.block.tensor
                def _(t):
                    for i in range(8):
                        if i >= 1:
                            t.wait_ge(se, se0 + i)
                        for dc in range(8):
                            ins = t.matmul(ps_r[:],
                                           qv[:, dc, i * 128:(i + 1) * 128],
                                           gv[:, dc, :], start=(dc == 0),
                                           stop=(dc == 7))
                        ins.then_inc(pe, 1)

                @c.block.scalar
                def _(s):
                    for i in range(8):
                        s.wait_ge(pe, pe0 + i + 1)
                        kw = {}
                        if L["route"] == "l0":
                            kw["scale"] = l2fb[:, i:i + 1]
                        s.activation(s3[:, i, :], ps_r[:], AF.Sigmoid, **kw)
                        s.drain().then_inc(se, 1)

                c.commit("f_pe", 8)
                c.commit("f_se", 8)
            c.barrier()

            if DEBUG and q == 0 and L["route"] == "l1":
                bufs["dbg_gm"] = dbgr[:, 0:32].rearrange("p (t g) -> p t g", g=4)
                bufs["dbg_g2"] = dbgr[:, 32:40].rearrange("p (t o) -> p t o", o=1)
                bufs["dbg_gs"] = dbgr[:, 40:72].rearrange("p (t g) -> p t g", g=4)
            _emit_routing(c, L, bufs)
            bufs.pop("dbg_gm", None)
            bufs.pop("dbg_g2", None)
            bufs.pop("dbg_gs", None)
            if DEBUG and q == 0:
                dsem = c.sem("dbg")
                d0 = c.base("dbg")
                @c.block.gpsimd
                def _(g):
                    g.dma_start(out=c.dram["dbg_sc"][:, :],
                                in_=scores[:]).then_inc(dsem, 16)
                    g.dma_start(out=c.dram["dbg_cmb"][:, :],
                                in_=cmb[:]).then_inc(dsem, 16)
                    g.dma_start(out=c.dram["dbg_l2f"][:, :],
                                in_=l2fb[:]).then_inc(dsem, 16)
                    g.dma_start(out=c.dram["dbg_q"][:, :],
                                in_=qarena[:]).then_inc(dsem, 16)
                    g.dma_start(out=c.dram["dbg_st"][:, :],
                                in_=dbgst[:]).then_inc(dsem, 16)
                    g.dma_start(out=c.dram["dbg_r"][:, :],
                                in_=dbgr[:]).then_inc(dsem, 16)
                    g.wait_ge(dsem, d0 + 96)
                c.commit("dbg", 96)
                c.barrier()

            _emit_experts(c, L, mk_units(), qv, av, hv, gbuf, c3, wgb, wub, wdb)

            if DEBUG and q == 0:
                dsem = c.sem("dbg")
                d0 = c.base("dbg")
                @c.block.gpsimd
                def _(g):
                    g.dma_start(out=c.dram["dbg_acc"][:, :],
                                in_=acc[:]).then_inc(dsem, 16)
                    g.dma_start(out=c.dram["dbg_h"][:, :],
                                in_=harena[:]).then_inc(dsem, 16)
                    g.wait_ge(dsem, d0 + 32)
                c.commit("dbg", 32)
                c.barrier()

            ve = c.sem("f_ve"); xl = c.sem("f_xl"); sts = c.sem("f_st")
            ve0 = c.base("f_ve"); xl0 = c.base("f_xl"); st0 = c.base("f_st")

            @c.block.sync
            def _(sy):
                for i, t in enumerate(tiles):
                    b = i % 2
                    if i >= 2:
                        sy.wait_ge(sts, st0 + 16 * (i - 1))
                    sy.dma_start(out=xres[b][:],
                                 in_=x_src[t * 128:(t + 1) * 128, :]
                                 ).then_inc(xl, 16)
                    sy.wait_ge(ve, ve0 + i + 1)
                    sy.dma_start(out=x_dst[t * 128:(t + 1) * 128, :],
                                 in_=xres[b][:]).then_inc(sts, 16)
                sy.wait_ge(sts, st0 + 16 * 8)

            @c.block.vector
            def _(v):
                for i, t in enumerate(tiles):
                    b = i % 2
                    v.wait_ge(xl, xl0 + 16 * (i + 1))
                    v.tensor_add(xres[b][:], av[:, i, :], xres[b][:])
                    v.drain().then_inc(ve, 1)

            c.commit("f_xl", 16 * 8)
            c.commit("f_st", 16 * 8)
            c.commit("f_ve", 8)
            c.barrier()


def sublayer_l0f(c, x_src, x_dst):
    sublayer_ffn(c, x_src, x_dst,
                 {"prefix": "l0f", "has_gate": True, "route": "l0"})


def sublayer_l1f(c, x_src, x_dst):
    sublayer_ffn(c, x_src, x_dst,
                 {"prefix": "l1f", "has_gate": False, "route": "l1"})


def declare_l0f(c, nc):
    if "l0f_gate_w" in c.dram:
        return
    for n_, sh in [("l0f_gate_w", [16, 1024]),
                   ("l0f_w_gate", [16, 1024, 512]),
                   ("l0f_w_up", [16, 1024, 512]),
                   ("l0f_w_down", [16, 512, 1024]),
                   ("l0f_sw_gate", [1024, 512]), ("l0f_sw_up", [1024, 512]),
                   ("l0f_sw_down", [512, 1024])]:
        c.dram[n_] = nc.dram_tensor(n_, sh, F32, kind="ExternalInput").ap()


def declare_l1f(c, nc):
    if "l1f_gwT" in c.dram:
        return
    for n_, sh in [("l1f_gwT", [1024, 16]),
                   ("l1f_expert_up", [16, 1024, 512]),
                   ("l1f_expert_down", [16, 512, 1024]),
                   ("l1f_shared_up", [1024, 2048]),
                   ("l1f_shared_down", [2048, 1024])]:
        c.dram[n_] = nc.dram_tensor(n_, sh, F32, kind="ExternalInput").ap()


def prep_l0f(c):
    if "l0f_gwnT" in c.sb:
        return
    emit_gwn_prep_l0(c)


def prep_l1f(c):
    if "l1f_gwT" in c.sb:
        return
    nc, st = c.nc, c.stack
    out = st.enter_context(nc.sbuf_tensor("l1f_gwT", [128, 8 * 16], BF16))
    c.sb["l1f_gwT"] = out
    ld = c.sem("g_ld")
    ld0 = c.base("g_ld")

    @c.block.gpsimd
    def _(g):
        g.dma_start(out=out[:].rearrange("p (dc e) -> p dc e", e=16),
                    in_=c.dram["l1f_gwT"].rearrange("(dc p) e -> p dc e",
                                                    p=128)).then_inc(ld, 16)
        g.wait_ge(ld, ld0 + 16)
    c.commit("g_ld", 16)
    c.barrier()


PART_PREFIX = {"l0aa": "l0aa", "l1aa": "l1aa", "l0f1": "l0f", "l0f2": "l0f",
               "l1f1": "l1f", "l1f2": "l1f", "l0ab": "l0ab", "l1ab": "l1ab"}
PART_DECL.update({"l0f1": declare_l0f, "l0f2": declare_l0f,
                  "l1f1": declare_l1f, "l1f2": declare_l1f})
PART_PREP.update({"l0f1": prep_l0f, "l0f2": prep_l0f,
                  "l1f1": prep_l1f, "l1f2": prep_l1f})
PART_EMIT.update({"l0f1": sublayer_l0f, "l0f2": sublayer_l0f,
                  "l1f1": sublayer_l1f, "l1f2": sublayer_l1f})


def host_inputs_ffn(inputs, m):
    m["l0f_gate_w"] = inputs["L0_ffn_gate_w"]
    m["l0f_w_gate"] = inputs["L0_ffn_w_gate"]
    m["l0f_w_up"] = inputs["L0_ffn_w_up"]
    m["l0f_w_down"] = inputs["L0_ffn_w_down"]
    m["l0f_sw_gate"] = inputs["L0_ffn_sw_gate"]
    m["l0f_sw_up"] = inputs["L0_ffn_sw_up"]
    m["l0f_sw_down"] = inputs["L0_ffn_sw_down"]
    m["l1f_gwT"] = np.ascontiguousarray(inputs["L1_ffn_gate_weight"].T)
    m["l1f_expert_up"] = inputs["L1_ffn_expert_up"]
    m["l1f_expert_down"] = inputs["L1_ffn_expert_down"]
    m["l1f_shared_up"] = inputs["L1_ffn_shared_up"]
    m["l1f_shared_down"] = inputs["L1_ffn_shared_down"]


def emit_kvs(c, arena, w_sb, sT_arena):
    nc = c.nc
    xv = arena[:].rearrange("p (dc q) -> p dc q", q=4096)
    sv = sT_arena[:].rearrange("p (cc q) -> p cc q", q=4096)
    wv = w_sb[:].rearrange("p (dc f) -> p dc f", f=2048)
    pe = c.sem("kv_pe"); ve = c.sem("kv_ve"); ev = c.sem("kv_ev")
    pe0 = c.base("kv_pe"); ve0 = c.base("kv_ve"); ev0 = c.base("kv_ev")
    with contextlib.ExitStack() as ph:
        ps = [ph.enter_context(nc.psum_tensor(f"kvp{i}", [128, 512], F32))
              for i in range(4)]
        ps_t = [ph.enter_context(nc.psum_tensor(f"kvt{i}", [128, 1024], BF16))
                for i in range(2)]
        ksb = ph.enter_context(nc.sbuf_tensor("kvk", [128, 1024], F32))
        sbf = [ph.enter_context(nc.sbuf_tensor(f"kvs{i}", [128, 1024], BF16))
               for i in range(2)]

        @c.block.tensor
        def _(t):
            t.wait_ge(c.sem("w_ld"), c.cnt["w_ld"])
            for ti in range(NT):
                if ti >= 1:
                    t.wait_ge(ve, ve0 + 2 * ti)
                for dc in range(8):
                    for j in range(4):
                        ins = t.matmul(ps[j][:],
                                       xv[:, dc, ti * 128:(ti + 1) * 128],
                                       wv[:, dc, j * 512:(j + 1) * 512],
                                       start=(dc == 0), stop=(dc == 7))
                        if dc == 7:
                            ins.then_inc(pe, 1)
                t.wait_ge(ve, ve0 + 2 * ti + 2)
                if ti >= 2:
                    t.wait_ge(ev, ev0 + ti - 1)
                for cc in range(8):
                    ins = t.transpose(ps_t[ti % 2][:, cc * 128:(cc + 1) * 128],
                                      sbf[ti % 2][:, cc * 128:(cc + 1) * 128],
                                      c.sb["ident_bf"][:])
                ins.then_inc(pe, 1)

        @c.block.vector
        def _(v):
            for ti in range(NT):
                v.wait_ge(pe, pe0 + 5 * ti + 4)
                v.tensor_copy(ksb[:, 0:512], ps[0][:])
                v.tensor_copy(ksb[:, 512:1024], ps[1][:])
                v.drain().then_inc(ve, 1)
                v.tensor_add(sbf[ti % 2][:, 0:512], ps[2][:], ksb[:, 0:512])
                v.tensor_add(sbf[ti % 2][:, 512:1024], ps[3][:],
                             ksb[:, 512:1024])
                v.drain().then_inc(ve, 1)
                v.wait_ge(pe, pe0 + 5 * ti + 5)
                v.tensor_copy(sv[:, :, ti * 128:(ti + 1) * 128],
                              ps_t[ti % 2][:].rearrange("p (cc q) -> p cc q",
                                                        q=128))
                v.drain().then_inc(ev, 1)

        c.commit("kv_pe", 5 * NT)
        c.commit("kv_ve", 2 * NT)
        c.commit("kv_ev", NT)
    c.barrier()


def emit_latent(c, sT_arena, wd_sb, latT):
    nc = c.nc
    sv = sT_arena[:].rearrange("p (dc q) -> p dc q", q=4096)
    wv = wd_sb[:].rearrange("p (dc f) -> p dc f", f=256)
    lv = latT[:].rearrange("p (lc q) -> p lc q", q=4096)
    pe = c.sem("lt_pe"); se = c.sem("lt_se")
    pe0 = c.base("lt_pe"); se0 = c.base("lt_se")
    with contextlib.ExitStack() as ph:
        ps = [ph.enter_context(nc.psum_tensor(f"ltp{i}", [128, 512], F32))
              for i in range(8)]

        @c.block.tensor
        def _(t):
            t.wait_ge(c.sem("w_ld"), c.cnt["w_ld"])
            for oc in range(2):
                for tb in range(8):
                    if oc == 1:
                        t.wait_ge(se, se0 + tb + 1)
                    for dc in range(8):
                        ins = t.matmul(ps[tb][:],
                                       wv[:, dc, oc * 128:(oc + 1) * 128],
                                       sv[:, dc, tb * 512:(tb + 1) * 512],
                                       start=(dc == 0), stop=(dc == 7))
                    ins.then_inc(pe, 1)

        @c.block.scalar
        def _(s):
            for oc in range(2):
                for tb in range(8):
                    s.wait_ge(pe, pe0 + oc * 8 + tb + 1)
                    s.activation(lv[:, oc, tb * 512:(tb + 1) * 512],
                                 ps[tb][:], AF.Copy)
                    s.drain().then_inc(se, 1)

        c.commit("lt_pe", 16)
        c.commit("lt_se", 16)
    c.barrier()


def emit_recup(c, latT, w_sb, karena, varena):
    nc = c.nc
    lv = latT[:].rearrange("p (lc q) -> p lc q", q=4096)
    wv = w_sb[:].rearrange("p (lc f) -> p lc f", f=2048)
    kv = karena[:].rearrange("p (cc q) -> p cc q", q=4096)
    vv = varena[:].rearrange("p (t h e) -> p t h e", h=16, e=65)
    pe = c.sem("ru_pe"); se = c.sem("ru_se"); ve = c.sem("ru_ve")
    ev = c.sem("ru_ev")
    pe0 = c.base("ru_pe"); se0 = c.base("ru_se"); ve0 = c.base("ru_ve")
    ev0 = c.base("ru_ev")
    with contextlib.ExitStack() as ph:
        ps = [ph.enter_context(nc.psum_tensor(f"rup{i}", [128, 512], F32))
              for i in range(4)]
        ps_t = [ph.enter_context(nc.psum_tensor(f"rut{i}", [128, 1024], BF16))
                for i in range(2)]
        ktok = [ph.enter_context(nc.sbuf_tensor(f"ruk{i}", [128, 1024], BF16))
                for i in range(2)]

        @c.block.tensor
        def _(t):
            t.wait_ge(c.sem("w_ld"), c.cnt["w_ld"])
            for ti in range(NT):
                if ti >= 1:
                    t.wait_ge(se, se0 + 4 * ti)
                for lc in range(2):
                    for j in range(4):
                        ins = t.matmul(ps[j][:],
                                       lv[:, lc, ti * 128:(ti + 1) * 128],
                                       wv[:, lc, j * 512:(j + 1) * 512],
                                       start=(lc == 0), stop=(lc == 1))
                        if lc == 1:
                            ins.then_inc(pe, 1)
                t.wait_ge(se, se0 + 4 * ti + 2)
                if ti >= 2:
                    t.wait_ge(ev, ev0 + ti - 1)
                for cc in range(8):
                    ins = t.transpose(ps_t[ti % 2][:, cc * 128:(cc + 1) * 128],
                                      ktok[ti % 2][:, cc * 128:(cc + 1) * 128],
                                      c.sb["ident_bf"][:])
                ins.then_inc(pe, 1)

        @c.block.scalar
        def _(s):
            for ti in range(NT):
                for j in range(2):
                    s.wait_ge(pe, pe0 + 5 * ti + j + 1)
                    s.activation(ktok[ti % 2][:, j * 512:(j + 1) * 512],
                                 ps[j][:], AF.Copy)
                    s.drain().then_inc(se, 1)
                for j in range(2):
                    s.wait_ge(pe, pe0 + 5 * ti + j + 3)
                    for hh in range(8):
                        s.activation(vv[:, ti, j * 8 + hh, 0:64],
                                     ps[2 + j][:, hh * 64:(hh + 1) * 64],
                                     AF.Copy)
                    s.drain().then_inc(se, 1)

        @c.block.vector
        def _(v):
            for ti in range(NT):
                v.wait_ge(pe, pe0 + 5 * ti + 5)
                v.tensor_copy(kv[:, :, ti * 128:(ti + 1) * 128],
                              ps_t[ti % 2][:].rearrange("p (cc q) -> p cc q",
                                                        q=128))
                v.drain().then_inc(ev, 1)

        c.commit("ru_pe", 5 * NT)
        c.commit("ru_se", 4 * NT)
        c.commit("ru_ev", NT)
    c.barrier()


def emit_qtok(c, arena, wq_sb, qtok_h):
    nc = c.nc
    xv = arena[:].rearrange("p (dc q) -> p dc q", q=4096)
    wv = wq_sb[:].rearrange("p (dc f) -> p dc f", f=1024)
    pe = c.sem("qt_pe"); se = c.sem("qt_se"); hb = c.sem("qt_hb")
    pe0 = c.base("qt_pe"); se0 = c.base("qt_se"); hb0 = c.base("qt_hb")
    with contextlib.ExitStack() as ph:
        ps = [ph.enter_context(nc.psum_tensor(f"qtp{i}", [128, 512], F32))
              for i in range(2)]
        stage = [ph.enter_context(nc.sbuf_tensor(f"qts{i}", [128, 1024], BF16))
                 for i in range(2)]

        @c.block.tensor
        def _(t):
            t.wait_ge(c.sem("w_ld"), c.cnt["w_ld"])
            for ti in range(NT):
                for ob in range(2):
                    if ti >= 1:
                        t.wait_ge(se, se0 + 2 * (ti - 1) + ob + 1)
                    for dc in range(8):
                        ins = t.matmul(ps[ob][:],
                                       xv[:, dc, ti * 128:(ti + 1) * 128],
                                       wv[:, dc, ob * 512:(ob + 1) * 512],
                                       start=(dc == 0), stop=(dc == 7))
                    ins.then_inc(pe, 1)

        @c.block.scalar
        def _(s):
            for ti in range(NT):
                for ob in range(2):
                    s.wait_ge(pe, pe0 + 2 * ti + ob + 1)
                    if ti >= 2:
                        s.wait_ge(hb, hb0 + 16 * (ti - 1))
                    s.activation(stage[ti % 2][:, ob * 512:(ob + 1) * 512],
                                 ps[ob][:], AF.Copy)
                    s.drain().then_inc(se, 1)

        @c.block.gpsimd
        def _(g):
            for ti in range(NT):
                g.wait_ge(se, se0 + 2 * ti + 2)
                g.dma_start(out=qtok_h[ti, :, :],
                            in_=stage[ti % 2][:]).then_inc(hb, 16)
            g.wait_ge(hb, hb0 + 16 * NT)

        c.commit("qt_pe", 2 * NT)
        c.commit("qt_se", 2 * NT)
        c.commit("qt_hb", 16 * NT)
    c.barrier()


def sublayer_l0ab(c, x_src, x_dst):
    nc = c.nc
    xnt_h = c.dram["xnt_h"]
    qtok_h = c.dram["qtok_h"]
    with contextlib.ExitStack() as s1:
        emit_sub_consts(c, s1, DECAY_CONSTS)
        karena = s1.enter_context(nc.sbuf_tensor("karena", [128, 8 * 4096],
                                                 BF16))
        with contextlib.ExitStack() as s2:
            arena = s2.enter_context(nc.sbuf_tensor("arena", [128, 8 * 4096],
                                                    BF16))
            av = arena[:].rearrange("p (cc q) -> p cc q", q=4096)
            emit_ln_transpose(c, x_src,
                              lambda i, t: av[:, :, t * 128:(t + 1) * 128])
            with contextlib.ExitStack() as s25:
                w_sb = s25.enter_context(nc.sbuf_tensor("wkv",
                                                        [128, 8 * 2048], BF16))
                emit_wload(c, [(w_sb, "l0ab_wkv")])
                emit_kvs(c, arena, w_sb, karena)
            with contextlib.ExitStack() as s26:
                wq_sb = s26.enter_context(nc.sbuf_tensor("wqq",
                                                         [128, 8 * 1024],
                                                         BF16))
                emit_wload(c, [(wq_sb, "l0ab_wq")])
                emit_qtok(c, arena, wq_sb, qtok_h)
        varena = s1.enter_context(nc.sbuf_tensor("varena",
                                                 [128, NT * 1040], BF16))
        emit_ones_prefill(c, varena)
        with contextlib.ExitStack() as s3:
            latT = s3.enter_context(nc.sbuf_tensor("latT", [128, 2 * 4096],
                                                   BF16))
            with contextlib.ExitStack() as s35:
                wd_sb = s35.enter_context(nc.sbuf_tensor("wdkv",
                                                         [128, 8 * 256], BF16))
                emit_wload(c, [(wd_sb, "l0ab_wdown")])
                emit_latent(c, karena, wd_sb, latT)
            with contextlib.ExitStack() as s36:
                wu_sb = s36.enter_context(nc.sbuf_tensor("wup",
                                                         [128, 2 * 2048],
                                                         BF16))
                emit_wload(c, [(wu_sb, "l0ab_wup")])
                emit_recup(c, latT, wu_sb, karena, varena)
        wo_sb = s1.enter_context(nc.sbuf_tensor("wo_sb", [128, 8 * 1024],
                                                BF16))
        emit_wload(c, [(wo_sb, "l0ab_wo")])
        qlo = s1.enter_context(nc.sbuf_tensor("qlo", [128, 8 * 512], BF16))
        qhi = s1.enter_context(nc.sbuf_tensor("qhi", [128, 8 * 512], BF16))
        qbt = s1.enter_context(nc.sbuf_tensor("qbt", [128, 4 * 1024], BF16))
        qtk = [s1.enter_context(nc.sbuf_tensor(f"qtk{i}", [128, 1024], BF16))
               for i in range(2)]
        ps_qt = s1.enter_context(nc.psum_tensor("psqt", [128, 1024], BF16))

        kvv = karena[:].rearrange("p (cc q) -> p cc q", q=4096)
        vvv = varena[:].rearrange("p (t h e) -> p t h e", h=16, e=65)
        qlov = qlo[:].rearrange("p (cc q) -> p cc q", q=512)
        qhiv = qhi[:].rearrange("p (cc q) -> p cc q", q=512)
        qbv = qbt[:].rearrange("p (t f) -> p t f", f=1024)
        aq = c.sem("aq_ld"); aqp = c.sem("aq_pe"); aqs = c.sem("aq_se")
        aqv = c.sem("aq_ve")
        aq0 = c.base("aq_ld"); aqp0 = c.base("aq_pe"); aqs0 = c.base("aq_se")
        aqv0 = c.base("aq_ve")

        def pe_at(qb, t4, i):
            return aqp0 + 8 * qb + 2 * t4 + i

        def se_at(qb, t4, i):
            return aqs0 + 8 * qb + 2 * t4 + i

        def ve_at(qb, t4, i):
            return aqv0 + 8 * qb + 2 * t4 + i

        def qprep(eng, E, qb, meta, hd):
            if eng == "gpsimd":
                if qb >= 1:
                    E.wait_ge(aqs, aqs0 + 8 * qb)
                E.dma_start(out=qbt[:].rearrange("p (t f) -> p t f", f=1024),
                            in_=qtok_h[qb * 4:(qb + 1) * 4, :, :]
                            .rearrange("t p f -> p t f")).then_inc(aq, 16)
            elif eng == "scalar":
                E.wait_ge(aq, aq0 + 16 * (qb + 1))
                for t4 in range(4):
                    tglob = qb * 4 + t4
                    for lohi in range(2):
                        if qb > 0 or t4 > 0:
                            pqb, pt4 = (qb, t4 - 1) if t4 > 0 else (qb - 1, 3)
                            E.wait_ge(aqp, pe_at(pqb, pt4, lohi + 1))
                        scl = (c.sb["dqlo"] if lohi == 0
                               else c.sb["dqhi"])[:, tglob:tglob + 1]
                        E.activation(qtk[lohi][:], qbv[:, t4, :],
                                     AF.Identity, scale=scl)
                        E.drain().then_inc(aqs, 1)
            elif eng == "tensor":
                for t4 in range(4):
                    for lohi in range(2):
                        E.wait_ge(aqs, se_at(qb, t4, lohi + 1))
                        if qb > 0 or t4 > 0 or lohi > 0:
                            pqb, pt4, pl = ((qb, t4, 0) if lohi == 1 else
                                            ((qb, t4 - 1, 1) if t4 > 0
                                             else (qb - 1, 3, 1)))
                            E.wait_ge(aqv, ve_at(pqb, pt4, pl + 1))
                        if qb >= 1 and t4 == 0 and lohi == 0:
                            E.wait_ge(c.sem("a_pe"),
                                      meta["hq"][(qb - 1, 15)]["pe_s_last"])
                        for cc in range(8):
                            ins = E.transpose(
                                ps_qt[:, cc * 128:(cc + 1) * 128],
                                qtk[lohi][:, cc * 128:(cc + 1) * 128],
                                c.sb["ident_bf"][:])
                        ins.then_inc(aqp, 1)
            elif eng == "vector":
                for t4 in range(4):
                    for lohi in range(2):
                        E.wait_ge(aqp, pe_at(qb, t4, lohi + 1))
                        dst = (qlov if lohi == 0 else qhiv)[
                            :, :, t4 * 128:(t4 + 1) * 128]
                        E.tensor_copy(dst, ps_qt[:].rearrange(
                            "p (cc q) -> p cc q", q=128))
                        E.drain().then_inc(aqv, 1)

        def smm(t, h, qb, kt, ps_ap, meta, hd):
            r0 = (h % 2) * 64
            upper = kt > 4 * qb + 3
            qsrc = (qhiv if upper else qlov)[r0:r0 + 64, h // 2, :]
            return t.matmul(ps_ap, kvv[r0:r0 + 64, h // 2,
                                       kt * 128:(kt + 1) * 128],
                            qsrc, start=True, stop=True)

        def pv(t, h, qb, kt, pt_ap, start, stop, hd):
            hq = qb * 16 + h
            return t.matmul(hd["ps_oz"][hq % 2][0:65, :],
                            vvv[:, kt, h, 0:65], pt_ap,
                            start=start, stop=stop)

        def hprep(eng, E, h, qb, hq, meta, hd):
            if eng == "tensor" and h == 0:
                E.wait_ge(aqv, aqv0 + 8 * (qb + 1))

        p = {
            "H": 16,
            "kts": lambda qb: list(range(32)),
            "diag": lambda qb, kt: (kt - 4 * qb)
            if 4 * qb <= kt <= 4 * qb + 3 else None,
            "mask_kind": "patch",
            "sep_z": False, "oz_rows": 65, "o_rows": 64,
            "qprep": qprep, "qprep_incs": {},
            "hprep": hprep, "hprep_incs": {},
            "smm": smm, "pv": pv,
            "exp_scale": lambda h, qb, kt:
                (c.sb["dcolinv"] if kt > 4 * qb + 3 else
                 c.sb["dcol"])[:, kt:kt + 1],
            "exp_bias": lambda h, qb, kt: c.sb["kconst"][:, 1:2],
            "out_w": wo_sb[:].rearrange("p (dc f) -> p dc f", f=1024),
            "out_bias_row": None,
            "oproj_mode": "mm", "ps_y_n": 1,
            "x_src": x_src, "x_dst": x_dst,
        }
        emit_attention(c, p)
        c.commit("aq_ld", 16 * NQB)
        c.commit("aq_pe", 8 * NQB)
        c.commit("aq_se", 8 * NQB)
        c.commit("aq_ve", 8 * NQB)


def declare_l0ab(c, nc):
    for n_, sh in [("l0ab_wkv", [1024, 2048]), ("l0ab_wdown", [1024, 256]),
                   ("l0ab_wup", [256, 2048]), ("l0ab_wq", [1024, 1024]),
                   ("l0ab_wo", [1024, 1024])]:
        c.dram[n_] = nc.dram_tensor(n_, sh, F32, kind="ExternalInput").ap()


def prep_l0ab(c):
    pass


PART_PREFIX["l0ab"] = "l0ab"
PART_DECL["l0ab"] = declare_l0ab
PART_PREP["l0ab"] = prep_l0ab
PART_EMIT["l0ab"] = sublayer_l0ab


def host_inputs_l0ab(inputs, m):
    m["l0ab_wkv"] = np.ascontiguousarray(
        np.concatenate([inputs["L0_attn_b_W_k"], inputs["L0_attn_b_W_v"]],
                       axis=1))
    m["l0ab_wdown"] = inputs["L0_attn_b_W_down_kv"]
    m["l0ab_wup"] = np.ascontiguousarray(
        np.concatenate([inputs["L0_attn_b_W_up_k"],
                        inputs["L0_attn_b_W_up_v"]], axis=1))
    m["l0ab_wq"] = inputs["L0_attn_b_W_q"]
    m["l0ab_wo"] = inputs["L0_attn_b_W_o"]


def emit_fmT(c, st, name, dram_name, r, cc):
    nc = c.nc
    out = st.enter_context(nc.sbuf_tensor(name, [r, cc], F32))
    c.sb[name] = out
    ld = c.sem("g_ld"); pe = c.sem("g_pe"); ve = c.sem("g_ve")
    ldt = c.base("g_ld") + 16
    pet = c.base("g_pe") + 1
    vet = c.base("g_ve") + 1
    with contextlib.ExitStack() as ph:
        tmp = ph.enter_context(nc.sbuf_tensor("fmt_t", [cc, r], F32))
        ps = ph.enter_context(nc.psum_tensor("fmt_p", [r, cc], F32))

        @c.block.gpsimd
        def _(g):
            g.dma_start(out=tmp[:], in_=c.dram[dram_name][:, :]).then_inc(ld, 16)
            g.wait_ge(ld, ldt)

        @c.block.tensor
        def _(t):
            t.wait_ge(ld, ldt)
            t.transpose(ps[:], tmp[:],
                        c.sb["ident_f"][0:cc, 0:cc]).then_inc(pe, 1)

        @c.block.vector
        def _(v):
            v.wait_ge(pe, pet)
            v.tensor_copy(out[:], ps[:])
            v.drain().then_inc(ve, 1)

        c.commit("g_ld", 16)
        c.commit("g_pe", 1)
        c.commit("g_ve", 1)
    c.barrier()


def emit_cproj(c, arena, w_sb, cq_h, ckvT, krT):
    nc = c.nc
    xv = arena[:].rearrange("p (dc q) -> p dc q", q=4096)
    wv = w_sb[:].rearrange("p (dc f) -> p dc f", f=96)
    pe = c.sem("cp_pe"); se = c.sem("cp_se"); hb = c.sem("cp_hb")
    pe0 = c.base("cp_pe"); se0 = c.base("cp_se"); hb0 = c.base("cp_hb")
    biases = ["l1ab_bdq_fm", "l1ab_bdkv_fm", "l1ab_bkr_fm"]
    with contextlib.ExitStack() as ph:
        ps = [ph.enter_context(nc.psum_tensor(f"cpp{i}", [32, 512], F32))
              for i in range(2)]
        cq_sb = ph.enter_context(nc.sbuf_tensor("cqsb", [32, 4096], BF16))
        dsts = [cq_sb, ckvT, krT]

        @c.block.tensor
        def _(t):
            t.wait_ge(c.sem("w_ld"), c.cnt["w_ld"])
            gi = 0
            for j in range(3):
                for tb in range(8):
                    if gi >= 2:
                        t.wait_ge(se, se0 + gi - 1)
                    for dc in range(8):
                        ins = t.matmul(ps[gi % 2][:],
                                       wv[:, dc, j * 32:(j + 1) * 32],
                                       xv[:, dc, tb * 512:(tb + 1) * 512],
                                       start=(dc == 0), stop=(dc == 7))
                    ins.then_inc(pe, 1)
                    gi += 1

        @c.block.scalar
        def _(s):
            gi = 0
            for j in range(3):
                for tb in range(8):
                    s.wait_ge(pe, pe0 + gi + 1)
                    s.activation(dsts[j][:, tb * 512:(tb + 1) * 512],
                                 ps[gi % 2][:], AF.Identity,
                                 bias=c.sb[biases[j]][:, 0:1])
                    s.drain().then_inc(se, 1)
                    gi += 1

        @c.block.gpsimd
        def _(g):
            g.wait_ge(se, se0 + 8)
            g.dma_start(out=cq_h[:, :], in_=cq_sb[:]).then_inc(hb, 16)
            g.wait_ge(hb, hb0 + 16)

        c.commit("cp_pe", 24)
        c.commit("cp_se", 24)
        c.commit("cp_hb", 16)
    c.barrier()


def emit_kcv(c, ckvT, wuk, wuv, buv_row, karena, varena):
    nc = c.nc
    kv = karena[:].rearrange("p (h q) -> p h q", q=4096)
    vv = varena[:].rearrange("p (t h d) -> p t h d", h=8, d=128)
    pe = c.sem("kc_pe"); se = c.sem("kc_se")
    pe0 = c.base("kc_pe"); se0 = c.base("kc_se")
    with contextlib.ExitStack() as ph:
        ps = [ph.enter_context(nc.psum_tensor(f"kcp{i}", [128, 512], F32))
              for i in range(2)]

        @c.block.tensor
        def _(t):
            t.wait_ge(c.sem("w_ld"), c.cnt["w_ld"])
            gi = 0
            for h in range(8):
                for tb in range(8):
                    if gi >= 2:
                        t.wait_ge(se, se0 + gi - 1)
                    t.matmul(ps[gi % 2][:], wuk[0:32, h * 128:(h + 1) * 128],
                             ckvT[0:32, tb * 512:(tb + 1) * 512],
                             start=True, stop=True).then_inc(pe, 1)
                    gi += 1
            for ti in range(NT):
                for ob in range(2):
                    if gi >= 2:
                        t.wait_ge(se, se0 + gi - 1)
                    t.matmul(ps[gi % 2][:], c.sb["ones1"][0:1, 0:128],
                             buv_row[0:1, ob * 512:(ob + 1) * 512],
                             start=True, stop=False)
                    ins = t.matmul(ps[gi % 2][:],
                                   ckvT[0:32, ti * 128:(ti + 1) * 128],
                                   wuv[0:32, ob * 512:(ob + 1) * 512],
                                   start=False, stop=True)
                    ins.then_inc(pe, 1)
                    gi += 1

        @c.block.scalar
        def _(s):
            gi = 0
            for h in range(8):
                for tb in range(8):
                    s.wait_ge(pe, pe0 + gi + 1)
                    s.activation(kv[:, h, tb * 512:(tb + 1) * 512],
                                 ps[gi % 2][:], AF.Identity,
                                 bias=c.sb["l1ab_buk_fm"][:, h:h + 1])
                    s.drain().then_inc(se, 1)
                    gi += 1
            for ti in range(NT):
                for ob in range(2):
                    s.wait_ge(pe, pe0 + gi + 1)
                    s.activation(vv[:, ti, ob * 4:(ob + 1) * 4, :],
                                 ps[gi % 2][:].rearrange("p (h d) -> p h d",
                                                         d=128), AF.Copy)
                    s.drain().then_inc(se, 1)
                    gi += 1

        c.commit("kc_pe", 64 + 64)
        c.commit("kc_se", 64 + 64)
    c.barrier()


def sublayer_l1ab(c, x_src, x_dst):
    nc = c.nc
    cq_h = c.dram["cq_h"]
    with contextlib.ExitStack() as s1:
        emit_sub_consts(c, s1, [("onescol", "c_onescol", [128, 1], BF16)])
        emit_bias_prep(c, [("l1ab_buq_fm", "l1ab_buq"),
                           ("l1ab_buk_fm", "l1ab_buk")],
                       [("l1ab_buv_row", "l1ab_buv"),
                        ("l1ab_bo_row", "l1ab_bo")], st=s1)
        emit_fmT(c, s1, "l1ab_bdq_fm", "l1ab_bdq", 32, 1)
        emit_fmT(c, s1, "l1ab_bdkv_fm", "l1ab_bdkv", 32, 1)
        emit_fmT(c, s1, "l1ab_bkr_fm", "l1ab_bkr", 32, 1)
        emit_fmT(c, s1, "l1ab_bqr_fm", "l1ab_bqr", 32, 8)
        karena = s1.enter_context(nc.sbuf_tensor("karena", [128, 8 * 4096],
                                                 BF16))
        ckvT = s1.enter_context(nc.sbuf_tensor("ckvT", [32, 4096], BF16))
        krT = s1.enter_context(nc.sbuf_tensor("krT", [32, 4096], BF16))
        with contextlib.ExitStack() as s2:
            arena = s2.enter_context(nc.sbuf_tensor("arena", [128, 8 * 4096],
                                                    BF16))
            av = arena[:].rearrange("p (cc q) -> p cc q", q=4096)
            emit_ln_transpose(c, x_src,
                              lambda i, t: av[:, :, t * 128:(t + 1) * 128])
            with contextlib.ExitStack() as s25:
                w_sb = s25.enter_context(nc.sbuf_tensor("w3d",
                                                        [128, 8 * 96], BF16))
                emit_wload(c, [(w_sb, "l1ab_wdown3")])
                emit_cproj(c, arena, w_sb, cq_h, ckvT, krT)
        varena = s1.enter_context(nc.sbuf_tensor("varena",
                                                 [128, NT * 8 * 128], BF16))
        wuk = s1.enter_context(nc.sbuf_tensor("wuk", [32, 1024], BF16))
        wuv = s1.enter_context(nc.sbuf_tensor("wuv", [32, 1024], BF16))
        wuq = s1.enter_context(nc.sbuf_tensor("wuq", [32, 1024], BF16))
        wqr = s1.enter_context(nc.sbuf_tensor("wqr", [32, 256], BF16))
        ld = c.sem("w_ld")

        @c.block.gpsimd
        def _(g):
            for sb, dn in [(wuk, "l1ab_wuk"), (wuv, "l1ab_wuv"),
                           (wuq, "l1ab_wuq"), (wqr, "l1ab_wqr")]:
                g.dma_start(out=sb[:], in_=c.dram[dn][:, :]).then_inc(ld, 16)
        c.commit("w_ld", 16 * 4)
        emit_kcv(c, ckvT, wuk[:], wuv[:], c.sb["l1ab_buv_row"][:],
                 karena, varena)

        wo_sb = s1.enter_context(nc.sbuf_tensor("wo_sb", [128, 8 * 1024],
                                                BF16))
        emit_wload(c, [(wo_sb, "l1ab_wo")])
        wlt = c.cnt["w_ld"]
        cqb = [s1.enter_context(nc.sbuf_tensor(f"cqb{i}", [32, 512], BF16))
               for i in range(2)]
        qch = [s1.enter_context(nc.sbuf_tensor(f"qch{i}", [128, 512], BF16))
               for i in range(2)]
        qrh = [s1.enter_context(nc.sbuf_tensor(f"qrh{i}", [32, 512], BF16))
               for i in range(2)]

        kvv = karena[:].rearrange("p (h q) -> p h q", q=4096)
        vvv = varena[:].rearrange("p (t h d) -> p t h d", h=8, d=128)
        aq = c.sem("aq_ld")
        aq0 = c.base("aq_ld")
        hql = [(qb, h) for qb in range(NQB) for h in range(8)]

        def qprep(eng, E, qb, meta, hd):
            if eng != "gpsimd":
                return
            if qb >= 2:
                E.wait_ge(c.sem("a_pe"),
                          meta["hq"][(qb - 2, 7)]["prep"]["a_pe"])
            E.dma_start(out=cqb[qb % 2][:],
                        in_=cq_h[:, qb * 512:(qb + 1) * 512]).then_inc(aq, 16)

        def hprep(eng, E, h, qb, hq, meta, hd):
            mk = meta["hq"][(qb, h)]["prep"]
            ps_hp = hd["ps_b"]
            if eng == "tensor":
                if hq == 0:
                    E.wait_ge(c.sem("w_ld"), wlt)
                E.wait_ge(aq, aq0 + 16 * (qb + 1))
                if hq >= 1:
                    E.wait_ge(c.sem("a_se"),
                              meta["hqs"][hq - 1]["norm"]["se_rz"])
                    E.wait_ge(c.sem("a_se"),
                              meta["hq"][hql[hq - 1]]["prep"]["a_se"])
                if hq >= 2:
                    E.wait_ge(c.sem("a_pe"),
                              meta["hq"][hql[hq - 2]]["pe_s_last"])
                E.matmul(ps_hp[:], wuq[0:32, h * 128:(h + 1) * 128],
                         cqb[qb % 2][:], start=True,
                         stop=True).then_inc(c.sem("a_pe"), 1)
                E.wait_ge(c.sem("a_se"), mk["a_se"] - 1)
                E.matmul(ps_hp[0:32, :], wqr[0:32, h * 32:(h + 1) * 32],
                         cqb[qb % 2][:], start=True,
                         stop=True).then_inc(c.sem("a_pe"), 1)
            elif eng == "scalar":
                E.wait_ge(c.sem("a_pe"), mk["a_pe"] - 1)
                E.activation(qch[hq % 2][:], ps_hp[:], AF.Identity,
                             bias=c.sb["l1ab_buq_fm"][:, h:h + 1])
                E.drain().then_inc(c.sem("a_se"), 1)
                E.wait_ge(c.sem("a_pe"), mk["a_pe"])
                E.activation(qrh[hq % 2][:], ps_hp[0:32, :], AF.Identity,
                             bias=c.sb["l1ab_bqr_fm"][:, h:h + 1])
                E.drain().then_inc(c.sem("a_se"), 1)

        def smm(t, h, qb, kt, ps_ap, meta, hd):
            hq = qb * 8 + h
            mk = meta["hq"][(qb, h)]["prep"]
            t.wait_ge(c.sem("a_se"), mk["a_se"])
            t.matmul(ps_ap, kvv[:, h, kt * 128:(kt + 1) * 128],
                     qch[hq % 2][:], start=True, stop=False)
            return t.matmul(ps_ap, krT[0:32, kt * 128:(kt + 1) * 128],
                            qrh[hq % 2][:], start=False, stop=True)

        def pv(t, h, qb, kt, pt_ap, start, stop, hd):
            hq = qb * 8 + h
            t.matmul(hd["ps_oz"][hq % 2][:], vvv[:, kt, h, :], pt_ap,
                     start=start, stop=stop)
            return t.matmul(hd["ps_z"][hq % 2][:], c.sb["onescol"][:, 0:1],
                            pt_ap, start=start, stop=stop)

        p = {
            "H": 8,
            "kts": lambda qb: list(range(32)),
            "diag": lambda qb, kt: None,
            "mask_kind": None,
            "sep_z": True, "oz_rows": 128, "o_rows": 128,
            "qprep": qprep, "qprep_incs": {},
            "hprep": hprep,
            "hprep_incs": {"a_pe": 2, "a_se": 2},
            "smm": smm, "pv": pv,
            "exp_scale": lambda h, qb, kt: 1.0 / math.sqrt(160.0),
            "exp_bias": lambda h, qb, kt: c.sb["kconst"][:, 2:3],
            "out_w": wo_sb[:].rearrange("p (dc f) -> p dc f", f=1024),
            "out_bias_row": c.sb["l1ab_bo_row"][:],
            "oproj_mode": "mm", "ps_y_n": 1,
            "x_src": x_src, "x_dst": x_dst,
        }
        emit_attention(c, p)
        c.commit("aq_ld", 16 * NQB)


def declare_l1ab(c, nc):
    for n_, sh in [("l1ab_wdown3", [1024, 96]), ("l1ab_wuk", [32, 1024]),
                   ("l1ab_wuv", [32, 1024]), ("l1ab_wuq", [32, 1024]),
                   ("l1ab_wqr", [32, 256]), ("l1ab_wo", [1024, 1024]),
                   ("l1ab_bdq", [1, 32]), ("l1ab_bdkv", [1, 32]),
                   ("l1ab_bkr", [1, 32]), ("l1ab_buq", [1, 1024]),
                   ("l1ab_buk", [1, 1024]), ("l1ab_buv", [1, 1024]),
                   ("l1ab_bqr", [8, 32]), ("l1ab_bo", [1, 1024])]:
        c.dram[n_] = nc.dram_tensor(n_, sh, F32, kind="ExternalInput").ap()
    c.dram["cq_h"] = nc.dram_tensor("cq_h", [32, 4096], BF16,
                                    kind="Internal").ap()


def prep_l1ab(c):
    pass


PART_PREFIX["l1ab"] = "l1ab"
PART_DECL["l1ab"] = declare_l1ab
PART_PREP["l1ab"] = prep_l1ab
PART_EMIT["l1ab"] = sublayer_l1ab


def host_inputs_l1ab(inputs, m):
    g = lambda n: inputs["L1_attn_b_" + n]
    m["l1ab_wdown3"] = np.ascontiguousarray(
        np.concatenate([g("W_DQ"), g("W_DKV"), g("W_KR")], axis=1))
    m["l1ab_wuk"] = g("W_UK")
    m["l1ab_wuv"] = g("W_UV")
    m["l1ab_wuq"] = g("W_UQ")
    m["l1ab_wqr"] = g("W_QR")
    m["l1ab_wo"] = g("W_O")
    m["l1ab_bdq"] = g("b_DQ").reshape(1, 32)
    m["l1ab_bdkv"] = g("b_DKV").reshape(1, 32)
    m["l1ab_bkr"] = g("b_KR").reshape(1, 32)
    m["l1ab_buq"] = g("b_UQ").reshape(1, 1024)
    m["l1ab_buk"] = g("b_UK").reshape(1, 1024)
    m["l1ab_buv"] = g("b_UV").reshape(1, 1024)
    m["l1ab_bqr"] = np.ascontiguousarray(g("b_QR").reshape(8, 32))
    m["l1ab_bo"] = g("b_O").reshape(1, 1024)
